# version 03 (final)
# baseline (speedup 1.0000x reference)
import contextlib
import os
import numpy as np
import concourse.bass as bass
import concourse.mybir as mybir
from concourse.bass_utils import run_bass_kernel_spmd

F32 = mybir.dt.float32
BF16 = mybir.dt.bfloat16
U8 = mybir.dt.uint8
AF = mybir.ActivationFunctionType
ALU = mybir.AluOpType
AX = mybir.AxisListType

T = 2048
D = 1024
NCF = 8
NTT = 4
TTW = 512
BLOCKS_ALL = ["l0a", "l0f", "l1a", "l1f", "l2a", "l2f", "l3a", "l3f"]


class Ev:
    __slots__ = ("sem", "val")
    def __init__(self, sem, val):
        self.sem, self.val = sem, val


class Prog:
    ENGINES = ["sync", "gpsimd", "tensor", "vector", "scalar"]
    NDMA = 4
    SEMS = (["sync", "gpsimd", "tensor", "vector", "scalar", "cc0"]
            + [f"d{q}{i}" for q in ("sync", "gp", "scalar") for i in range(4)])
    DMA_Q = {"sync": "dsync", "gpsimd": "dgp", "scalar": "dscalar"}

    SELF_SYNC = ("vector", "scalar", "gpsimd")

    def __init__(self, nc):
        self.nc = nc
        self.queues = {e: [] for e in self.ENGINES}
        self.cnt = {s: 0 for s in self.SEMS}
        self.waited = {e: {} for e in self.ENGINES}
        self.last_compute = {e: 0 for e in self.ENGINES}
        self.dma_rr = {}
        self.sems = {}

    def wait_self(self, q, val):
        w = self.waited[q]
        key = "_self_" + q
        if w.get(key, -1) >= val:
            return
        w[key] = val
        self.queues[q].append(lambda eng: eng.wait_ge(self.sems[q], val))

    def wait(self, q, ev):
        if ev is None:
            return
        if isinstance(ev, (list, tuple)):
            for e in ev:
                self.wait(q, e)
            return
        val = ev.val
        if ev.sem == q:
            return
        w = self.waited[q]
        if w.get(ev.sem, -1) >= val:
            return
        w[ev.sem] = val
        s, v = ev.sem, val
        self.queues[q].append(lambda eng: eng.wait_ge(self.sems[s], v))

    SERIAL = os.environ.get("KSERIAL", "") == "1"
    PHASES = set(p for p in os.environ.get("KSER_PHASES", "").split(",") if p)
    cur_phase = ""

    def phase(self, name):
        self.cur_phase = name
        if name in self.PHASES:
            self._ser_on = True
        else:
            self._ser_on = False

    def _serialize(self, q):
        for s in self.SEMS:
            if s != q and self.cnt[s] > 0:
                self.wait(q, Ev(s, self.cnt[s]))

    def op(self, q, fn, inc=True, after=None, drain=False, dsync=False):
        if q in self.SELF_SYNC:
            inc = True
            if self.last_compute[q] > 0:
                self.wait_self(q, self.last_compute[q])
        if getattr(self, "_ser_on", False):
            self._serialize(q)
            drain = True
            inc = True
        if self.SERIAL:
            self._serialize(q)
            drain = True
            inc = True
        if after is not None:
            self.wait(q, after)
        if drain:
            self.queues[q].append(lambda eng: eng.drain())
        if inc:
            self.cnt[q] += 1
            v = self.cnt[q]
            if q in self.SELF_SYNC:
                self.last_compute[q] = v
            def emit(eng, fn=fn, q=q):
                fn(eng).then_inc(self.sems[q], 1)
            self.queues[q].append(emit)
            return Ev(q, v)
        self.queues[q].append(lambda eng, fn=fn: fn(eng))
        return None

    def dma(self, q, out, in_, after=None, **kw):
        if getattr(self, "_ser_on", False):
            self._serialize(q)
        if self.SERIAL:
            self._serialize(q)
        if after is not None:
            self.wait(q, after)
        base = self.DMA_Q[q]
        idx = self.dma_rr.get(q, 0)
        self.dma_rr[q] = idx + 1
        ds = f"{base[1:] if False else base.replace('dma','')}"
        ds = base.replace("dsync", "dsync").replace("dgp", "dgp")
        ds = {"dsync": "dsync", "dgp": "dgp", "dscalar": "dscalar"}[base]
        ds = ds.replace("dsync", "dsync")
        sem = f"d{'sync' if q == 'sync' else ('gp' if q == 'gpsimd' else 'scalar')}{idx % self.NDMA}"
        if self.cnt[sem] > 0:
            self.wait(q, Ev(sem, self.cnt[sem]))
        self.cnt[sem] += 16
        v = self.cnt[sem]
        def emit(eng, out=out, in_=in_, kw=kw, sem=sem):
            eng.dma_start(out=out, in_=in_, **kw).then_inc(self.sems[sem], 16)
        self.queues[q].append(emit)
        return Ev(sem, v)

    def dma_custom(self, q, fn, after=None):
        if getattr(self, "_ser_on", False) or self.SERIAL:
            self._serialize(q)
        if after is not None:
            self.wait(q, after)
        idx = self.dma_rr.get(q, 0)
        self.dma_rr[q] = idx + 1
        sem = f"d{'sync' if q == 'sync' else ('gp' if q == 'gpsimd' else 'scalar')}{idx % self.NDMA}"
        if self.cnt[sem] > 0:
            self.wait(q, Ev(sem, self.cnt[sem]))
        self.cnt[sem] += 16
        v = self.cnt[sem]
        def emit(eng, fn=fn, sem=sem):
            fn(eng).then_inc(self.sems[sem], 16)
        self.queues[q].append(emit)
        return Ev(sem, v)

    def cc(self, kind, op, ins, outs, after=None):
        q = "gpsimd"
        if getattr(self, "_ser_on", False) or self.SERIAL:
            self._serialize(q)
        if after is not None:
            self.wait(q, after)
        self.cnt["cc0"] += 1
        v = self.cnt["cc0"]
        def emit(eng, kind=kind, op=op, ins=ins, outs=outs):
            eng.collective_compute(kind, op, replica_groups=[list(range(8))],
                                   ins=ins, outs=outs).then_inc(self.sems["cc0"], 1)
        self.queues[q].append(emit)
        return Ev("cc0", v)

    def fence(self, q):
        return Ev(q, self.cnt[q])

    def play(self, block):
        for e in self.ENGINES:
            qops = self.queues[e]
            def mk(qops):
                def body(eng):
                    for f in qops:
                        f(eng)
                return body
            getattr(block, e)(mk(qops))


WSHAPES = {
    "l0a_wT": (1024, 3072), "l0a_wout": (1024, 1024),
    "l0f_baseT": (1024, 1024), "l0f_baseb": (128, 8), "l0f_sharedAT": (1024, 16),
    "l0f_sharedBT": (16, 1024), "l0f_routedAT": (1024, 96), "l0f_routedBT": (96, 1024),
    "l0f_routerT": (1024, 6), "l0f_routerb": (128, 6), "l0f_blk16": (6, 96),
    "l1a_wqT": (1024, 1024), "l1a_wkT": (1024, 256), "l1a_wvT": (1024, 256),
    "l1a_wprojT": (1024, 1024), "l1a_cwq": (128, 8, 4), "l1a_cwk": (128, 2, 4),
    "l1a_cwv": (128, 2, 4), "l1a_wbetaT": (1024, 4), "l1a_bbeta": (1, 4),
    "l1a_wgkT": (1024, 4), "l1a_alog": (1, 4), "l1a_dt": (1, 4),
    "l1a_wgateT": (1024, 1024), "l1a_rmsw": (1, 64),
    "l1f_wrT": (1024, 1024), "l1f_wkT": (1024, 1024), "l1f_wvT": (1024, 1024),
    "l1f_mur": (128, 8), "l1f_muk": (128, 8),
    "l2a_wq": (1024, 1024), "l2a_bq": (128, 8), "l2a_wk": (1024, 1024),
    "l2a_bk": (128, 8), "l2a_wv": (1024, 1024), "l2a_bv": (1, 1024),
    "l2a_wo": (1024, 1024), "l2a_bo": (128, 8),
    "l2f_gateT": (1024, 8), "l2f_lng": (128, 8), "l2f_lnb": (128, 8),
    "l2f_wi": (1024, 4096), "l2f_wo": (4096, 1024), "l2f_pc": (128, 24),
    "l3a_wqT": (1024, 1024), "l3a_wkT": (1024, 256), "l3a_wvT": (1024, 256),
    "l3a_woT": (1024, 1024),
    "l3f_w1T": (1024, 4096), "l3f_w2T": (1024, 4096), "l3f_h1T": (4096, 4096),
    "l3f_h2T": (4096, 4096), "l3f_w3T": (4096, 1024),
    "l3f_s1": (128, 32), "l3f_s2": (128, 32), "l3f_sh1": (128, 32),
    "l3f_sh2": (128, 32), "l3f_s3": (128, 8),
}


def prep_host(inputs):
    g = lambda kk: np.asarray(inputs[kk])
    tr = lambda a: np.ascontiguousarray(np.asarray(a, dtype=np.float32).T)
    cc = lambda a: np.ascontiguousarray(np.asarray(a, dtype=np.float32))
    w = {}
    w["l0a_wT"] = tr(g("L0_attn_w"))
    w["l0a_wout"] = cc(g("L0_attn_w_out"))
    w["l0f_baseT"] = tr(g("L0_ffn_base_W"))
    w["l0f_baseb"] = cc(np.asarray(g("L0_ffn_base_b"), np.float32).reshape(8, 128).T)
    w["l0f_sharedAT"] = tr(g("L0_ffn_shared_A"))
    w["l0f_sharedBT"] = tr(g("L0_ffn_shared_B"))
    w["l0f_routedAT"] = np.ascontiguousarray(
        np.concatenate([np.asarray(g("L0_ffn_routed_A"), np.float32)[e].T for e in range(6)], axis=1))
    w["l0f_routedBT"] = np.ascontiguousarray(
        np.concatenate([np.asarray(g("L0_ffn_routed_B"), np.float32)[e].T for e in range(6)], axis=0))
    w["l0f_routerT"] = tr(g("L0_ffn_router_W"))
    w["l0f_routerb"] = cc(np.tile(np.asarray(g("L0_ffn_router_b"), np.float32).reshape(1, 6), (128, 1)))
    w["l0f_blk16"] = cc(np.kron(np.eye(6, dtype=np.float32), np.ones((1, 16), np.float32)))
    QPERM = [( (c % 4) + 4 * (rh + 2 * (c // 4)) ) for c in range(8) for rh in range(2)]
    def _permcols(wt):
        blocks = []
        for slot in QPERM:
            blocks.append(wt[:, slot * 64:(slot + 1) * 64])
        return np.ascontiguousarray(np.concatenate(blocks, axis=1))
    def _permrows(wt):
        blocks = []
        for slot in QPERM:
            blocks.append(wt[slot * 64:(slot + 1) * 64, :])
        return np.ascontiguousarray(np.concatenate(blocks, axis=0))
    w["l1a_wqT"] = _permcols(tr(g("L1_attn_Wq")))
    w["l1a_wkT"] = tr(g("L1_attn_Wk"))
    w["l1a_wvT"] = tr(g("L1_attn_Wv"))
    w["l1a_wprojT"] = _permrows(tr(g("L1_attn_Wproj")))
    _cwq = cc(g("L1_attn_cwq"))
    _cwq = np.concatenate([_cwq[s * 64:(s + 1) * 64] for s in QPERM], axis=0)
    w["l1a_cwq"] = np.ascontiguousarray(_cwq.reshape(8, 128, 4).transpose(1, 0, 2))
    w["l1a_cwk"] = np.ascontiguousarray(cc(g("L1_attn_cwk")).reshape(2, 128, 4).transpose(1, 0, 2))
    w["l1a_cwv"] = np.ascontiguousarray(cc(g("L1_attn_cwv")).reshape(2, 128, 4).transpose(1, 0, 2))
    w["l1a_wbetaT"] = tr(g("L1_attn_Wbeta"))
    w["l1a_bbeta"] = cc(g("L1_attn_b_beta").reshape(1, 4))
    w["l1a_wgkT"] = tr(g("L1_attn_Wgk"))
    w["l1a_alog"] = cc(g("L1_attn_A_log").reshape(1, 4))
    w["l1a_dt"] = cc(g("L1_attn_dt_bias").reshape(1, 4))
    w["l1a_wgateT"] = _permcols(tr(g("L1_attn_Wgate")))
    w["l1a_rmsw"] = cc(g("L1_attn_rms_w").reshape(1, 64))
    w["l1f_wrT"] = tr(g("L1_ffn_W_r"))
    w["l1f_wkT"] = tr(g("L1_ffn_W_k"))
    w["l1f_wvT"] = tr(g("L1_ffn_W_v"))
    w["l1f_mur"] = cc(np.asarray(g("L1_ffn_mu_r"), np.float32).reshape(8, 128).T)
    w["l1f_muk"] = cc(np.asarray(g("L1_ffn_mu_k"), np.float32).reshape(8, 128).T)
    w["l2a_wq"] = cc(g("L2_attn_Wq"))
    w["l2a_bq"] = cc(np.asarray(g("L2_attn_bq"), np.float32).reshape(8, 128).T)
    w["l2a_wk"] = cc(g("L2_attn_Wk"))
    w["l2a_bk"] = cc(np.asarray(g("L2_attn_bk"), np.float32).reshape(8, 128).T)
    w["l2a_wv"] = cc(g("L2_attn_Wv"))
    w["l2a_bv"] = cc(g("L2_attn_bv").reshape(1, D))
    w["l2a_wo"] = cc(g("L2_attn_Wo"))
    w["l2a_bo"] = cc(np.asarray(g("L2_attn_bo"), np.float32).reshape(8, 128).T)
    w["l2f_gateT"] = tr(g("L2_ffn_gate_weight"))
    w["l2f_lng"] = cc(np.asarray(g("L2_ffn_ln_gamma"), np.float32).reshape(8, 128).T)
    w["l2f_lnb"] = cc(np.asarray(g("L2_ffn_ln_beta"), np.float32).reshape(8, 128).T)
    w["l2f_wi"] = cc(g("L2_ffn_wi"))
    w["l2f_wo"] = cc(g("L2_ffn_wo"))
    _perm = [0, 4, 1, 5, 2, 6, 3, 7, 8, 12, 9, 13, 10, 14, 11, 15]
    _wqT = tr(g("L3_attn_wq"))
    w["l3a_wqT"] = np.ascontiguousarray(
        np.concatenate([_wqT[:, h * 64:(h + 1) * 64] for h in _perm], axis=1))
    w["l3a_wkT"] = tr(g("L3_attn_wk"))
    w["l3a_wvT"] = tr(g("L3_attn_wv"))
    _woT = tr(g("L3_attn_wo"))
    w["l3a_woT"] = np.ascontiguousarray(
        np.concatenate([_woT[h * 64:(h + 1) * 64, :] for h in _perm], axis=0))
    w["l3f_w1T"] = tr(g("L3_ffn_w1_w"))
    w["l3f_w2T"] = tr(g("L3_ffn_w2_w"))
    w["l3f_h1T"] = tr(g("L3_ffn_h1_w"))
    w["l3f_h2T"] = tr(g("L3_ffn_h2_w"))
    w["l3f_w3T"] = tr(g("L3_ffn_w3_w"))
    w["l3f_s1"] = cc(np.asarray(g("L3_ffn_w1_s"), np.float32).reshape(32, 128).T)
    w["l3f_s2"] = cc(np.asarray(g("L3_ffn_w2_s"), np.float32).reshape(32, 128).T)
    w["l3f_sh1"] = cc(np.asarray(g("L3_ffn_h1_s"), np.float32).reshape(32, 128).T)
    w["l3f_sh2"] = cc(np.asarray(g("L3_ffn_h2_s"), np.float32).reshape(32, 128).T)
    w["l3f_s3"] = cc(np.asarray(g("L3_ffn_w3_s"), np.float32).reshape(8, 128).T)
    x = np.asarray(inputs["L0_attn_x"], dtype=np.float32)
    xs = [np.ascontiguousarray(x[b].T) for b in range(8)]
    return w, xs


def xcol(tt_, c, t0=0, n=T):
    return tt_[:, c * T + t0: c * T + t0 + n]


class K:
    def __init__(self, blocks):
        self.blocks = blocks
        self.nc = bass.Bass("TRN2", target_bir_lowering=False)
        self.P = Prog(self.nc)
        self.hb = {}
        self.t = {}
        self.psfree = {}

    def hin(self, name, shape):
        self.hb[name] = self.nc.dram_tensor(name, list(shape), F32, kind="ExternalInput").ap()
        return self.hb[name]


IDENT = lambda t: t["consts"][:, 0:128]
MASK_GE = lambda t: t["consts"][:, 128:256]
MASK_GT = lambda t: t["consts"][:, 256:384]
ALLONES = lambda t: t["consts"][:, 384:512]
BLK64 = lambda t: t["consts"][:, 512:640]
PIDX = lambda t: t["consts"][:, 640:641]
IOTA8 = lambda t: t["consts"][:, 643:651]
ONESCOL = lambda t: t["consts"][:, 641:642]


def build_program(blocks):
    k = K(blocks)
    nc, P = k.nc, k.P
    k.hin("x_t", (D, T))
    k.hin("core_id", (1, 1))
    need = {w for w in WSHAPES if any(w.startswith(b + "_") for b in blocks)}
    for wname in sorted(need):
        k.hin(wname, WSHAPES[wname])
    k.out_t = nc.dram_tensor("out_t", [D, T], F32, kind="ExternalOutput").ap()
    k.row_scr = nc.dram_tensor("row_scr", [128, T], F32, kind="Internal").ap()
    k.spill = nc.dram_tensor("spill", [D, T], F32, kind="Internal").ap()
    k.spill2 = nc.dram_tensor("spill2", [D, T], F32, kind="Internal").ap()
    if "l2f" in blocks:
        U32 = mybir.dt.uint32
        IDN = 8 * 2176 + 8
        k.cc_t_in = nc.dram_tensor("cc_t_in", [T, D], F32, kind="Internal")
        k.cc_t_out = nc.dram_tensor("cc_t_out", [8 * T, D], F32, kind="Internal",
                                    addr_space="Shared")
        k.idt_in = nc.dram_tensor("idt_in", [IDN, 1], U32, kind="Internal")
        k.idt_out = nc.dram_tensor("idt_out", [8 * IDN, 1], U32, kind="Internal",
                                   addr_space="Shared")
        k.cc_y_in = nc.dram_tensor("cc_y_in", [T, D], F32, kind="Internal")
        k.cc_y_out = nc.dram_tensor("cc_y_out", [8 * T, D], F32, kind="Internal",
                                    addr_space="Shared")

    stack = contextlib.ExitStack()
    def sb(name, shape, dt=F32):
        tt_ = stack.enter_context(nc.sbuf_tensor(name, list(shape), dt))
        k.t[name] = tt_
        return tt_
    def psum(name, shape):
        tt_ = stack.enter_context(nc.psum_tensor(name, list(shape), F32))
        k.t[name] = tt_
        return tt_

    with stack:
        sb("x", (128, NCF * T))
        sb("y", (128, NCF * T))
        sb("big", (128, NCF * T))
        sb("stats", (128, T))
        sb("small", (128, 256))
        sb("wbuf", (128, 1024))
        sb("consts", (128, 652))
        sb("cid", (1, 1))
        if "l2f" in blocks:
            sb("u32a", (128, 64), dt=mybir.dt.uint32)
        psum("ps_a", (128, 4, 512))
        psum("ps_b", (128, 2, 512))
        psum("ps_c", (128, 2, 512))
        for s in Prog.SEMS:
            P.sems[s] = stack.enter_context(nc.semaphore("sem_" + s))

        with nc.Block() as block:
            emit_all(k)
            P.play(block)
    return k


def emit_all(k):
    P, t, hb = k.P, k.t, k.hb
    x = t["x"]
    ev_x = P.dma("sync", out=x[:].rearrange("p (c t) -> p c t", c=NCF),
                 in_=hb["x_t"].rearrange("(c p) t -> p c t", p=128))
    ev_cid = P.dma("sync", out=t["cid"][:], in_=hb["core_id"])
    ev_c = emit_consts(P, t)
    evs = [ev_x, ev_cid] + ev_c

    for b in k.blocks:
        evs = EMITTERS[b](k, evs)

    P.wait("gpsimd", evs)
    evo = P.dma("gpsimd", out=k.out_t.rearrange("(c p) t -> p c t", p=128),
                in_=x[:].rearrange("p (c t) -> p c t", c=NCF))
    P.wait("gpsimd", evo)
    for s in P.SEMS:
        if s.startswith("d") and P.cnt[s] > 0:
            def fw(eng, s=s, v=P.cnt[s]):
                eng.wait_ge(P.sems[s], v)
            P.queues["gpsimd"].append(fw)
    P.queues["gpsimd"].append(lambda eng: eng.wait_ge(P.sems["gpsimd"], P.cnt["gpsimd"]))


def emit_consts(P, t):
    c = t["consts"]
    e0 = P.op("gpsimd", lambda e: e.iota(c[:, 0:128], pattern=[[1, 128]], base=0,
                                         channel_multiplier=0, allow_small_or_imprecise_dtypes=True))
    e1 = P.op("gpsimd", lambda e: e.iota(c[:, 640:641], pattern=[[0, 1]], base=0,
                                         channel_multiplier=1, allow_small_or_imprecise_dtypes=True))
    e2 = P.op("gpsimd", lambda e: e.memset(c[:, 641:642], 1.0))
    e2b = P.op("gpsimd", lambda e: e.memset(c[:, 384:512], 1.0))
    P.wait("vector", [e0, e1, e2b])
    e3 = P.op("vector", lambda e: e.tensor_scalar(c[:, 128:256], c[:, 0:128], c[:, 640:641], None, ALU.is_ge))
    e4 = P.op("vector", lambda e: e.tensor_scalar(c[:, 256:384], c[:, 0:128], c[:, 640:641], None, ALU.is_gt))
    e4b = P.op("vector", lambda e: e.tensor_scalar(c[:, 512:640], c[:, 0:128], 64.0, None, ALU.is_ge))
    e4c = P.op("vector", lambda e: e.tensor_scalar(c[:, 642:643], c[:, 640:641], 64.0, None, ALU.is_ge))
    e4d = P.op("vector", lambda e: e.tensor_scalar(c[:, 512:640], c[:, 512:640], c[:, 642:643], None, ALU.is_equal))
    e5 = P.op("vector", lambda e: e.tensor_scalar(c[:, 0:128], c[:, 0:128], c[:, 640:641], None, ALU.is_equal))
    e6 = P.op("gpsimd", lambda e: e.iota(c[:, 643:651], pattern=[[1, 8]], base=0,
                                         channel_multiplier=0, allow_small_or_imprecise_dtypes=True))
    return [e2, e3, e4, e4d, e5, e6]



def bfence(k, evs):
    P = k.P
    return list(evs) + [P.fence(q) for q in ("vector", "scalar", "tensor", "gpsimd")]


def ps_wait_list(k, name, banks):
    out = []
    for b in banks:
        v = k.psfree.get((name, b))
        if v is None:
            continue
        out.extend(v if isinstance(v, list) else [v])
    return out


def ps_release(k, name, bank, ev):
    k.psfree[(name, bank)] = ev


def emit_ln(k, src_name, evs, dst_name="y", sq_scratch=None):
    P, t = k.P, k.t
    P.phase("ln")
    src, dst = t[src_name], t[dst_name]
    stats = t["stats"]
    pa, pb, pc = t["ps_a"], t["ps_b"], t["ps_c"]
    ao = ALLONES(t)
    sq = sq_scratch if sq_scratch is not None else xcol(t["big"], 0)

    def sq_bank(tt):
        return pb[:, tt, :] if tt < 2 else pc[:, tt - 2, :]

    P.wait("tensor", list(evs) + ps_wait_list(k, "ps_a", range(4))
           + ps_wait_list(k, "ps_b", range(2)) + ps_wait_list(k, "ps_c", range(2)))
    last_mm = None
    for c in range(NCF):
        e_sq = P.op("scalar", lambda e, c=c: e.activation(sq, xcol(src, c), AF.Square),
                    after=(evs if c == 0 else [last_mm]))
        for tt in range(NTT):
            P.op("tensor", lambda e, c=c, tt=tt: e.matmul(
                pa[:, tt, :], ao, xcol(src, c, tt * TTW, TTW),
                start=(c == 0), stop=(c == NCF - 1)), inc=False)
        P.wait("tensor", e_sq)
        for tt in range(NTT):
            ev = P.op("tensor", lambda e, tt=tt, c=c: e.matmul(
                sq_bank(tt), ao, sq[:, tt * TTW: tt * TTW + TTW],
                start=(c == 0), stop=(c == NCF - 1)), inc=(tt == NTT - 1))
        last_mm = ev
    P.wait("vector", last_mm)
    ev_nm = []
    for tt in range(NTT):
        ev = P.op("vector", lambda e, tt=tt: e.tensor_scalar(
            stats[:, tt * TTW:(tt + 1) * TTW], pa[:, tt, :], -1.0 / D, None, ALU.mult))
        ev_nm.append(ev)
        ps_release(k, "ps_a", tt, ev)
    add_evs = []
    for c in range(NCF):
        ev = P.op("vector", lambda e, c=c: e.tensor_tensor(xcol(dst, c), xcol(src, c), stats[:], ALU.add))
        add_evs.append(ev)
    ev_sq2 = P.op("vector", lambda e: e.tensor_tensor(stats[:], stats[:], stats[:], ALU.mult),
                  after=add_evs[-1])
    ev_var = []
    for tt in range(NTT):
        ev = P.op("vector", lambda e, tt=tt: e.scalar_tensor_tensor(
            stats[:, tt * TTW:(tt + 1) * TTW], sq_bank(tt), 1.0 / D,
            stats[:, tt * TTW:(tt + 1) * TTW], ALU.mult, ALU.subtract_rev)
            if False else e.scalar_tensor_tensor(
            stats[:, tt * TTW:(tt + 1) * TTW], sq_bank(tt), 1.0 / D,
            stats[:, tt * TTW:(tt + 1) * TTW], ALU.mult, ALU.subtract))
        ev_var.append(ev)
        ps_release(k, "ps_b" if tt < 2 else "ps_c", tt % 2, ev)
    ev_eps = P.op("vector", lambda e: e.tensor_scalar(stats[:], stats[:], 1e-5, None, ALU.add))
    P.wait("scalar", ev_eps)
    P.op("scalar", lambda e: e.activation(stats[:], stats[:], AF.Ln), inc=False)
    ev_r = P.op("scalar", lambda e: e.activation(stats[:], stats[:], AF.Exp, scale=-0.5))
    out_evs = []
    P.wait("vector", ev_r)
    for c in range(NCF):
        ev = P.op("vector", lambda e, c=c: e.tensor_tensor(xcol(dst, c), xcol(dst, c), stats[:], ALU.mult))
        out_evs.append(ev)
    P.phase("")
    return out_evs



def proj_fm(k, wname, src, src_base, HT, evs, writer, n_out=None, w_col_off=0,
            src_nchunks=NCF, psname="ps_a"):
    if src_base is None:
        src_base = 0
    """out[m-chunk, tokens HT] = sum_c W^T[c-chunk, m] @ src[c-chunk block of width HT].
    src: sbuf tensor with chunk c at cols [src_base + c*HT, +HT).
    writer(P, ps_ap, m, tt, ev_mm) -> ev  (must read psum and return completion ev).
    Returns list of writer events."""
    P, t, hb = k.P, k.t, k.hb
    wb = t["wbuf"]
    ps_ = t[psname]
    nbanks = 4 if psname == "ps_a" else 2
    ntt = max(1, HT // TTW)
    ttw = min(HT, TTW)
    wap = hb[wname]
    nks = src_nchunks
    nms = (n_out if n_out is not None else WSHAPES[wname][1]) // 128
    out_evs = []
    prev_use = getattr(k, "wbuf_free", None)
    for m in range(nms):
        ev_w = P.dma("sync",
                     out=wb[:, 0:nks * 128].rearrange("p (kk mm) -> p kk mm", kk=nks),
                     in_=wap[:, w_col_off + m * 128: w_col_off + (m + 1) * 128]
                         .rearrange("(kk p) mm -> p kk mm", p=128),
                     after=prev_use)
        P.wait("tensor", [ev_w] + (list(evs) if m == 0 else []))
        mm_evs = []
        for tt in range(ntt):
            bank = (m * ntt + tt) % nbanks
            P.wait("tensor", ps_wait_list(k, psname, [bank]))
            for c in range(nks):
                ev_mm = P.op("tensor", lambda e, m=m, tt=tt, c=c, bank=bank, ttw=ttw: e.matmul(
                    ps_[:, bank, 0:ttw], wb[:, c * 128:(c + 1) * 128],
                    src[:, src_base + c * HT + tt * ttw: src_base + c * HT + tt * ttw + ttw],
                    start=(c == 0), stop=(c == nks - 1)),
                    inc=(c == nks - 1))
            mm_evs.append((tt, bank, ev_mm))
        prev_use = mm_evs[-1][2]
        for (tt, bank, ev_mm) in mm_evs:
            ev = writer(P, ps_[:, bank, 0:ttw], m, tt, ev_mm)
            ps_release(k, psname, bank, ev)
            out_evs.append(ev)
    k.wbuf_free = prev_use
    return out_evs



def emit_l1f(k, evs):
    P, t, hb = k.P, k.t, k.hb
    x, y, big, small = t["x"], t["y"], t["big"], t["small"]
    evs = emit_ln(k, "x", evs, dst_name="y")
    _bf = bfence(k, evs)
    ev_mu = P.dma("sync", out=small[:, 16:24], in_=hb["l1f_mur"], after=_bf)
    ev_mk = P.dma("sync", out=small[:, 24:32], in_=hb["l1f_muk"])
    P.wait("vector", [ev_mu, ev_mk])
    ev_m2 = P.op("vector", lambda e: e.tensor_scalar(small[:, 32:48], small[:, 16:32], -1.0, 1.0, ALU.mult, ALU.add))
    HT = 512
    B = 8 * HT
    xr_b, xk_b, r_b, ka_b = 0, B, 2 * B, 3 * B
    x_evs = list(evs)
    for quart in range(4):
        t0 = quart * HT
        mu_r = lambda c: small[:, 16 + c:17 + c]
        mu_k = lambda c: small[:, 24 + c:25 + c]
        om_r = lambda c: small[:, 32 + c:33 + c]
        om_k = lambda c: small[:, 40 + c:41 + c]
        pre = []
        for c in range(NCF):
            for (base, mu, om) in ((xr_b, mu_r, om_r), (xk_b, mu_k, om_k)):
                col0 = lambda off, n, base=base, c=c: big[:, base + c * HT + off: base + c * HT + off + n]
                if quart == 0:
                    P.op("vector", lambda e, c=c, col0=col0, mu=mu: e.tensor_scalar(
                        col0(0, 1), xcol(y, c, 0, 1), mu(c), None, ALU.mult), inc=False,
                        after=(list(evs) + [ev_m2] if c == 0 and base == xr_b else None))
                else:
                    P.op("vector", lambda e, c=c, col0=col0, om=om, t0=t0: e.tensor_scalar(
                        col0(0, 1), xcol(y, c, t0 - 1, 1), om(c), None, ALU.mult), inc=False,
                        after=(x_evs if c == 0 and base == xr_b else None))
                    P.op("vector", lambda e, c=c, col0=col0, mu=mu, t0=t0: e.scalar_tensor_tensor(
                        col0(0, 1), xcol(y, c, t0, 1), mu(c), col0(0, 1), ALU.mult, ALU.add), inc=False)
                P.op("vector", lambda e, c=c, col0=col0, mu=mu, t0=t0: e.tensor_scalar(
                    col0(1, HT - 1), xcol(y, c, t0 + 1, HT - 1), mu(c), None, ALU.mult), inc=False)
                e2 = P.op("vector", lambda e, c=c, col0=col0, om=om, t0=t0: e.scalar_tensor_tensor(
                    col0(1, HT - 1), xcol(y, c, t0, HT - 1), om(c), col0(1, HT - 1), ALU.mult, ALU.add))
                pre.append(e2)
        def sigw(P_, ps_ap, m, tt, ev, r_b=r_b, HT=HT):
            return P_.op("scalar", lambda e: e.activation(
                big[:, r_b + m * HT: r_b + m * HT + HT], ps_ap, AF.Sigmoid), after=ev)
        r_evs = proj_fm(k, "l1f_wrT", big, xr_b, HT, pre, sigw)
        def kaw(P_, ps_ap, m, tt, ev, ka_b=ka_b, HT=HT):
            dst_ap = big[:, ka_b + m * HT: ka_b + m * HT + HT]
            e1 = P_.op("vector", lambda e: e.tensor_scalar(dst_ap, ps_ap, 0.0, None, ALU.max), after=ev)
            return P_.op("scalar", lambda e: e.activation(dst_ap, dst_ap, AF.Square), after=e1)
        ka_evs = proj_fm(k, "l1f_wkT", big, xk_b, HT, r_evs, kaw)
        import os
        dbg = os.environ.get("DBGL1F", "")
        if dbg:
            srcb = {"xr": xr_b, "xk": xk_b, "r": r_b, "ka": ka_b}[dbg]
            need = {"xr": pre, "xk": pre, "r": r_evs, "ka": ka_evs}[dbg]
            x_evs = []
            P.wait("vector", need)
            for c in range(NCF):
                x_evs.append(P.op("vector", lambda e, c=c, srcb=srcb, t0=t0: e.tensor_copy(
                    xcol(t["x"], c, t0, HT), big[:, srcb + c * HT: srcb + c * HT + HT])))
            evs = x_evs
            continue
        def finw(P_, ps_ap, m, tt, ev, r_b=r_b, HT=HT, t0=t0):
            rr = big[:, r_b + m * HT: r_b + m * HT + HT]
            e1 = P_.op("vector", lambda e: e.tensor_tensor(rr, rr, ps_ap, ALU.mult), after=ev)
            return P_.op("vector", lambda e, m=m: e.tensor_tensor(
                xcol(t["x"], m, t0, HT), xcol(t["x"], m, t0, HT), rr, ALU.add), after=e1)
        x_evs = proj_fm(k, "l1f_wvT", big, ka_b, HT, ka_evs, finw)
    return x_evs


def emit_l0f(k, evs):
    P, t, hb = k.P, k.t, k.hb
    x, y, big, small, stats = t["x"], t["y"], t["big"], t["small"], t["stats"]
    pa, pb, pc = t["ps_a"], t["ps_b"], t["ps_c"]
    evs = emit_ln(k, "x", evs, dst_name="y")
    evw = []
    evw.append(P.dma("sync", out=big[:, 4096:4864].rearrange("p (c r) -> p c r", c=8),
                     in_=hb["l0f_routedAT"].rearrange("(c p) r -> p c r", p=128),
                     after=getattr(k, "wbuf_free", None)))
    evw.append(P.dma("sync", out=big[:, 4864:4992].rearrange("p (c r) -> p c r", c=8),
                     in_=hb["l0f_sharedAT"].rearrange("(c p) r -> p c r", p=128)))
    evw.append(P.dma("sync", out=big[:, 4992:5040].rearrange("p (c r) -> p c r", c=8),
                     in_=hb["l0f_routerT"].rearrange("(c p) r -> p c r", p=128)))
    evw.append(P.dma("sync", out=big[0:16, 5120:6144], in_=hb["l0f_sharedBT"]))
    evw.append(P.dma("sync", out=big[0:96, 6144:7168], in_=hb["l0f_routedBT"]))
    evw.append(P.dma("sync", out=big[0:6, 7168:7264], in_=hb["l0f_blk16"]))
    evw.append(P.dma("sync", out=big[:, 7264:7270], in_=hb["l0f_routerb"]))
    evw.append(P.dma("sync", out=small[:, 48:56], in_=hb["l0f_baseb"], after=bfence(k, evs)))
    ev_ms1 = P.op("gpsimd", lambda e: e.memset(small[:, 64:72], 0.0))
    ev_ms2 = P.op("gpsimd", lambda e: e.memset(stats[:, 0:T], 0.0), after=evs)
    P.wait("vector", [ev_ms1, ev_ms2])
    w6_evs = []
    for tl in range(16):
        P.wait("tensor", evw + (evs if tl == 0 else []) + ps_wait_list(k, "ps_b", [0, 1]))
        for c in range(NCF):
            ev_mm = P.op("tensor", lambda e, tl=tl, c=c: e.matmul(
                pb[:, 0, 0:6], xcol(y, c, tl * 128, 128), big[:, 4992 + c * 6: 4998 + c * 6],
                start=(c == 0), stop=(c == NCF - 1)), inc=(c == NCF - 1))
        e1 = P.op("vector", lambda e: e.tensor_tensor(small[:, 64:70], pb[:, 0, 0:6], big[:, 7264:7270], ALU.add), after=ev_mm, drain=True)
        ps_release(k, "ps_b", 0, e1)
        e2 = P.op("vector", lambda e: e.reduce_max(small[:, 72:73], small[:, 64:70], axis=AX.X, negate=True), drain=True)
        P.wait("scalar", e2)
        e3 = P.op("scalar", lambda e: e.activation(small[:, 64:70], small[:, 64:70], AF.Exp,
                                                   bias=small[:, 72:73], accum_out=small[:, 73:74]), drain=True)
        P.wait("vector", e3)
        e4 = P.op("vector", lambda e: e.reciprocal(small[:, 72:73], small[:, 73:74]), drain=True)
        e5 = P.op("vector", lambda e: e.tensor_scalar(small[:, 64:70], small[:, 64:70], small[:, 72:73], None, ALU.mult), drain=True)
        P.op("vector", lambda e: e.reduce_max(small[:, 72:73], small[:, 64:70], axis=AX.X), inc=False, drain=True)
        P.op("vector", lambda e: e.tensor_scalar(small[:, 74:80], small[:, 64:70], small[:, 72:73], None, ALU.is_equal), inc=False, drain=True)
        P.op("vector", lambda e: e.scalar_tensor_tensor(small[:, 74:80], small[:, 74:80], -65536.0, IOTA8(t)[:, 0:6], ALU.mult, ALU.add), inc=False, drain=True)
        P.op("vector", lambda e: e.tensor_reduce(small[:, 73:74], small[:, 74:80], axis=AX.X, op=ALU.min), inc=False, drain=True)
        P.op("vector", lambda e: e.tensor_scalar(small[:, 73:74], small[:, 73:74], 65536.0, None, ALU.add), inc=False, drain=True)
        P.op("vector", lambda e: e.tensor_scalar(small[:, 74:80], IOTA8(t)[:, 0:6], small[:, 73:74], None, ALU.is_equal), inc=False, drain=True)
        P.op("vector", lambda e: e.tensor_copy(small[:, 82:83], small[:, 72:73]), inc=False, drain=True)
        P.op("vector", lambda e: e.scalar_tensor_tensor(small[:, 84:90], small[:, 74:80], -65536.0, small[:, 64:70], ALU.mult, ALU.add), inc=False, drain=True)
        P.op("vector", lambda e: e.reduce_max(small[:, 83:84], small[:, 84:90], axis=AX.X), inc=False, drain=True)
        P.op("vector", lambda e: e.tensor_tensor(small[:, 82:83], small[:, 82:83], small[:, 83:84], ALU.add), inc=False, drain=True)
        P.op("vector", lambda e: e.tensor_scalar(small[:, 82:83], small[:, 82:83], 1e-6, None, ALU.add), inc=False, drain=True)
        P.op("vector", lambda e: e.tensor_scalar(small[:, 90:96], small[:, 84:90], small[:, 83:84], None, ALU.is_equal), inc=False, drain=True)
        P.op("vector", lambda e: e.scalar_tensor_tensor(small[:, 90:96], small[:, 90:96], -65536.0, IOTA8(t)[:, 0:6], ALU.mult, ALU.add), inc=False, drain=True)
        P.op("vector", lambda e: e.tensor_reduce(small[:, 73:74], small[:, 90:96], axis=AX.X, op=ALU.min), inc=False, drain=True)
        P.op("vector", lambda e: e.tensor_scalar(small[:, 73:74], small[:, 73:74], 65536.0, None, ALU.add), inc=False, drain=True)
        P.op("vector", lambda e: e.tensor_scalar(small[:, 90:96], IOTA8(t)[:, 0:6], small[:, 73:74], None, ALU.is_equal), inc=False, drain=True)
        P.op("vector", lambda e: e.tensor_tensor(small[:, 74:80], small[:, 74:80], small[:, 90:96], ALU.add), inc=False, drain=True)
        P.op("vector", lambda e: e.tensor_tensor(small[:, 64:70], small[:, 64:70], small[:, 74:80], ALU.mult), inc=False, drain=True)
        P.op("vector", lambda e: e.reciprocal(small[:, 82:83], small[:, 82:83]), inc=False, drain=True)
        e6 = P.op("vector", lambda e: e.tensor_scalar(small[:, 64:72], small[:, 64:72], small[:, 82:83], None, ALU.mult), drain=True)
        import os as _os
        if _os.environ.get("DBGL0F") == "g":
            e6 = P.op("vector", lambda e: e.tensor_copy(small[:, 64:72], small[:, 64:72]))
        P.wait("tensor", [e6] + ps_wait_list(k, "ps_b", [1]))
        e7 = P.op("tensor", lambda e: e.transpose(pb[0:8, 1, 0:128], small[:, 64:72], IDENT(t)))
        e8 = P.op("vector", lambda e, tl=tl: e.tensor_copy(stats[0:6, tl * 128:(tl + 1) * 128], pb[0:6, 1, 0:128]), after=e7)
        ps_release(k, "ps_b", 1, e8)
        w6_evs.append(e8)
    import os
    if os.environ.get("DBGL0F") == "w6":
        x_evs = []
        P.wait("vector", w6_evs)
        x_evs.append(P.op("vector", lambda e: e.tensor_copy(xcol(x, 0)[0:6, :], stats[0:6, :])))
        return x_evs
    hall_evs = []
    for tt in range(NTT):
        P.wait("tensor", ps_wait_list(k, "ps_a", [tt % 4]))
        for c in range(NCF):
            ev_mm = P.op("tensor", lambda e, tt=tt, c=c: e.matmul(
                pa[0:96, tt, :], big[:, 4096 + c * 96: 4096 + (c + 1) * 96],
                xcol(y, c, tt * TTW, TTW), start=(c == 0), stop=(c == NCF - 1)),
                inc=(c == NCF - 1))
        ev = P.op("vector", lambda e, tt=tt: e.tensor_copy(
            big[0:96, tt * TTW:(tt + 1) * TTW], pa[0:96, tt, :]), after=ev_mm)
        ps_release(k, "ps_a", tt % 4, ev)
        hall_evs.append(ev)
    hs_evs = []
    for tt in range(NTT):
        P.wait("tensor", ps_wait_list(k, "ps_c", [tt % 2]))
        for c in range(NCF):
            ev_mm = P.op("tensor", lambda e, tt=tt, c=c: e.matmul(
                pc[0:16, tt % 2, :], big[:, 4864 + c * 16: 4864 + (c + 1) * 16],
                xcol(y, c, tt * TTW, TTW), start=(c == 0), stop=(c == NCF - 1)),
                inc=(c == NCF - 1))
        ev = P.op("vector", lambda e, tt=tt: e.tensor_copy(
            big[0:16, 2048 + tt * TTW: 2048 + (tt + 1) * TTW], pc[0:16, tt % 2, :]), after=ev_mm)
        ps_release(k, "ps_c", tt % 2, ev)
        hs_evs.append(ev)
    hw_evs = []
    for tt in range(NTT):
        P.wait("tensor", w6_evs + hall_evs + ps_wait_list(k, "ps_b", [0]))
        ev_mm = P.op("tensor", lambda e, tt=tt: e.matmul(
            pb[0:96, 0, :], big[0:6, 7168:7264], stats[0:6, tt * TTW:(tt + 1) * TTW],
            start=True, stop=True))
        ev = P.op("vector", lambda e, tt=tt: e.tensor_tensor(
            big[0:96, tt * TTW:(tt + 1) * TTW], big[0:96, tt * TTW:(tt + 1) * TTW],
            pb[0:96, 0, :], ALU.mult), after=ev_mm)
        ps_release(k, "ps_b", 0, ev)
        hw_evs.append(ev)
    wb = t["wbuf"]
    x_evs = []
    prev_use = getattr(k, "wbuf_free", None)
    for m in range(8):
        ev_w = P.dma("sync", out=wb[:, 0:1024].rearrange("p (kk mm) -> p kk mm", kk=8),
                     in_=hb["l0f_baseT"][:, m * 128:(m + 1) * 128].rearrange("(kk p) mm -> p kk mm", p=128),
                     after=prev_use)
        P.wait("tensor", [ev_w] + (hs_evs + hw_evs if m == 0 else []))
        for tt in range(NTT):
            bank = tt
            P.wait("tensor", ps_wait_list(k, "ps_a", [bank]))
            for c in range(NCF):
                P.op("tensor", lambda e, tt=tt, c=c, bank=bank: e.matmul(
                    pa[:, bank, :], wb[:, c * 128:(c + 1) * 128],
                    xcol(y, c, tt * TTW, TTW), start=(c == 0), stop=False), inc=False)
            P.op("tensor", lambda e, tt=tt, m=m, bank=bank: e.matmul(
                pa[:, bank, :], big[0:16, 5120 + m * 128: 5120 + (m + 1) * 128],
                big[0:16, 2048 + tt * TTW: 2048 + (tt + 1) * TTW], start=False, stop=False), inc=False)
            ev_mm = P.op("tensor", lambda e, tt=tt, m=m, bank=bank: e.matmul(
                pa[:, bank, :], big[0:96, 6144 + m * 128: 6144 + (m + 1) * 128],
                big[0:96, tt * TTW:(tt + 1) * TTW], start=False, stop=True))
            ev = P.op("vector", lambda e, tt=tt, m=m, bank=bank: e.scalar_tensor_tensor(
                xcol(x, m, tt * TTW, TTW), pa[:, bank, :], small[:, 48 + m:49 + m],
                xcol(x, m, tt * TTW, TTW), ALU.add, ALU.add), after=ev_mm)
            ps_release(k, "ps_a", bank, ev)
            x_evs.append(ev)
        prev_use = ev_mm
    k.wbuf_free = prev_use
    return x_evs


def emit_l2a(k, evs):
    P, t, hb = k.P, k.t, k.hb
    x, y, big, small = t["x"], t["y"], t["big"], t["small"]
    pa, pb, pc = t["ps_a"], t["ps_b"], t["ps_c"]
    wb = t["wbuf"]
    evs = emit_ln(k, "x", evs, dst_name="y")
    ev_b = [P.dma("sync", out=small[:, 16:24], in_=hb["l2a_bq"], after=bfence(k, evs)),
            P.dma("sync", out=small[:, 24:32], in_=hb["l2a_bk"]),
            P.dma("sync", out=small[:, 32:40], in_=hb["l2a_bo"])]
    ev_b.append(P.dma("sync", out=big[0:1, 15360:16384], in_=hb["l2a_bv"]))
    Q2, K2, V2, PB, O2, PT = 0, 2048, 4096, 6144, 8192, 10240
    x_evs = list(evs)
    prev_wb = getattr(k, "wbuf_free", None)
    for g in range(8):
        proj_evs = []
        for (wn, dstb, bc) in (("l2a_wq", Q2, 16), ("l2a_wk", K2, 24)):
            ev_w = P.dma("sync", out=wb[:, 0:1024].rearrange("p (kk mm) -> p kk mm", kk=8),
                         in_=hb[wn][:, g * 128:(g + 1) * 128].rearrange("(kk p) mm -> p kk mm", p=128),
                         after=prev_wb)
            P.wait("tensor", [ev_w] + (x_evs + ev_b if g == 0 else []))
            for tt in range(NTT):
                bank = tt
                P.wait("tensor", ps_wait_list(k, "ps_a", [bank]))
                for c in range(NCF):
                    ev_mm = P.op("tensor", lambda e, tt=tt, c=c, bank=bank: e.matmul(
                        pa[:, bank, :], wb[:, c * 128:(c + 1) * 128],
                        xcol(y, c, tt * TTW, TTW), start=(c == 0), stop=(c == NCF - 1)),
                        inc=(c == NCF - 1))
                ev = P.op("vector", lambda e, tt=tt, dstb=dstb, bank=bank, bc=bc, g=g: e.tensor_scalar(
                    big[:, dstb + tt * TTW: dstb + (tt + 1) * TTW], pa[:, bank, :],
                    small[:, bc + g: bc + g + 1], None, ALU.add), after=ev_mm)
                ps_release(k, "ps_a", bank, ev)
                proj_evs.append(ev)
            prev_wb = ev_mm
        ev_w = P.dma("sync", out=wb[:, 0:1024].rearrange("p (kk mm) -> p kk mm", kk=8),
                     in_=hb["l2a_wv"][:, g * 128:(g + 1) * 128].rearrange("(kk p) mm -> p kk mm", p=128),
                     after=prev_wb)
        P.wait("tensor", [ev_w])
        for tl in range(16):
            bank = tl % 4
            P.wait("tensor", ps_wait_list(k, "ps_a", [bank]))
            for c in range(NCF):
                P.op("tensor", lambda e, tl=tl, c=c, bank=bank: e.matmul(
                    pa[:, bank, 0:128], xcol(y, c, tl * 128, 128), wb[:, c * 128:(c + 1) * 128],
                    start=(c == 0), stop=False), inc=False)
            ev_mm = P.op("tensor", lambda e, bank=bank, g=g: e.matmul(
                pa[:, bank, 0:128], ALLONES(t)[0:1, 0:128], big[0:1, 15360 + g * 128: 15360 + (g + 1) * 128],
                start=False, stop=True))
            ev = P.op("vector", lambda e, tl=tl, bank=bank: e.tensor_copy(
                big[:, V2 + tl * 128: V2 + (tl + 1) * 128], pa[:, bank, 0:128]), after=ev_mm)
            ps_release(k, "ps_a", bank, ev)
            proj_evs.append(ev)
        prev_wb = ev_mm
        for hh in range(2):
            qb = hh * 64
            for qi in range(16):
                ns = qi + 1
                nsb = (ns + 3) // 4
                P.wait("tensor", proj_evs if (hh == 0 and qi == 0) else None)
                mm_evs = []
                for sb in range(nsb):
                    w_ = min(512, ns * 128 - sb * 512)
                    bank = sb
                    P.wait("tensor", ps_wait_list(k, "ps_a", [bank]))
                    ev_mm = P.op("tensor", lambda e, sb=sb, w_=w_, qi=qi, qb=qb, bank=bank: e.matmul(
                        pa[:, bank, 0:w_], big[qb:qb + 64, Q2 + qi * 128: Q2 + (qi + 1) * 128],
                        big[qb:qb + 64, K2 + sb * 512: K2 + sb * 512 + w_], start=True, stop=True))
                    mm_evs.append((sb, w_, bank, ev_mm))
                pevs = []
                for (sb, w_, bank, ev_mm) in mm_evs:
                    if (sb + 1) * 512 >= ns * 128:
                        dw = w_ - 128
                        if dw > 0:
                            P.op("vector", lambda e, sb=sb, dw=dw, bank=bank: e.tensor_copy(
                                big[:, PB + sb * 512: PB + sb * 512 + dw], pa[:, bank, 0:dw]),
                                after=ev_mm, inc=False)
                        ev = P.op("vector", lambda e, sb=sb, dw=dw, w_=w_, bank=bank: e.scalar_tensor_tensor(
                            big[:, PB + sb * 512 + dw: PB + sb * 512 + w_], MASK_GT(t), -30000.0,
                            pa[:, bank, dw:w_], ALU.mult, ALU.add), after=ev_mm)
                    else:
                        ev = P.op("vector", lambda e, sb=sb, w_=w_, bank=bank: e.tensor_copy(
                            big[:, PB + sb * 512: PB + sb * 512 + w_], pa[:, bank, 0:w_]), after=ev_mm)
                    ps_release(k, "ps_a", bank, ev)
                    pevs.append(ev)
                W = ns * 128
                P.wait("vector", pevs)
                P.op("vector", lambda e, W=W: e.reduce_max(small[:, 64:65], big[:, PB:PB + W], axis=AX.X),
                     inc=False, drain=True)
                ev_m = P.op("vector", lambda e: e.tensor_scalar(small[:, 65:66], small[:, 64:65], -0.125, None, ALU.mult), drain=True)
                P.wait("scalar", ev_m)
                ev_e = P.op("scalar", lambda e, W=W: e.activation(big[:, PB:PB + W], big[:, PB:PB + W], AF.Exp,
                                                                  bias=small[:, 65:66], scale=0.125,
                                                                  accum_out=small[:, 66:67]))
                P.wait("vector", ev_e)
                ev_r = P.op("vector", lambda e: e.reciprocal(small[:, 67:68], small[:, 66:67]), drain=True)
                ev_p = P.op("vector", lambda e, W=W: e.tensor_scalar(big[:, PB:PB + W], big[:, PB:PB + W],
                                                                     small[:, 67:68], None, ALU.mult), drain=True)
                P.wait("tensor", ev_p)
                ev_o = None
                for sb in range(ns):
                    P.wait("tensor", ps_wait_list(k, "ps_b", [0]))
                    ev_t = P.op("tensor", lambda e, sb=sb: e.transpose(
                        pb[:, 0, 0:128], big[:, PB + sb * 128: PB + (sb + 1) * 128], IDENT(t)))
                    ev_c = P.op("vector", lambda e: e.tensor_copy(big[:, PT:PT + 128], pb[:, 0, 0:128]),
                                after=ev_t)
                    ps_release(k, "ps_b", 0, ev_c)
                    P.wait("tensor", ev_c if sb > 0 else [ev_c] + ps_wait_list(k, "ps_b", [1]))
                    ev_o = P.op("tensor", lambda e, sb=sb, qb=qb, ns=ns: e.matmul(
                        pb[qb:qb + 64, 1, 0:128],
                        big[:, V2 + sb * 128 + qb: V2 + sb * 128 + qb + 64],
                        big[:, PT:PT + 128], start=(sb == 0), stop=(sb == ns - 1)))
                ev_oe = P.op("vector", lambda e, qb=qb, qi=qi: e.tensor_copy(
                    big[qb:qb + 64, O2 + qi * 128: O2 + (qi + 1) * 128], pb[qb:qb + 64, 1, 0:128]),
                    after=ev_o)
                ps_release(k, "ps_b", 1, ev_oe)
                x_evs.append(ev_oe)
        ev_w = P.dma("sync", out=wb[:, 0:1024].rearrange("p (kk mm) -> p kk mm", kk=8),
                     in_=hb["l2a_wo"][g * 128:(g + 1) * 128, :].rearrange("p (kk mm) -> p kk mm", kk=8),
                     after=prev_wb)
        P.wait("tensor", [ev_w] + x_evs[-32:])
        gevs = []
        for m in range(8):
            for tt in range(NTT):
                bank = tt
                P.wait("tensor", ps_wait_list(k, "ps_c", [bank % 2]))
                ev_mm = P.op("tensor", lambda e, m=m, tt=tt, bank=bank: e.matmul(
                    pc[:, bank % 2, :], wb[:, m * 128:(m + 1) * 128],
                    big[:, O2 + tt * TTW: O2 + (tt + 1) * TTW], start=True, stop=True))
                if g == 7:
                    ev = P.op("vector", lambda e, m=m, tt=tt, bank=bank: e.scalar_tensor_tensor(
                        xcol(x, m, tt * TTW, TTW), pc[:, bank % 2, :], small[:, 32 + m:33 + m],
                        xcol(x, m, tt * TTW, TTW), ALU.add, ALU.add), after=ev_mm)
                else:
                    ev = P.op("vector", lambda e, m=m, tt=tt, bank=bank: e.tensor_tensor(
                        xcol(x, m, tt * TTW, TTW), xcol(x, m, tt * TTW, TTW), pc[:, bank % 2, :],
                        ALU.add), after=ev_mm)
                ps_release(k, "ps_c", bank % 2, ev)
                gevs.append(ev)
        prev_wb = ev_mm
        x_evs = gevs
    k.wbuf_free = prev_wb
    return x_evs


def emit_l3a(k, evs):
    P, t, hb = k.P, k.t, k.hb
    x, y, big, small = t["x"], t["y"], t["big"], t["small"]
    pa, pb, pc = t["ps_a"], t["ps_b"], t["ps_c"]
    wb = t["wbuf"]
    evs = emit_ln(k, "x", evs, dst_name="y")
    Q2, KB, VB, PB, O2, PT = 0, 2048, 6144, 10240, 12288, 14336
    prev_wb = getattr(k, "wbuf_free", None)
    kv_evs = []
    for m in range(2):
        ev_w = P.dma("sync", out=wb[:, 0:1024].rearrange("p (kk mm) -> p kk mm", kk=8),
                     in_=hb["l3a_wkT"][:, m * 128:(m + 1) * 128].rearrange("(kk p) mm -> p kk mm", p=128),
                     after=prev_wb)
        P.wait("tensor", [ev_w] + (list(evs) if m == 0 else []))
        for tt in range(NTT):
            bank = tt
            P.wait("tensor", ps_wait_list(k, "ps_a", [bank]))
            for c in range(NCF):
                ev_mm = P.op("tensor", lambda e, tt=tt, c=c, bank=bank: e.matmul(
                    pa[:, bank, :], wb[:, c * 128:(c + 1) * 128],
                    xcol(y, c, tt * TTW, TTW), start=(c == 0), stop=(c == NCF - 1)),
                    inc=(c == NCF - 1))
            ev = P.op("vector", lambda e, m=m, tt=tt, bank=bank: e.tensor_copy(
                big[:, KB + m * 2048 + tt * TTW: KB + m * 2048 + (tt + 1) * TTW], pa[:, bank, :]),
                after=ev_mm)
            ps_release(k, "ps_a", bank, ev)
            kv_evs.append(ev)
        prev_wb = ev_mm
    for vh in range(2):
        ev_w = P.dma("sync", out=wb[:, 0:1024].rearrange("p (kk mm) -> p kk mm", kk=8),
                     in_=hb["l3a_wvT"][:, vh * 128:(vh + 1) * 128].rearrange("(kk p) mm -> p kk mm", p=128),
                     after=prev_wb)
        P.wait("tensor", [ev_w])
        for tl in range(16):
            bank = tl % 4
            P.wait("tensor", ps_wait_list(k, "ps_a", [bank]))
            for c in range(NCF):
                ev_mm = P.op("tensor", lambda e, tl=tl, c=c, bank=bank: e.matmul(
                    pa[:, bank, 0:128], xcol(y, c, tl * 128, 128), wb[:, c * 128:(c + 1) * 128],
                    start=(c == 0), stop=(c == NCF - 1)), inc=(c == NCF - 1))
            ev = P.op("vector", lambda e, tl=tl, vh=vh, bank=bank: e.tensor_copy(
                big[:, VB + tl * 256 + vh * 128: VB + tl * 256 + vh * 128 + 128], pa[:, bank, 0:128]),
                after=ev_mm)
            ps_release(k, "ps_a", bank, ev)
            kv_evs.append(ev)
        prev_wb = ev_mm
    x_evs = []
    for g in range(8):
        proj_evs = []
        ev_w = P.dma("sync", out=wb[:, 0:1024].rearrange("p (kk mm) -> p kk mm", kk=8),
                     in_=hb["l3a_wqT"][:, g * 128:(g + 1) * 128].rearrange("(kk p) mm -> p kk mm", p=128),
                     after=prev_wb)
        P.wait("tensor", [ev_w] + (kv_evs if g == 0 else []))
        for tt in range(NTT):
            bank = tt
            P.wait("tensor", ps_wait_list(k, "ps_a", [bank]))
            for c in range(NCF):
                ev_mm = P.op("tensor", lambda e, tt=tt, c=c, bank=bank: e.matmul(
                    pa[:, bank, :], wb[:, c * 128:(c + 1) * 128],
                    xcol(y, c, tt * TTW, TTW), start=(c == 0), stop=(c == NCF - 1)),
                    inc=(c == NCF - 1))
            ev = P.op("vector", lambda e, tt=tt, bank=bank: e.tensor_copy(
                big[:, Q2 + tt * TTW: Q2 + (tt + 1) * TTW], pa[:, bank, :]), after=ev_mm)
            ps_release(k, "ps_a", bank, ev)
            proj_evs.append(ev)
        prev_wb = ev_mm
        PERM = [0, 4, 1, 5, 2, 6, 3, 7, 8, 12, 9, 13, 10, 14, 11, 15]
        for hh in range(2):
            qb = hh * 64
            h = PERM[g * 2 + hh]
            n = h // 4
            kbase = KB + (n // 2) * 2048
            krow = (n % 2) * 64
            assert krow == qb, (g, hh, h, n)
            for qi in range(16):
                ns = qi + 1
                nsb = (ns + 3) // 4
                P.wait("tensor", proj_evs if (hh == 0 and qi == 0) else None)
                mm_evs = []
                for sb in range(nsb):
                    w_ = min(512, ns * 128 - sb * 512)
                    bank = sb
                    P.wait("tensor", ps_wait_list(k, "ps_a", [bank]))
                    ev_mm = P.op("tensor", lambda e, sb=sb, w_=w_, qi=qi, qb=qb, bank=bank, kbase=kbase, krow=krow: e.matmul(
                        pa[:, bank, 0:w_], big[qb:qb + 64, Q2 + qi * 128: Q2 + (qi + 1) * 128],
                        big[krow:krow + 64, kbase + sb * 512: kbase + sb * 512 + w_], start=True, stop=True))
                    mm_evs.append((sb, w_, bank, ev_mm))
                pevs = []
                for (sb, w_, bank, ev_mm) in mm_evs:
                    if (sb + 1) * 512 >= ns * 128:
                        dw = w_ - 128
                        if dw > 0:
                            P.op("vector", lambda e, sb=sb, dw=dw, bank=bank: e.tensor_copy(
                                big[:, PB + sb * 512: PB + sb * 512 + dw], pa[:, bank, 0:dw]),
                                after=ev_mm, inc=False)
                        ev = P.op("vector", lambda e, sb=sb, dw=dw, w_=w_, bank=bank: e.scalar_tensor_tensor(
                            big[:, PB + sb * 512 + dw: PB + sb * 512 + w_], MASK_GT(t), -30000.0,
                            pa[:, bank, dw:w_], ALU.mult, ALU.add), after=ev_mm)
                    else:
                        ev = P.op("vector", lambda e, sb=sb, w_=w_, bank=bank: e.tensor_copy(
                            big[:, PB + sb * 512: PB + sb * 512 + w_], pa[:, bank, 0:w_]), after=ev_mm)
                    ps_release(k, "ps_a", bank, ev)
                    pevs.append(ev)
                W = ns * 128
                P.wait("vector", pevs)
                P.op("vector", lambda e, W=W: e.reduce_max(small[:, 64:65], big[:, PB:PB + W], axis=AX.X),
                     inc=False, drain=True)
                ev_m = P.op("vector", lambda e: e.tensor_scalar(small[:, 65:66], small[:, 64:65], -0.125, None, ALU.mult), drain=True)
                P.wait("scalar", ev_m)
                ev_e = P.op("scalar", lambda e, W=W: e.activation(big[:, PB:PB + W], big[:, PB:PB + W], AF.Exp,
                                                                  bias=small[:, 65:66], scale=0.125,
                                                                  accum_out=small[:, 66:67]))
                P.wait("vector", ev_e)
                ev_r = P.op("vector", lambda e: e.reciprocal(small[:, 67:68], small[:, 66:67]), drain=True)
                ev_p = P.op("vector", lambda e, W=W: e.tensor_scalar(big[:, PB:PB + W], big[:, PB:PB + W],
                                                                     small[:, 67:68], None, ALU.mult), drain=True)
                P.wait("tensor", ev_p)
                ev_o = None
                for sb in range(ns):
                    P.wait("tensor", ps_wait_list(k, "ps_b", [0]))
                    ev_t = P.op("tensor", lambda e, sb=sb: e.transpose(
                        pb[:, 0, 0:128], big[:, PB + sb * 128: PB + (sb + 1) * 128], IDENT(t)))
                    ev_c = P.op("vector", lambda e: e.tensor_copy(big[:, PT:PT + 128], pb[:, 0, 0:128]),
                                after=ev_t)
                    ps_release(k, "ps_b", 0, ev_c)
                    P.wait("tensor", ev_c if sb > 0 else [ev_c] + ps_wait_list(k, "ps_b", [1]))
                    ev_o = P.op("tensor", lambda e, sb=sb, qb=qb, ns=ns, n=n: e.matmul(
                        pb[qb:qb + 64, 1, 0:128],
                        big[:, VB + sb * 256 + n * 64: VB + sb * 256 + n * 64 + 64],
                        big[:, PT:PT + 128], start=(sb == 0), stop=(sb == ns - 1)))
                ev_oe = P.op("vector", lambda e, qb=qb, qi=qi: e.tensor_copy(
                    big[qb:qb + 64, O2 + qi * 128: O2 + (qi + 1) * 128], pb[qb:qb + 64, 1, 0:128]),
                    after=ev_o)
                ps_release(k, "ps_b", 1, ev_oe)
                x_evs.append(ev_oe)
        ev_w = P.dma("sync", out=wb[:, 0:1024].rearrange("p (kk mm) -> p kk mm", kk=8),
                     in_=hb["l3a_woT"][g * 128:(g + 1) * 128, :].rearrange("p (kk mm) -> p kk mm", kk=8),
                     after=prev_wb)
        P.wait("tensor", [ev_w] + x_evs[-32:])
        gevs = []
        for m in range(8):
            for tt in range(NTT):
                bank = tt % 2
                P.wait("tensor", ps_wait_list(k, "ps_c", [bank]))
                ev_mm = P.op("tensor", lambda e, m=m, tt=tt, bank=bank: e.matmul(
                    pc[:, bank, :], wb[:, m * 128:(m + 1) * 128],
                    big[:, O2 + tt * TTW: O2 + (tt + 1) * TTW], start=True, stop=True))
                ev = P.op("vector", lambda e, m=m, tt=tt, bank=bank: e.tensor_tensor(
                    xcol(x, m, tt * TTW, TTW), xcol(x, m, tt * TTW, TTW), pc[:, bank, :],
                    ALU.add), after=ev_mm)
                ps_release(k, "ps_c", bank, ev)
                gevs.append(ev)
        prev_wb = ev_mm
        x_evs = gevs
    k.wbuf_free = prev_wb
    return x_evs


def emit_ln_range(k, src_name, t0, evs, ydst, ybase, sq_off=512, st_off=0):
    P, t = k.P, k.t
    src = t[src_name]
    stats = t["stats"]
    pa, pb = t["ps_a"], t["ps_b"]
    ao = ALLONES(t)
    W = 512
    sq = stats[:, sq_off:sq_off + W]
    stt = stats[:, st_off:st_off + W]
    P.wait("tensor", list(evs) + ps_wait_list(k, "ps_a", [0]) + ps_wait_list(k, "ps_b", [0]))
    last_mm = None
    for c in range(NCF):
        e_sq = P.op("scalar", lambda e, c=c: e.activation(sq, xcol(src, c, t0, W), AF.Square),
                    after=(evs if c == 0 else [last_mm]))
        P.op("tensor", lambda e, c=c: e.matmul(pa[:, 0, :], ao, xcol(src, c, t0, W),
                                               start=(c == 0), stop=(c == NCF - 1)), inc=False)
        P.wait("tensor", e_sq)
        last_mm = P.op("tensor", lambda e, c=c: e.matmul(pb[:, 0, :], ao, sq,
                                                         start=(c == 0), stop=(c == NCF - 1)))
    P.wait("vector", last_mm)
    ev_nm = P.op("vector", lambda e: e.tensor_scalar(stt, pa[:, 0, :], -1.0 / D, None, ALU.mult))
    ps_release(k, "ps_a", 0, ev_nm)
    add_evs = []
    for c in range(NCF):
        ev = P.op("vector", lambda e, c=c: e.tensor_tensor(
            ydst[:, ybase + c * 512: ybase + (c + 1) * 512], xcol(src, c, t0, W), stt, ALU.add))
        add_evs.append(ev)
    P.op("vector", lambda e: e.tensor_tensor(stt, stt, stt, ALU.mult), inc=False)
    ev_var = P.op("vector", lambda e: e.scalar_tensor_tensor(stt, pb[:, 0, :], 1.0 / D, stt, ALU.mult, ALU.subtract))
    ps_release(k, "ps_b", 0, ev_var)
    ev_eps = P.op("vector", lambda e: e.tensor_scalar(stt, stt, 1e-5, None, ALU.add))
    P.wait("scalar", ev_eps)
    P.op("scalar", lambda e: e.activation(stt, stt, AF.Ln), inc=False)
    ev_r = P.op("scalar", lambda e: e.activation(stt, stt, AF.Exp, scale=-0.5))
    out_evs = []
    P.wait("vector", ev_r)
    for c in range(NCF):
        ev = P.op("vector", lambda e, c=c: e.tensor_tensor(
            ydst[:, ybase + c * 512: ybase + (c + 1) * 512],
            ydst[:, ybase + c * 512: ybase + (c + 1) * 512], stt, ALU.mult))
        out_evs.append(ev)
    return out_evs


def _stream_mm(k, wname, col0, nk_chunks, rhs_fn, psum_ap, extra_after=None):
    P, t, hb = k.P, k.t, k.hb
    wb = t["wbuf"]
    nsub = nk_chunks // 8
    ev_mm = None
    for s in range(nsub):
        ev_w = P.dma("sync", out=wb[:, 0:1024].rearrange("p (kk mm) -> p kk mm", kk=8),
                     in_=hb[wname][s * 1024:(s + 1) * 1024, col0:col0 + 128]
                         .rearrange("(kk p) mm -> p kk mm", p=128),
                     after=getattr(k, "wbuf_free", None))
        P.wait("tensor", [ev_w] + (extra_after if (s == 0 and extra_after) else []))
        for kk in range(8):
            kc = s * 8 + kk
            ev_mm = P.op("tensor", lambda e, s=s, kk=kk, kc=kc: e.matmul(
                psum_ap, wb[:, kk * 128:(kk + 1) * 128], rhs_fn(kc),
                start=(kc == 0), stop=(kc == nk_chunks - 1)),
                inc=(kk == 7))
        k.wbuf_free = ev_mm
    return ev_mm


def emit_l3f(k, evs):
    P, t, hb = k.P, k.t, k.hb
    x, y, big, small, stats = t["x"], t["y"], t["big"], t["small"], t["stats"]
    pa, pb, pc = t["ps_a"], t["ps_b"], t["ps_c"]
    ev_s = [P.dma("sync", out=small[:, 100:132], in_=hb["l3f_s1"], after=bfence(k, evs)),
            P.dma("sync", out=small[:, 132:164], in_=hb["l3f_s2"]),
            P.dma("sync", out=small[:, 164:196], in_=hb["l3f_sh1"]),
            P.dma("sync", out=small[:, 196:228], in_=hb["l3f_sh2"]),
            P.dma("sync", out=small[:, 228:236], in_=hb["l3f_s3"])]
    P.wait("tensor", ev_s)
    x_evs = list(evs)
    for q in range(4):
        t0 = q * 512
        ln_evs = emit_ln_range(k, "x", t0, x_evs, y, 0)
        h_evs = []
        for hc in range(32):
            bank_t = hc % 2
            P.wait("tensor", ps_wait_list(k, "ps_c", [bank_t]))
            ev1 = _stream_mm(k, "l3f_w1T", hc * 128, 8,
                             lambda kc: y[:, kc * 512:(kc + 1) * 512],
                             pc[:, bank_t, :], extra_after=ln_evs if hc == 0 else None)
            ev_t1 = P.op("scalar", lambda e, hc=hc, bank_t=bank_t: e.activation(
                stats[:, 1024:1536], pc[:, bank_t, :], AF.Silu, scale=small[:, 100 + hc:101 + hc]),
                after=ev1)
            ps_release(k, "ps_c", bank_t, ev_t1)
            bank2 = 2 + hc % 2
            P.wait("tensor", ps_wait_list(k, "ps_a", [bank2]))
            ev2 = _stream_mm(k, "l3f_w2T", hc * 128, 8,
                             lambda kc: y[:, kc * 512:(kc + 1) * 512],
                             pa[:, bank2, :])
            ev_h = P.op("vector", lambda e, hc=hc, bank2=bank2: e.scalar_tensor_tensor(
                big[:, hc * 512:(hc + 1) * 512], pa[:, bank2, :], small[:, 132 + hc:133 + hc],
                stats[:, 1024:1536], ALU.mult, ALU.mult), after=[ev2, ev_t1])
            ps_release(k, "ps_a", bank2, ev_h)
            h_evs.append(ev_h)
        h1_evs = []
        for oc in range(32):
            bank = oc % 2
            P.wait("tensor", ps_wait_list(k, "ps_a", [bank]))
            ev1 = _stream_mm(k, "l3f_h1T", oc * 128, 32,
                             lambda kc: big[:, kc * 512:(kc + 1) * 512],
                             pa[:, bank, :], extra_after=h_evs if oc == 0 else None)
            ev = P.op("scalar", lambda e, oc=oc, bank=bank: e.activation(
                y[:, oc * 512:(oc + 1) * 512], pa[:, bank, :], AF.Gelu,
                scale=small[:, 164 + oc:165 + oc]), after=ev1)
            ps_release(k, "ps_a", bank, ev)
            h1_evs.append(ev)
        h2_evs = []
        for oc in range(32):
            bank = oc % 2
            P.wait("tensor", ps_wait_list(k, "ps_a", [bank]))
            ev1 = _stream_mm(k, "l3f_h2T", oc * 128, 32,
                             lambda kc: y[:, kc * 512:(kc + 1) * 512],
                             pa[:, bank, :], extra_after=h1_evs if oc == 0 else None)
            ev = P.op("scalar", lambda e, oc=oc, bank=bank: e.activation(
                big[:, oc * 512:(oc + 1) * 512], pa[:, bank, :], AF.Gelu,
                scale=small[:, 196 + oc:197 + oc]), after=ev1)
            ps_release(k, "ps_a", bank, ev)
            h2_evs.append(ev)
        x_evs = []
        for m in range(8):
            bank = m % 2
            P.wait("tensor", ps_wait_list(k, "ps_a", [bank]))
            ev1 = _stream_mm(k, "l3f_w3T", m * 128, 32,
                             lambda kc: big[:, kc * 512:(kc + 1) * 512],
                             pa[:, bank, :], extra_after=h2_evs if m == 0 else None)
            ev = P.op("vector", lambda e, m=m, t0=t0, bank=bank: e.scalar_tensor_tensor(
                xcol(x, m, t0, 512), pa[:, bank, :], small[:, 228 + m:229 + m],
                xcol(x, m, t0, 512), ALU.mult, ALU.add), after=ev1)
            ps_release(k, "ps_a", bank, ev)
            x_evs.append(ev)
    return x_evs


def emit_l0a(k, evs):
    P, t, hb = k.P, k.t, k.hb
    x, y, big, small, stats = t["x"], t["y"], t["big"], t["big"], t["stats"]
    big = t["big"]
    pa, pb, pc = t["ps_a"], t["ps_b"], t["ps_c"]
    wb = t["wbuf"]
    evs = emit_ln(k, "x", evs, dst_name="y")
    QG, KE, VT, KN = 0, 4096, 8192, 12288
    prev_wb = getattr(k, "wbuf_free", None)
    x_evs = list(evs)
    import os as _os
    NG = int(_os.environ.get("NGROUPS", "4"))
    for g in range(NG):
        P.phase("proj")
        proj_evs = []
        for part, (colbase, dstbase, act) in enumerate(
                ((0 + g * 256, QG, "expq"), (1024 + g * 256, KE, "expk"))):
            for m in range(2):
                ev_w = P.dma("sync", out=wb[:, 0:1024].rearrange("p (kk mm) -> p kk mm", kk=8),
                             in_=hb["l0a_wT"][:, colbase + m * 128: colbase + (m + 1) * 128]
                                 .rearrange("(kk p) mm -> p kk mm", p=128),
                             after=prev_wb)
                P.wait("tensor", [ev_w] + (x_evs if g == 0 and part == 0 and m == 0 else []))
                for tt in range(NTT):
                    bank = tt
                    P.wait("tensor", ps_wait_list(k, "ps_a", [bank]))
                    for c in range(NCF):
                        ev_mm = P.op("tensor", lambda e, tt=tt, c=c, bank=bank: e.matmul(
                            pa[:, bank, :], wb[:, c * 128:(c + 1) * 128],
                            xcol(y, c, tt * TTW, TTW), start=(c == 0), stop=(c == NCF - 1)),
                            inc=(c == NCF - 1))
                    dst = big[:, dstbase + m * 2048 + tt * TTW: dstbase + m * 2048 + (tt + 1) * TTW]
                    ev1 = P.op("scalar", lambda e, dst=dst, bank=bank: e.activation(
                        dst, pa[:, bank, :], AF.Exp, scale=0.125), after=ev_mm)
                    if act == "expk":
                        ev1 = P.op("vector", lambda e, dst=dst: e.tensor_scalar(
                            dst, dst, 1e-6, None, ALU.add), after=ev1)
                    ps_release(k, "ps_a", bank, ev1)
                    proj_evs.append(ev1)
                prev_wb = ev_mm
        for vh in range(2):
            ev_w = P.dma("sync", out=wb[:, 0:1024].rearrange("p (kk mm) -> p kk mm", kk=8),
                         in_=hb["l0a_wT"][:, 2048 + g * 256 + vh * 128: 2048 + g * 256 + (vh + 1) * 128]
                             .rearrange("(kk p) mm -> p kk mm", p=128),
                         after=prev_wb)
            P.wait("tensor", [ev_w])
            for tl in range(16):
                bank = tl % 4
                P.wait("tensor", ps_wait_list(k, "ps_a", [bank]))
                for c in range(NCF):
                    ev_mm = P.op("tensor", lambda e, tl=tl, c=c, bank=bank: e.matmul(
                        pa[:, bank, 0:128], xcol(y, c, tl * 128, 128), wb[:, c * 128:(c + 1) * 128],
                        start=(c == 0), stop=(c == NCF - 1)), inc=(c == NCF - 1))
                ev = P.op("vector", lambda e, tl=tl, vh=vh, bank=bank: e.tensor_copy(
                    big[:, VT + tl * 256 + vh * 128: VT + tl * 256 + vh * 128 + 128], pa[:, bank, 0:128]),
                    after=ev_mm)
                ps_release(k, "ps_a", bank, ev)
                proj_evs.append(ev)
            prev_wb = ev_mm
        import os as _os
        _dbg = _os.environ.get("DBGL0A", "")
        if _dbg in ("q", "ke") and g == 0:
            srcb = QG if _dbg == "q" else KE
            P.wait("vector", proj_evs)
            return [P.op("vector", lambda e, c=c, srcb=srcb: e.tensor_copy(
                xcol(x, c), big[:, srcb // 2048 * 2048 + c * 2048 - 0 + (0 if c < 2 else 0):][:, 0:0] if False else big[:, srcb + c * 2048: srcb + (c + 1) * 2048])) for c in range(2)]
        if _dbg == "v" and g == 0:
            P.wait("vector", proj_evs)
            return [P.op("vector", lambda e, tl=tl: e.tensor_copy(
                xcol(x, 0, tl * 128, 128), big[:, VT + tl * 256: VT + tl * 256 + 128])) for tl in range(16)]
        P.phase("kn")
        kn_evs = []
        P.wait("vector", proj_evs)
        for ch in range(2):
            P.op("vector", lambda e, ch=ch: e.tensor_tensor_scan(
                big[:, KN + ch * 2048: KN + (ch + 1) * 2048],
                big[:, KE + ch * 2048: KE + (ch + 1) * 2048],
                big[:, KE + ch * 2048: KE + (ch + 1) * 2048], 0.0, ALU.add, ALU.bypass),
                inc=False, drain=True)
            ev = P.op("vector", lambda e, ch=ch: e.reciprocal(
                big[:, KN + ch * 2048: KN + (ch + 1) * 2048],
                big[:, KN + ch * 2048: KN + (ch + 1) * 2048]), drain=True)
            kn_evs.append(ev)
        P.phase("qs")
        qs_evs = []
        for ch in range(2):
            for tt in range(NTT):
                bank = 2 * (tt % 2)
                P.wait("tensor", ps_wait_list(k, "ps_b", [0, 1]) if False else ps_wait_list(k, "ps_b", [tt % 2]))
                ev_mm = P.op("tensor", lambda e, ch=ch, tt=tt: e.matmul(
                    pb[:, tt % 2, :], BLK64(t),
                    big[:, QG + ch * 2048 + tt * TTW: QG + ch * 2048 + (tt + 1) * TTW],
                    start=True, stop=True), after=(kn_evs if ch == 0 and tt == 0 else None))
                ev_r = P.op("vector", lambda e, tt=tt: e.reciprocal(
                    stats[:, 0:512], pb[:, tt % 2, :]), after=ev_mm, drain=True)
                ps_release(k, "ps_b", tt % 2, ev_r)
                qsl = big[:, QG + ch * 2048 + tt * TTW: QG + ch * 2048 + (tt + 1) * TTW]
                P.op("vector", lambda e, qsl=qsl: e.tensor_tensor(qsl, qsl, stats[:, 0:512], ALU.mult),
                     inc=False, drain=True)
                ev_q = P.op("vector", lambda e, qsl=qsl, ch=ch, tt=tt: e.tensor_tensor(
                    qsl, qsl, big[:, KN + ch * 2048 + tt * TTW: KN + ch * 2048 + (tt + 1) * TTW],
                    ALU.mult), dsync=True)
                qs_evs.append(ev_q)
        if _dbg in ("kn", "qs") and g == 0:
            srcb = KN if _dbg == "kn" else QG
            P.wait("vector", qs_evs)
            return [P.op("vector", lambda e, c=c, srcb=srcb: e.tensor_copy(
                xcol(x, c), big[:, srcb + c * 2048: srcb + (c + 1) * 2048])) for c in range(2)]
        P.phase("scan")
        oe_evs = []
        P.wait("gpsimd", qs_evs)
        last_s_read = {0: None, 64: None}
        for hh in range(4):
            qb = (hh % 2) * 64
            ch = hh // 2
            qoff = QG + ch * 2048
            keoff = KE + ch * 2048
            ooff = KN + ch * 2048
            ev_s0 = P.op("gpsimd", lambda e, qb=qb: e.memset(stats[qb:qb + 64, 0:64], 0.0),
                         after=last_s_read[qb])
            s_add = ev_s0
            for i in range(16):
                bank = i % 2
                P.wait("tensor", ps_wait_list(k, "ps_a", [bank]) + ([ev_s0] if i == 0 else []))
                ev_at = P.op("tensor", lambda e, i=i, qb=qb, qoff=qoff, keoff=keoff, bank=bank: e.matmul(
                    pa[:, bank, 0:128], big[qb:qb + 64, keoff + i * 128: keoff + (i + 1) * 128],
                    big[qb:qb + 64, qoff + i * 128: qoff + (i + 1) * 128], start=True, stop=True))
                ev_am = P.op("vector", lambda e, bank=bank: e.tensor_tensor(
                    stats[:, 64:192], pa[:, bank, 0:128], MASK_GE(t), ALU.mult), after=ev_at, dsync=True)
                ps_release(k, "ps_a", bank, ev_am)
                bank2 = 2 + i % 2
                P.wait("tensor", ps_wait_list(k, "ps_a", [bank2]))
                ev_kt = P.op("tensor", lambda e, i=i, qb=qb, keoff=keoff, bank2=bank2: e.transpose(
                    pa[0:128, bank2, 0:64], big[qb:qb + 64, keoff + i * 128: keoff + (i + 1) * 128],
                    IDENT(t)[qb:qb + 64, qb:qb + 64]))
                ev_kte = P.op("vector", lambda e, bank2=bank2: e.tensor_copy(
                    stats[:, 192:256], pa[0:128, bank2, 0:64]), after=ev_kt, dsync=True)
                ps_release(k, "ps_a", bank2, ev_kte)
                P.wait("tensor", ps_wait_list(k, "ps_b", [0]) + [ev_am, s_add])
                ev_y1 = P.op("tensor", lambda e, i=i, qb=qb, ch=ch: e.matmul(
                    pb[qb:qb + 64, 0, 0:128],
                    big[:, VT + i * 256 + ch * 128 + qb: VT + i * 256 + ch * 128 + qb + 64],
                    stats[:, 64:192], start=True, stop=True))
                P.wait("tensor", ps_wait_list(k, "ps_c", [0]))
                ev_y2 = P.op("tensor", lambda e, i=i, qb=qb, qoff=qoff: e.matmul(
                    pc[qb:qb + 64, 0, 0:128], stats[qb:qb + 64, 0:64],
                    big[qb:qb + 64, qoff + i * 128: qoff + (i + 1) * 128], start=True, stop=True))
                P.op("vector", lambda e, i=i, qb=qb, ooff=ooff: e.tensor_copy(
                    big[qb:qb + 64, ooff + i * 128: ooff + (i + 1) * 128], pb[qb:qb + 64, 0, 0:128]),
                    after=[ev_y1, ev_y2], inc=False)
                ev_oe = P.op("vector", lambda e, i=i, qb=qb, ooff=ooff: e.tensor_tensor(
                    big[qb:qb + 64, ooff + i * 128: ooff + (i + 1) * 128],
                    big[qb:qb + 64, ooff + i * 128: ooff + (i + 1) * 128],
                    pc[qb:qb + 64, 0, 0:128], ALU.add))
                ps_release(k, "ps_b", 0, ev_oe)
                ps_release(k, "ps_c", 0, ev_oe)
                oe_evs.append(ev_oe)
                last_s_read[qb] = ev_y2
                if i < 15:
                    P.wait("tensor", ps_wait_list(k, "ps_b", [1]) + [ev_kte])
                    ev_su = P.op("tensor", lambda e, i=i, qb=qb, ch=ch: e.matmul(
                        pb[qb:qb + 64, 1, 0:64], stats[:, 192:256],
                        big[:, VT + i * 256 + ch * 128 + qb: VT + i * 256 + ch * 128 + qb + 64],
                        start=True, stop=True))
                    s_add = P.op("vector", lambda e, qb=qb: e.tensor_tensor(
                        stats[qb:qb + 64, 0:64], stats[qb:qb + 64, 0:64], pb[qb:qb + 64, 1, 0:64], ALU.add),
                        after=ev_su, dsync=True)
                    ps_release(k, "ps_b", 1, s_add)
        if _dbg == "o" and g == 0:
            P.wait("vector", oe_evs)
            return [P.op("vector", lambda e, c=c: e.tensor_copy(
                xcol(x, c), big[:, KN + c * 2048: KN + (c + 1) * 2048])) for c in range(2)]
        P.phase("outproj")
        for m in range(8):
            ev_w = P.dma("sync", out=wb[:, 0:256].rearrange("p (kk mm) -> p kk mm", kk=2),
                         in_=hb["l0a_wout"][g * 256:(g + 1) * 256, m * 128:(m + 1) * 128]
                             .rearrange("(kk p) mm -> p kk mm", p=128),
                         after=prev_wb)
            P.wait("tensor", [ev_w] + (oe_evs if m == 0 else []))
            gevs = []
            for tt in range(NTT):
                bank = tt % 2
                P.wait("tensor", ps_wait_list(k, "ps_c", [bank]))
                for cc2 in range(2):
                    ev_mm = P.op("tensor", lambda e, m=m, tt=tt, cc2=cc2, bank=bank: e.matmul(
                        pc[:, bank, :], wb[:, cc2 * 128:(cc2 + 1) * 128],
                        big[:, KN + cc2 * 2048 + tt * TTW: KN + cc2 * 2048 + (tt + 1) * TTW],
                        start=(cc2 == 0), stop=(cc2 == 1)), inc=(cc2 == 1))
                ev = P.op("vector", lambda e, m=m, tt=tt, bank=bank: e.tensor_tensor(
                    xcol(x, m, tt * TTW, TTW), xcol(x, m, tt * TTW, TTW), pc[:, bank, :], ALU.add),
                    after=ev_mm)
                ps_release(k, "ps_c", bank, ev)
                gevs.append(ev)
            prev_wb = ev_mm
        x_evs = gevs
    P.phase("")
    k.wbuf_free = prev_wb
    return x_evs


def emit_l1a(k, evs):
    P, t, hb = k.P, k.t, k.hb
    x, y, big, small, stats = t["x"], t["y"], t["big"], t["small"], t["stats"]
    pa, pb, pc = t["ps_a"], t["ps_b"], t["ps_c"]
    evs = emit_ln(k, "x", evs, dst_name="y")

    ld = []
    ld.append(P.dma("sync", out=small[:, 0:32].rearrange("p (c f) -> p c f", c=8), in_=hb["l1a_cwq"],
                    after=bfence(k, evs)))
    ld.append(P.dma("sync", out=small[:, 32:40].rearrange("p (c f) -> p c f", c=2), in_=hb["l1a_cwk"]))
    ld.append(P.dma("sync", out=small[:, 40:48].rearrange("p (c f) -> p c f", c=2), in_=hb["l1a_cwv"]))
    ld.append(P.dma("sync", out=small[:, 48:52], in_=hb["l1a_dt"][0:1, :].squeeze(0).unsqueeze(0).broadcast_to([128, 4])))
    ld.append(P.dma("sync", out=small[:, 52:56], in_=hb["l1a_alog"][0:1, :].squeeze(0).unsqueeze(0).broadcast_to([128, 4])))
    ld.append(P.dma("sync", out=small[:, 56:60], in_=hb["l1a_bbeta"][0:1, :].squeeze(0).unsqueeze(0).broadcast_to([128, 4])))
    ld.append(P.dma("sync", out=small[:, 128:192], in_=hb["l1a_rmsw"][0:1, :].squeeze(0).unsqueeze(0).broadcast_to([128, 64])))
    ld.append(P.dma("sync", out=small[:, 192:224].rearrange("p (c f) -> p c f", c=8),
                    in_=hb["l1a_wbetaT"].rearrange("(c p) f -> p c f", p=128)))
    ld.append(P.dma("sync", out=small[:, 224:256].rearrange("p (c f) -> p c f", c=8),
                    in_=hb["l1a_wgkT"].rearrange("(c p) f -> p c f", p=128)))
    P.wait("vector", ld)
    e1 = P.op("vector", lambda e: e.tensor_scalar(small[:, 60:61], PIDX(t), 63.0, None, ALU.is_equal))
    e2 = P.op("vector", lambda e: e.tensor_scalar(small[:, 61:62], PIDX(t), 127.0, None, ALU.is_equal))
    e3 = P.op("vector", lambda e: e.tensor_tensor(small[:, 60:61], small[:, 60:61], small[:, 61:62], ALU.add))
    e63ev = P.op("vector", lambda e: e.tensor_copy(small[:, 64:128], small[:, 60:61].broadcast_to([128, 64])))
    ld.append(e63ev)

    def conv4(base_col, nch, cw_col, after):
        out_evs2 = []
        for c in range(nch):
            sl = lambda o, n, c=c: big[:, base_col + c * T + o: base_col + c * T + o + n]
            cw = lambda j, c=c: small[:, cw_col + c * 4 + j: cw_col + c * 4 + j + 1]
            P.op("vector", lambda e, sl=sl, cw=cw: e.tensor_scalar(
                stats[:, 0:T], sl(0, T), cw(3), None, ALU.mult), after=after if c == 0 else None)
            for j in range(3):
                sh = 3 - j
                P.op("vector", lambda e, sl=sl, cw=cw, j=j, sh=sh: e.scalar_tensor_tensor(
                    stats[:, sh:T], sl(0, T - sh), cw(j), stats[:, sh:T], ALU.mult, ALU.add))
            ev = P.op("vector", lambda e, sl=sl: e.tensor_copy(sl(0, T), stats[:, 0:T]))
            out_evs2.append(ev)
        return out_evs2

    def l2norm(base_col, nch, eps, after):
        oevs = []
        for c in range(nch):
            e_sq = P.op("scalar", lambda e, c=c: e.activation(
                stats[:, 0:T], big[:, base_col + c * T: base_col + (c + 1) * T], AF.Square),
                after=list(after) + oevs[-4:])
            for tt in range(NTT):
                bank = tt % 2
                P.wait("tensor", ps_wait_list(k, "ps_b", [bank]) + ([e_sq] if True else []))
                ev_mm = P.op("tensor", lambda e, tt=tt, bank=bank: e.matmul(
                    pb[:, bank, :], BLK64(t), stats[:, tt * TTW:(tt + 1) * TTW],
                    start=True, stop=True))
                ev_a = P.op("vector", lambda e, tt=tt, bank=bank, eps=eps: e.tensor_scalar(
                    stats[:, 1024 + 0:1024 + 512] if False else stats[:, tt * TTW:(tt + 1) * TTW],
                    pb[:, bank, :], eps, None, ALU.add), after=ev_mm)
                ps_release(k, "ps_b", bank, ev_a)
                P.wait("scalar", ev_a)
                P.op("scalar", lambda e, tt=tt: e.activation(
                    stats[:, tt * TTW:(tt + 1) * TTW], stats[:, tt * TTW:(tt + 1) * TTW], AF.Ln), inc=False)
                ev_l = P.op("scalar", lambda e, tt=tt: e.activation(
                    stats[:, tt * TTW:(tt + 1) * TTW], stats[:, tt * TTW:(tt + 1) * TTW], AF.Exp, scale=-0.5))
                ev = P.op("vector", lambda e, c=c, tt=tt: e.tensor_tensor(
                    big[:, base_col + c * T + tt * TTW: base_col + c * T + (tt + 1) * TTW],
                    big[:, base_col + c * T + tt * TTW: base_col + c * T + (tt + 1) * TTW],
                    stats[:, tt * TTW:(tt + 1) * TTW], ALU.mult), after=ev_l)
                oevs.append(ev)
        return oevs

    stg = [stats[:, 0:512], stats[:, 512:1024]]
    stg_dma = [None, None]
    def gatew(P_, ps_ap, m, tt, ev):
        slot = (m * NTT + tt) % 2
        e1_ = P_.op("scalar", lambda e, slot=slot: e.activation(stg[slot], ps_ap, AF.Silu),
                    after=[ev] + ([stg_dma[slot]] if stg_dma[slot] is not None else []))
        d = P_.dma("gpsimd", out=k.spill[m * 128:(m + 1) * 128, tt * TTW:(tt + 1) * TTW],
                   in_=stg[slot], after=e1_)
        stg_dma[slot] = d
        return d
    gevs = proj_fm(k, "l1a_wgateT", y, None, T, list(evs) + ld, gatew)
    def qw(P_, ps_ap, m, tt, ev):
        return P_.op("vector", lambda e: e.tensor_copy(
            big[:, m * T + tt * TTW: m * T + tt * TTW + TTW], ps_ap), after=ev)
    qevs = proj_fm(k, "l1a_wqT", y, None, T, gevs, qw)
    qevs = conv4(0, 8, 0, qevs)
    qevs = l2norm(0, 8, 1e-12, qevs)
    q_sp = []
    for c in range(NCF):
        q_sp.append(P.dma("gpsimd", out=k.spill2[c * 128:(c + 1) * 128, :], in_=xcol(big, c),
                          after=qevs))
    import os as _os
    _dbg = _os.environ.get("DBGL1A", "")
    if _dbg == "q":
        P.wait("vector", q_sp)
        return [P.op("vector", lambda e, c=c: e.tensor_copy(xcol(x, c), xcol(big, c))) for c in range(8)]
    KF, VF = 1024, 5120
    QCH = 0
    def kw(P_, ps_ap, m, tt, ev):
        return P_.op("vector", lambda e: e.tensor_copy(
            big[:, KF + m * T + tt * TTW: KF + m * T + tt * TTW + TTW], ps_ap), after=ev)
    kevs = proj_fm(k, "l1a_wkT", y, None, T, q_sp, kw, n_out=256)
    kevs = conv4(KF, 2, 32, kevs)
    kevs = l2norm(KF, 2, 1e-12, kevs)
    def vw(P_, ps_ap, m, tt, ev):
        return P_.op("vector", lambda e: e.tensor_copy(
            big[:, VF + m * T + tt * TTW: VF + m * T + tt * TTW + TTW], ps_ap), after=ev)
    vevs = proj_fm(k, "l1a_wvT", y, None, T, kevs, vw, n_out=256)
    vevs = conv4(VF, 2, 40, vevs)
    if _dbg in ("k", "v"):
        bcol0 = KF if _dbg == "k" else VF
        P.wait("vector", vevs)
        return [P.op("vector", lambda e, c=c, bcol0=bcol0: e.tensor_copy(
            xcol(x, c), big[:, bcol0 + c * T: bcol0 + (c + 1) * T])) for c in range(2)]
    BG = 10240
    BB = 10624
    SREG = 11008
    SCR = 11264
    bz_evs = []
    for i in range(32):
        for base in (0, 64):
            for (wcol, dst) in ((192, BB), (224, BG)):
                bank = (i + base // 64) % 2
                P.wait("tensor", ps_wait_list(k, "ps_c", [bank]) + (vevs if i == 0 else []))
                for c in range(NCF):
                    ev_mm = P.op("tensor", lambda e, i=i, c=c, base=base, wcol=wcol, bank=bank: e.matmul(
                        pc[base:base + 64, bank, 0:4], xcol(y, c, i * 64, 64),
                        small[:, wcol + c * 4: wcol + (c + 1) * 4],
                        start=(c == 0), stop=(c == NCF - 1)), inc=(c == NCF - 1))
                ev = P.op("vector", lambda e, i=i, base=base, dst=dst, bank=bank: e.tensor_copy(
                    big[base:base + 64, dst + i * 4: dst + (i + 1) * 4], pc[base:base + 64, bank, 0:4]),
                    after=ev_mm)
                ps_release(k, "ps_c", bank, ev)
                bz_evs.append(ev)
    P.wait("vector", bz_evs)
    ev_bb = P.op("vector", lambda e: e.tensor_tensor(
        big[:, BB:BB + 128].rearrange("p (i f) -> p i f", i=32), 
        big[:, BB:BB + 128].rearrange("p (i f) -> p i f", i=32),
        small[:, 56:60].unsqueeze(1).broadcast_to([128, 32, 4]), ALU.add))
    P.wait("scalar", ev_bb)
    ev_bb = P.op("scalar", lambda e: e.activation(big[:, BB:BB + 128], big[:, BB:BB + 128], AF.Sigmoid))
    ev_nb = P.op("vector", lambda e: e.tensor_scalar(big[:, BB + 256:BB + 384], big[:, BB:BB + 128], -1.0, None, ALU.mult), after=ev_bb)
    ev_z = P.op("vector", lambda e: e.tensor_tensor(
        big[:, BG:BG + 128].rearrange("p (i f) -> p i f", i=32),
        big[:, BG:BG + 128].rearrange("p (i f) -> p i f", i=32),
        small[:, 48:52].unsqueeze(1).broadcast_to([128, 32, 4]), ALU.add))
    P.wait("scalar", ev_z)
    P.op("scalar", lambda e: e.activation(big[:, BG:BG + 128], big[:, BG:BG + 128], AF.Exp), inc=False)
    ev_e = P.op("vector", lambda e: e.tensor_scalar(big[:, BG:BG + 128], big[:, BG:BG + 128], 1.0, None, ALU.add),
                after=P.fence("scalar"))
    P.wait("scalar", ev_e)
    ev_sp = P.op("scalar", lambda e: e.activation(big[:, BG:BG + 128], big[:, BG:BG + 128], AF.Ln))
    P.op("scalar", lambda e: e.activation(small[:, 52:56], small[:, 52:56], AF.Exp), inc=False)
    ev_an = P.op("scalar", lambda e: e.activation(small[:, 52:56], small[:, 52:56], AF.Sign) if False else
                 e.activation(small[:, 52:56], small[:, 52:56], AF.Abs))
    ev_an = P.op("vector", lambda e: e.tensor_scalar(small[:, 52:56], small[:, 52:56], -1.0, None, ALU.mult),
                 after=ev_an)
    ev_gk = P.op("vector", lambda e: e.tensor_tensor(
        big[:, BG:BG + 128].rearrange("p (i f) -> p i f", i=32),
        big[:, BG:BG + 128].rearrange("p (i f) -> p i f", i=32),
        small[:, 52:56].unsqueeze(1).broadcast_to([128, 32, 4]), ALU.mult), after=[ev_sp, ev_an])
    bs_evs = []
    for i in range(32):
        for base in (0, 64):
            bank = (i + base // 64) % 2
            P.wait("tensor", ps_wait_list(k, "ps_c", [bank]) + ([ev_gk] if i == 0 else []))
            ev_mm = P.op("tensor", lambda e, i=i, base=base, bank=bank: e.matmul(
                pc[base:base + 64, bank, 0:4], MASK_GE(t)[base:base + 64, base:base + 64],
                big[base:base + 64, BG + i * 4: BG + (i + 1) * 4], start=True, stop=True))
            ev = P.op("vector", lambda e, i=i, base=base, bank=bank: e.tensor_copy(
                big[base:base + 64, BG + 128 + i * 4: BG + 128 + (i + 1) * 4],
                pc[base:base + 64, bank, 0:4]), after=ev_mm)
            ps_release(k, "ps_c", bank, ev)
            bs_evs.append(ev)
    P.wait("scalar", bs_evs)
    ev_g = P.op("scalar", lambda e: e.activation(stats[:, 0:128], big[:, BG + 128:BG + 256], AF.Exp))
    ev_rg = P.op("scalar", lambda e: e.activation(stats[:, 128:256], big[:, BG + 128:BG + 256], AF.Exp, scale=-1.0))
    ev_gc = P.op("vector", lambda e: e.tensor_copy(big[:, BG + 256:BG + 384], stats[:, 0:128]), after=ev_g)
    ev_rgc = P.op("vector", lambda e: e.tensor_copy(big[:, 11136:11264], stats[:, 128:256]), after=ev_rg)
    ev_bg = P.op("vector", lambda e: e.tensor_tensor(big[:, BB + 128:BB + 256], big[:, BB:BB + 128],
                                                     big[:, BG + 256:BG + 384], ALU.mult), after=[ev_bb, ev_gc])
    GAM, RGAM, BGAM, NBET, BSUM = BG + 256, 11136, BB + 128, BB + 256, BG + 128
    pre_evs = [ev_nb, ev_bg, ev_rgc] + list(vevs) + list(kevs)

    if _dbg == "bg":
        P.wait("vector", pre_evs)
        evd = [P.op("vector", lambda e: e.tensor_copy(xcol(x, 0, 0, 128), big[:, BB:BB + 128])),
               P.op("vector", lambda e: e.tensor_copy(xcol(x, 0, 128, 128), big[:, GAM:GAM + 128])),
               P.op("vector", lambda e: e.tensor_copy(xcol(x, 0, 256, 128), big[:, BSUM:BSUM + 128]))]
        return evd
    oe_evs = []
    sc = lambda j: SCR + j * 64
    SCR2 = 12288
    ev_s0 = []
    P.wait("gpsimd", pre_evs)
    for n in range(4):
        base = (n % 2) * 64
        scol = SREG + (n // 2) * 64
        ev_s0.append(P.op("gpsimd", lambda e, base=base, scol=scol: e.memset(
            big[base:base + 64, scol:scol + 64], 0.0)))
    qslot_ev = [None, None]
    last_q_use = [None, None]
    for i in range(32):
        slot = i % 2
        ev_q = P.dma("gpsimd", out=big[:, QCH + slot * 512: QCH + slot * 512 + 512]
                     .rearrange("p (c w) -> p c w", c=8),
                     in_=k.spill2.rearrange("(c p) t -> p c t", p=128)[:, :, i * 64:(i + 1) * 64],
                     after=last_q_use[slot] if last_q_use[slot] is not None else (q_sp if i < 2 else None))
        qslot_ev[slot] = ev_q
        opair_evs = []
        for n in range(4):
            base = (n % 2) * 64
            scol = SREG + (n // 2) * 64
            S0 = big[base:base + 64, scol:scol + 64]
            kcol = KF + (n // 2) * T + i * 64
            vcol = VF + (n // 2) * T + i * 64
            Kf = big[base:base + 64, kcol:kcol + 64]
            Kpair = big[:, kcol:kcol + 64]
            Vpair = big[:, vcol:vcol + 64]
            I64 = IDENT(t)[0:64, 0:64]
            GE64 = MASK_GE(t)[0:64, 0:64]
            GT64 = MASK_GT(t)[0:64, 0:64]
            _c = {buf: big[0:64, buf + i * 4 + n: buf + i * 4 + n + 1]
                  for buf in (GAM, RGAM, BGAM, NBET, BSUM)}
            _cq = {buf: big[base:base + 64, buf + i * 4 + n: buf + i * 4 + n + 1]
                   for buf in (GAM, RGAM, BGAM, NBET, BSUM)}
            _T = [big[0:64, sc(j):sc(j) + 64] for j in range(16)]
            VT_P = big[0:64, sc(0):sc(0) + 128]
            KT_P = big[0:64, sc(2):sc(2) + 128]
            KHF_P = big[:, sc(4):sc(4) + 64]
            KTF_P = big[:, sc(5):sc(5) + 64]
            _rms = big[0:64, sc(14) + 63:sc(14) + 64]
            _gbar = big[base:base + 64, sc(15) + 62 - n: sc(15) + 63 - n] if False else big[base:base + 64, sc(14) + 62:sc(14) + 63]
            _sfac = big[base:base + 64, sc(14) + 61:sc(14) + 62]
            if n % 2 == 0:
                P.wait("tensor", ps_wait_list(k, "ps_b", [0, 1]) + (ev_s0 if i == 0 and n == 0 else []))
                evt1 = P.op("tensor", lambda e, Vpair=Vpair: e.transpose(pb[0:64, 0, 0:128], Vpair, IDENT(t)))
                evt2 = P.op("tensor", lambda e, Kpair=Kpair: e.transpose(pb[0:64, 1, 0:128], Kpair, IDENT(t)))
                ev_vt = P.op("vector", lambda e, VT_P=VT_P: e.tensor_copy(VT_P, pb[0:64, 0, 0:128]), after=evt1)
                ev_kt = P.op("vector", lambda e, KT_P=KT_P: e.tensor_copy(KT_P, pb[0:64, 1, 0:128]), after=evt2)
                ps_release(k, "ps_b", 0, ev_vt)
                ps_release(k, "ps_b", 1, ev_kt)
                k._l1a_pair_ev = (ev_vt, ev_kt)
            ev_vt, ev_kt = k._l1a_pair_ev
            nh = n % 2
            VT_n = big[0:64, sc(0) + nh * 64: sc(0) + nh * 64 + 64]
            KT_n = big[0:64, sc(2) + nh * 64: sc(2) + nh * 64 + 64]
            P.op("vector", lambda e, _T=_T, KT_n=KT_n, _c=_c: e.tensor_scalar(
                _T[6], KT_n, _c[RGAM], None, ALU.mult), after=ev_kt, inc=False)
            ev_sc2 = P.op("vector", lambda e, _T=_T, KT_n=KT_n, _c=_c: e.tensor_scalar(
                _T[7], KT_n, _c[BGAM], None, ALU.mult))
            P.wait("tensor", ps_wait_list(k, "ps_b", [0, 1]) + [ev_sc2])
            evt3 = P.op("tensor", lambda e, _T=_T: e.transpose(pb[0:64, 0, 0:64], _T[6], I64))
            evt4 = P.op("tensor", lambda e, _T=_T: e.transpose(pb[0:64, 1, 0:64], _T[7], I64))
            if base == 0:
                ev_c4 = P.op("vector", lambda e, KHF_P=KHF_P: e.tensor_copy(KHF_P[0:64, :], pb[0:64, 0, 0:64]), after=evt3)
                ev_c5 = P.op("vector", lambda e, KTF_P=KTF_P: e.tensor_copy(KTF_P[0:64, :], pb[0:64, 1, 0:64]), after=evt4)
                ps_release(k, "ps_b", 0, ev_c4)
                ps_release(k, "ps_b", 1, ev_c5)
            else:
                P.op("vector", lambda e, _T=_T: e.tensor_copy(_T[8], pb[0:64, 0, 0:64]), after=evt3, inc=False)
                ev_tmp = P.op("vector", lambda e, _T=_T: e.tensor_copy(_T[9], pb[0:64, 1, 0:64]), after=evt4)
                ps_release(k, "ps_b", 0, ev_tmp)
                ps_release(k, "ps_b", 1, ev_tmp)
                P.wait("tensor", ps_wait_list(k, "ps_b", [0, 1]) + [ev_tmp])
                evr1 = P.op("tensor", lambda e, _T=_T: e.matmul(pb[64:128, 0, 0:64], I64, _T[8], start=True, stop=True))
                evr2 = P.op("tensor", lambda e, _T=_T: e.matmul(pb[64:128, 1, 0:64], I64, _T[9], start=True, stop=True))
                ev_c4 = P.op("vector", lambda e, KHF_P=KHF_P: e.tensor_copy(KHF_P[64:128, :], pb[64:128, 0, 0:64]), after=evr1)
                ev_c5 = P.op("vector", lambda e, KTF_P=KTF_P: e.tensor_copy(KTF_P[64:128, :], pb[64:128, 1, 0:64]), after=evr2)
                ps_release(k, "ps_b", 0, ev_c4)
                ps_release(k, "ps_b", 1, ev_c5)
            KHF_n = KHF_P[base:base + 64, :]
            KTF_n = KTF_P[base:base + 64, :]
            P.wait("tensor", ps_wait_list(k, "ps_b", [0]))
            ev_ks = P.op("tensor", lambda e, Kf=Kf, S0=S0: e.matmul(
                pb[0:64, 0, 0:64], Kf, S0, start=True, stop=True))
            P.op("vector", lambda e, _T=_T, _c=_c, VT_n=VT_n: e.scalar_tensor_tensor(
                _T[10], pb[0:64, 0, 0:64], _c[GAM], VT_n, ALU.mult, ALU.subtract), after=[ev_ks, ev_vt], inc=False)
            ev_r = P.op("vector", lambda e, _T=_T, _c=_c: e.tensor_scalar(_T[10], _T[10], _c[NBET], None, ALU.mult))
            ps_release(k, "ps_b", 0, ev_r)
            P.wait("tensor", ps_wait_list(k, "ps_a", [0, 1]) + [ev_c4, ev_c5])
            ev_a1 = P.op("tensor", lambda e, KTF_n=KTF_n, KHF_n=KHF_n: e.matmul(pa[0:64, 0, 0:64], KTF_n, KHF_n, start=True, stop=True))
            ev_a2 = P.op("tensor", lambda e, KTF_n=KTF_n, KHF_n=KHF_n: e.matmul(pa[0:64, 1, 0:64], KHF_n, KTF_n, start=True, stop=True))
            P.op("vector", lambda e, _T=_T: e.tensor_tensor(_T[11], pa[0:64, 0, 0:64], GE64, ALU.mult), after=ev_a1, inc=False)
            ev_m1 = P.op("vector", lambda e, _T=_T: e.tensor_tensor(_T[11], pa[0:64, 0, 0:64], _T[11], ALU.subtract))
            ev_m2 = P.op("vector", lambda e, _T=_T: e.tensor_tensor(_T[12], pa[0:64, 1, 0:64], GT64, ALU.mult), after=ev_a2)
            ps_release(k, "ps_a", 0, ev_m1)
            ps_release(k, "ps_a", 1, ev_m2)
            P.op("vector", lambda e, _T=_T: e.tensor_tensor(_T[13], I64, _T[11], ALU.subtract), after=[ev_m1, ev_m2], inc=False)
            P.op("vector", lambda e, _T=_T: e.tensor_tensor(_T[14], I64, _T[12], ALU.subtract), inc=False)
            P.op("vector", lambda e, _T=_T: e.tensor_copy(_T[8], _T[11]), inc=False)
            ev_x0 = P.op("vector", lambda e, _T=_T: e.tensor_copy(_T[9], _T[12]))
            lev_ev = ev_x0
            for lev in range(5):
                P.wait("tensor", ps_wait_list(k, "ps_a", [0, 1, 2, 3]) + [lev_ev])
                ev_s1 = P.op("tensor", lambda e, _T=_T: e.matmul(pa[0:64, 0, 0:64], _T[9], _T[8], start=True, stop=True))
                ev_s2 = P.op("tensor", lambda e, _T=_T: e.matmul(pa[0:64, 1, 0:64], _T[8], _T[9], start=True, stop=True))
                ev_xe = P.op("vector", lambda e, _T=_T: e.tensor_copy(_T[8], pa[0:64, 0, 0:64]), after=[ev_s1, ev_s2])
                ev_xe2 = P.op("vector", lambda e, _T=_T: e.tensor_copy(_T[9], pa[0:64, 1, 0:64]), after=ev_s2)
                ps_release(k, "ps_a", 0, ev_xe)
                ps_release(k, "ps_a", 1, ev_xe2)
                P.wait("tensor", [ev_xe, ev_xe2])
                ev_g1 = P.op("tensor", lambda e, _T=_T: e.matmul(pa[0:64, 2, 0:64], _T[14], _T[8], start=True, stop=True))
                ev_g2 = P.op("tensor", lambda e, _T=_T: e.matmul(pa[0:64, 3, 0:64], _T[8], _T[14], start=True, stop=True))
                ev_tu = P.op("vector", lambda e, _T=_T: e.tensor_tensor(_T[13], _T[13], pa[0:64, 2, 0:64], ALU.add), after=ev_g1)
                ev_tu2 = P.op("vector", lambda e, _T=_T: e.tensor_tensor(_T[14], _T[14], pa[0:64, 3, 0:64], ALU.add), after=ev_g2)
                ps_release(k, "ps_a", 2, ev_tu)
                ps_release(k, "ps_a", 3, ev_tu2)
                lev_ev = ev_tu2
            P.wait("tensor", ps_wait_list(k, "ps_b", [0]) + [ev_r, lev_ev])
            ev_umm = P.op("tensor", lambda e, _T=_T: e.matmul(pb[0:64, 0, 0:64], _T[14], _T[10], start=True, stop=True))
            ev_u = P.op("vector", lambda e, _T=_T: e.tensor_copy(_T[8], pb[0:64, 0, 0:64]), after=ev_umm)
            ps_release(k, "ps_b", 0, ev_u)
            if _dbg == "c0" and i == 0 and n == 0:
                P.wait("vector", [ev_u])
                tap = []
                for (name, ap, dst) in (("A", _T[11], 0), ("At", _T[12], 64), ("T", _T[13], 128),
                                         ("Tt", _T[14], 192), ("r", _T[10], 256), ("u", _T[8], 320),
                                         ("Khf", KHF_P[0:64, :], 384), ("Ktf", KTF_P[0:64, :], 448),
                                         ("Kt", KT_n, 512), ("Vt", VT_n, 576)):
                    tap.append(P.op("vector", lambda e, ap=ap, dst=dst: e.tensor_copy(
                        x[0:64, dst:dst + 64], ap)))
                return tap
            o2_list = []
            for jq in range(4):
                h = 4 * n + jq
                c_h = (h % 4) + 4 * (n // 2)
                Qf = big[base:base + 64, QCH + slot * 512 + c_h * 64: QCH + slot * 512 + c_h * 64 + 64]
                KHF_q = KHF_P[base:base + 64, :]
                P.wait("tensor", ps_wait_list(k, "ps_a", [0, 1]) + [qslot_ev[slot], ev_u])
                ev_b1 = P.op("tensor", lambda e, Qf=Qf, KHF_q=KHF_q: e.matmul(
                    pa[0:64, 0, 0:64], Qf, KHF_q, start=True, stop=True))
                P.op("vector", lambda e, _T=_T, _c=_c: e.tensor_scalar(_T[11], pa[0:64, 0, 0:64], _c[GAM], None, ALU.mult), after=ev_b1, inc=False)
                P.op("vector", lambda e, _T=_T: e.tensor_tensor(_T[12], _T[11], GT64, ALU.mult), inc=False)
                ev_bm = P.op("vector", lambda e, _T=_T: e.tensor_tensor(_T[11], _T[11], _T[12], ALU.subtract))
                ps_release(k, "ps_a", 0, ev_bm)
                P.wait("tensor", ps_wait_list(k, "ps_a", [1]) + [ev_bm])
                ev_bt = P.op("tensor", lambda e, _T=_T: e.transpose(pa[0:64, 1, 0:64], _T[11], I64))
                ev_bte = P.op("vector", lambda e, _T=_T: e.tensor_copy(_T[12], pa[0:64, 1, 0:64]), after=ev_bt)
                ps_release(k, "ps_a", 1, ev_bte)
                P.wait("tensor", ps_wait_list(k, "ps_b", [0, 1]) + [ev_bte])
                ev_o1 = P.op("tensor", lambda e, _T=_T: e.matmul(pb[0:64, 0, 0:64], _T[12], _T[8], start=True, stop=True))
                ev_o2 = P.op("tensor", lambda e, Qf=Qf, S0=S0: e.matmul(pb[0:64, 1, 0:64], Qf, S0, start=True, stop=True))
                o2_list.append(ev_o2)
                P.op("vector", lambda e, _T=_T, _c=_c: e.tensor_scalar(_T[15], pb[0:64, 1, 0:64], _c[GAM], 0.125, ALU.mult, ALU.mult), after=ev_o2, inc=False)
                ev_oo = P.op("vector", lambda e, _T=_T: e.scalar_tensor_tensor(_T[15], pb[0:64, 0, 0:64], 0.125, _T[15], ALU.mult, ALU.add), after=ev_o1)
                ps_release(k, "ps_b", 0, ev_oo)
                ps_release(k, "ps_b", 1, ev_oo)
                if _dbg == "h0" and i == 0 and n == 0 and jq == 0:
                    P.op("vector", lambda e, _T=_T: e.tensor_copy(x[0:64, 256:320], _T[15]), after=ev_oo, inc=False)
                ev_sq = P.op("scalar", lambda e, _T=_T: e.activation(_T[14] if False else _T[11], _T[15], AF.Square), after=ev_oo)
                P.wait("vector", ev_sq)
                P.op("vector", lambda e, _T=_T, _rms=_rms: e.tensor_reduce(_rms, _T[11], axis=AX.X, op=ALU.add), inc=False)
                ev_rd = P.op("vector", lambda e, _rms=_rms: e.tensor_scalar(_rms, _rms, 1.0 / 64, 1e-5, ALU.mult, ALU.add))
                P.wait("scalar", ev_rd)
                P.op("scalar", lambda e, _rms=_rms: e.activation(_rms, _rms, AF.Ln), inc=False)
                ev_rs = P.op("scalar", lambda e, _rms=_rms: e.activation(_rms, _rms, AF.Exp, scale=-0.5))
                P.wait("vector", ev_rs)
                P.op("vector", lambda e, _T=_T, _rms=_rms: e.tensor_scalar(_T[15], _T[15], _rms, None, ALU.mult), inc=False)
                ev_on = P.op("vector", lambda e, _T=_T, c_h=c_h, nh2=(n % 2): e.tensor_tensor(
                    big[0:64, SCR2 + c_h * 128 + nh2 * 64: SCR2 + c_h * 128 + nh2 * 64 + 64],
                    _T[15], small[0:64, 128:192], ALU.mult))
                oe_evs.append(ev_on)
                opair_evs.append(ev_on)
                if _dbg == "h0" and i == 0 and n == 0 and jq == 0:
                    P.wait("vector", [ev_on])
                    tap = []
                    for (ap, dst) in ((_T[15], 0), (_T[12], 64), (big[0:64, SCR2 + c_h * 128: SCR2 + c_h * 128 + 64], 128), (Qf, 192)):
                        tap.append(P.op("vector", lambda e, ap=ap, dst=dst: e.tensor_copy(
                            x[0:64, dst:dst + 64], ap)))
                    return tap
                if jq == 3:
                    last_q_use[slot] = ev_o2
            P.wait("tensor", ps_wait_list(k, "ps_b", [1]))
            ev_e63 = P.op("tensor", lambda e, base=base, _cq=_cq: e.matmul(
                pb[base:base + 64, 1, 0:1], small[base:base + 64, 64:128], _cq[BSUM], start=True, stop=True))
            P.wait("scalar", ev_e63)
            ev_gb = P.op("scalar", lambda e, base=base, _gbar=_gbar: e.activation(
                _gbar, pb[base:base + 64, 1, 0:1], AF.Exp))
            ev_sf0 = P.op("vector", lambda e, base=base, _sfac=_sfac, _cq=_cq: e.tensor_tensor(
                _sfac, pb[base:base + 64, 1, 0:1], _cq[BSUM], ALU.subtract), after=ev_e63)
            ps_release(k, "ps_b", 1, [ev_gb, ev_sf0])
            P.wait("scalar", ev_sf0)
            ev_sf = P.op("scalar", lambda e, _sfac=_sfac: e.activation(_sfac, _sfac, AF.Exp))
            ev_sg = P.op("vector", lambda e, S0=S0, _gbar=_gbar: e.tensor_scalar(
                S0, S0, _gbar, None, ALU.mult), after=[ev_gb] + o2_list)
            ev_e63b = P.op("tensor", lambda e, _c=_c: e.matmul(
                pc[0:64, 0, 0:1], small[0:64, 64:128], _c[BSUM], start=True, stop=True),
                after=ps_wait_list(k, "ps_c", [0]))
            ev_sf0b = P.op("vector", lambda e, _T=_T, _c=_c: e.tensor_tensor(
                big[0:64, sc(14) + 60:sc(14) + 61], pc[0:64, 0, 0:1], _c[BSUM], ALU.subtract), after=ev_e63b)
            ps_release(k, "ps_c", 0, ev_sf0b)
            P.wait("scalar", ev_sf0b)
            ev_sfb = P.op("scalar", lambda e: e.activation(
                big[0:64, sc(14) + 60:sc(14) + 61], big[0:64, sc(14) + 60:sc(14) + 61], AF.Exp))
            ev_kb = P.op("vector", lambda e, _T=_T, KT_n=KT_n: e.tensor_scalar(
                _T[9], KT_n, big[0:64, sc(14) + 60:sc(14) + 61], None, ALU.mult), after=ev_sfb)
            P.wait("tensor", ps_wait_list(k, "ps_b", [1]) + [ev_kb, ev_u])
            ev_ds = P.op("tensor", lambda e, _T=_T, base=base: e.matmul(
                pb[base:base + 64, 1, 0:64], _T[9], _T[8], start=True, stop=True))
            ev_sa = P.op("vector", lambda e, S0=S0, base=base: e.tensor_tensor(
                S0, S0, pb[base:base + 64, 1, 0:64], ALU.add), after=[ev_ds, ev_sg])
            ps_release(k, "ps_b", 1, ev_sa)
            if _dbg == "s0" and i == 1 and n == 0:
                P.wait("vector", [ev_sa])
                return [P.op("vector", lambda e, S0=S0: e.tensor_copy(x[0:64, 0:64], S0))]
        for c in range(8):
            P.wait("tensor", ps_wait_list(k, "ps_c", [1]) + opair_evs)
            ev_pt = P.op("tensor", lambda e, c=c: e.transpose(
                pc[:, 1, 0:64], big[0:64, SCR2 + c * 128: SCR2 + (c + 1) * 128], IDENT(t)[0:64, 0:64]))
            ev_pe = P.op("vector", lambda e, c=c, i=i: e.tensor_copy(
                y[:, c * T + i * 64: c * T + (i + 1) * 64], pc[:, 1, 0:64]), after=ev_pt)
            ps_release(k, "ps_c", 1, ev_pe)
            oe_evs.append(ev_pe)
    if _dbg == "o":
        P.wait("vector", oe_evs)
        return [P.op("vector", lambda e, c=c: e.tensor_copy(xcol(x, c), xcol(y, c))) for c in range(8)]
    P.wait("vector", oe_evs)
    og_evs = []
    for c in range(NCF):
        ev_gd = P.dma("sync", out=stats[:, 0:T], in_=k.spill[c * 128:(c + 1) * 128, :],
                      after=(og_evs[-1:] if og_evs else list(oe_evs)))
        ev = P.op("vector", lambda e, c=c: e.tensor_tensor(xcol(y, c), xcol(y, c), stats[:, 0:T], ALU.mult),
                  after=ev_gd)
        og_evs.append(ev)
    def xw(P_, ps_ap, m, tt, ev):
        return P_.op("vector", lambda e, m=m, tt=tt: e.tensor_tensor(
            xcol(x, m, tt * TTW, TTW), xcol(x, m, tt * TTW, TTW), ps_ap, ALU.add), after=ev)
    x_evs = proj_fm(k, "l1a_wprojT", y, None, T, og_evs, xw)
    return x_evs


def emit_l2f(k, evs):
    P, t, hb = k.P, k.t, k.hb
    x, y, big, small, stats = t["x"], t["y"], t["big"], t["small"], t["stats"]
    u32 = t["u32a"]
    pa, pb, pc = t["ps_a"], t["ps_b"], t["ps_c"]
    P.phase("l2f")
    ROWL, IDN = 2176, 8 * 2176 + 8
    ld = [P.dma("sync", out=small[:, 32:96].rearrange("p (c e) -> p c e", c=8),
                in_=hb["l2f_gateT"].rearrange("(c p) e -> p c e", p=128),
                after=bfence(k, evs)),
          P.dma("sync", out=small[:, 96:104], in_=hb["l2f_lng"]),
          P.dma("sync", out=small[:, 104:112], in_=hb["l2f_lnb"]),
          P.dma("sync", out=small[:, 112:136], in_=hb["l2f_pc"])]
    ev_ib = P.op("vector", lambda e: e.tensor_scalar(small[:, 136:144], IOTA8(t), 65536.0, None, ALU.add),
                 after=evs)
    ev_ig = P.op("gpsimd", lambda e: e.iota(small[:, 144:160], pattern=[[128, 16]], base=0,
                                            channel_multiplier=1, allow_small_or_imprecise_dtypes=True),
                 after=bfence(k, evs))
    ln1 = emit_ln(k, "x", evs, dst_name="y", sq_scratch=xcol(big, 0))
    ln2 = emit_ln(k, "y", ln1, dst_name="big", sq_scratch=stats[:, 0:T])
    gb_evs = []
    for c in range(NCF):
        ev = P.op("vector", lambda e, c=c: e.scalar_tensor_tensor(
            xcol(big, c), xcol(big, c), small[:, 96 + c:97 + c],
            small[:, 104 + c:105 + c].broadcast_to([128, T]), ALU.mult, ALU.add),
            after=([ln2[c]] + ([ld[1], ld[2]] if c == 0 else [])))
        gb_evs.append(ev)
    xadd_evs = []
    for c in range(NCF):
        xadd_evs.append(P.op("vector", lambda e, c=c: e.tensor_tensor(
            xcol(x, c), xcol(x, c), xcol(y, c), ALU.add)))
    LG, NEGM, SUME, MAXL, T8 = 8192, 8200, 8201, 8202, 8204
    GLR, GT1, GW0, GRK, GKP, GSL, GIS, GRS, GTMP, GIDS = (8300, 8316, 8332, 8348,
                                                          8364, 8380, 8396, 8412, 8428, 8444)
    OFFB, CNTB, OFFMY, TOTG = 8460, 8468, 8476, 8484
    stage = [y[:, 0:1024], y[:, 1024:2048]]
    st_dma = [None, None]
    ag_t_dep = []
    oh_evs = []
    e_lg_prev, ohe_prev = None, None
    pbev = [None, None]
    P.wait("tensor", ld + [ev_ig, ev_ib] + gb_evs
           + ps_wait_list(k, "ps_a", range(4)) + ps_wait_list(k, "ps_b", [0, 1])
           + ps_wait_list(k, "ps_c", [0, 1]))
    for q in range(16):
        P.wait("tensor", e_lg_prev)
        for c in range(NCF):
            ev_mm = P.op("tensor", lambda e, q=q, c=c: e.matmul(
                pa[:, 0, 0:8], big[:, c * T + q * 128: c * T + q * 128 + 128],
                small[:, 32 + c * 8: 40 + c * 8],
                start=(c == 0), stop=(c == NCF - 1)), inc=(c == NCF - 1))
        e_lg = P.op("vector", lambda e: e.tensor_copy(y[:, LG:LG + 8], pa[:, 0, 0:8]), after=ev_mm)
        e_lg_prev = e_lg
        P.op("vector", lambda e: e.reduce_max(y[:, NEGM:NEGM + 1], y[:, LG:LG + 8],
                                              axis=AX.X, negate=True), inc=False)
        ev_nm = P.op("vector", lambda e: e.reduce_max(y[:, MAXL:MAXL + 1], y[:, LG:LG + 8], axis=AX.X))
        P.wait("scalar", ev_nm)
        e_se = P.op("scalar", lambda e: e.activation(y[:, T8:T8 + 8], y[:, LG:LG + 8], AF.Exp,
                                                     bias=y[:, NEGM:NEGM + 1],
                                                     accum_out=y[:, SUME:SUME + 1]))
        P.wait("vector", e_se)
        P.op("vector", lambda e, q=q: e.reciprocal(y[:, GW0 + q:GW0 + q + 1],
                                                   y[:, SUME:SUME + 1]), inc=False)
        P.op("vector", lambda e: e.tensor_scalar(y[:, T8:T8 + 8], y[:, LG:LG + 8],
                                                 y[:, MAXL:MAXL + 1], None, ALU.is_equal), inc=False)
        P.op("vector", lambda e: e.scalar_tensor_tensor(y[:, T8:T8 + 8], y[:, T8:T8 + 8], -65536.0,
                                                        small[:, 136:144], ALU.mult, ALU.add), inc=False)
        P.op("vector", lambda e, q=q: e.tensor_reduce(y[:, GT1 + q:GT1 + q + 1], y[:, T8:T8 + 8],
                                                      axis=AX.X, op=ALU.min), inc=False)
        ev_oh = P.op("vector", lambda e, q=q: e.tensor_scalar(
            y[:, T8:T8 + 8], IOTA8(t), y[:, GT1 + q:GT1 + q + 1], None, ALU.is_equal))
        P.wait("tensor", [ev_oh, ohe_prev])
        ev_T = P.op("tensor", lambda e: e.transpose(pc[0:8, 0, 0:128], y[:, T8:T8 + 8], IDENT(t)))
        ev_ohe = P.op("vector", lambda e, q=q: e.tensor_copy(
            y[0:8, 2048 + q * 128: 2048 + (q + 1) * 128], pc[0:8, 0, 0:128]), after=ev_T)
        ohe_prev = ev_ohe
        oh_evs.append(ev_ohe)
        sb_ = stage[q % 2]
        tev = []
        P.wait("tensor", st_dma[q % 2])
        for c in range(NCF):
            evT = P.op("tensor", lambda e, q=q, c=c: e.transpose(
                pb[:, c % 2, 0:128], big[:, c * T + q * 128: c * T + q * 128 + 128], IDENT(t)),
                after=pbev[c % 2])
            eve = P.op("vector", lambda e, c=c, sb_=sb_: e.tensor_copy(
                sb_[:, c * 128:(c + 1) * 128], pb[:, c % 2, 0:128]), after=evT)
            pbev[c % 2] = eve
            tev.append(eve)
        d = P.dma("sync", out=k.cc_t_in.ap()[q * 128:(q + 1) * 128, :], in_=sb_, after=tev)
        st_dma[q % 2] = d
        ag_t_dep.append(d)
    ev_ag_t = P.cc("AllGather", ALU.bypass, ins=[k.cc_t_in.ap().opt()],
                   outs=[k.cc_t_out.ap().opt()], after=ag_t_dep)
    ps_release(k, "ps_a", 0, e_lg_prev)
    ps_release(k, "ps_c", 0, ohe_prev)
    P.op("vector", lambda e: e.tensor_tensor_scan(
        y[0:8, 4096:6144], y[0:8, 2048:4096], y[0:8, 2048:4096], 0.0,
        ALU.add, ALU.bypass), inc=False, after=oh_evs)
    P.op("vector", lambda e: e.tensor_copy(y[0:8, 8509:8510], y[0:8, 6143:6144]), inc=False)
    P.op("vector", lambda e: e.tensor_tensor(y[0:8, 6144:8192], y[0:8, 4096:6144],
                                             y[0:8, 2048:4096], ALU.subtract), inc=False)
    ev_pr = P.op("vector", lambda e: e.tensor_tensor(y[0:8, 6144:8192], y[0:8, 6144:8192],
                                                     y[0:8, 2048:4096], ALU.mult))
    lr_evs = []
    prev_lrT = None
    P.wait("tensor", ev_pr)
    for q in range(16):
        P.wait("tensor", prev_lrT)
        evT = P.op("tensor", lambda e, q=q: e.transpose(
            pc[:, 1, 0:8], y[0:8, 6144 + q * 128: 6144 + (q + 1) * 128], IDENT(t)[0:8, 0:8]))
        ev = P.op("vector", lambda e, q=q: e.tensor_reduce(
            y[:, GLR + q:GLR + q + 1], pc[:, 1, 0:8], axis=AX.X, op=ALU.add), after=evT)
        prev_lrT = ev
        lr_evs.append(ev)
    ps_release(k, "ps_c", 1, lr_evs[-1])
    P.op("vector", lambda e: e.scalar_tensor_tensor(
        y[:, GTMP:GTMP + 16], y[:, GT1:GT1 + 16], float(ROWL), y[:, GLR:GLR + 16],
        ALU.mult, ALU.add), inc=False, after=lr_evs)
    ev_si = P.op("vector", lambda e: e.tensor_copy(u32[:, 0:16], y[:, GTMP:GTMP + 16]))
    P.op("vector", lambda e: e.tensor_scalar(y[:, GIDS:GIDS + 16], small[:, 144:160],
                                             small[:, 112:113], None, ALU.add), inc=False,
         after=[ev_ig, ld[3]])
    ev_sv = P.op("vector", lambda e: e.tensor_copy(u32[:, 16:32], y[:, GIDS:GIDS + 16]))
    sc_evs = []
    for q in range(16):
        ev = P.dma_custom("gpsimd", lambda e, q=q: e.indirect_dma_start(
            out=k.idt_in.ap(), out_offset=bass.IndirectOffsetOnAxis(ap=u32[:, q:q + 1], axis=0),
            in_=u32[:, 16 + q:17 + q], in_offset=None), after=[ev_si, ev_sv])
        sc_evs.append(ev)
    ev_cu = P.op("vector", lambda e: e.tensor_copy(u32[0:8, 32:33], y[0:8, 8509:8510]))
    ev_cd = P.dma("sync", out=k.idt_in.ap()[8 * ROWL:8 * ROWL + 8, :], in_=u32[0:8, 32:33],
                  after=ev_cu)
    ev_ag_i = P.cc("AllGather", ALU.bypass, ins=[k.idt_in.ap().opt()],
                   outs=[k.idt_out.ap().opt()], after=sc_evs + [ev_cd])
    ev_cl = P.dma("sync", out=u32[0:8, 36:44],
                  in_=k.idt_out.ap().rearrange("(i r) one -> i (r one)", i=8)[:, 8 * ROWL:8 * ROWL + 8],
                  after=ev_ag_i)
    ev_cf = P.op("vector", lambda e: e.tensor_copy(y[0:8, 8500:8508], u32[0:8, 36:44]), after=ev_cl)
    P.op("vector", lambda e: e.tensor_tensor(y[0:8, 8510:8518], y[0:8, 8500:8508],
                                             small[0:8, 120:128], ALU.mult), inc=False)
    ev_ce = P.op("vector", lambda e: e.tensor_reduce(y[0:8, 8508:8509], y[0:8, 8510:8518],
                                                     axis=AX.X, op=ALU.add))
    P.wait("tensor", [ev_ce, pbev[0], pbev[1]] + ps_wait_list(k, "ps_a", [1, 2, 3]))
    ev_m1 = P.op("tensor", lambda e: e.matmul(pa[0:8, 1, 0:1], MASK_GT(t)[0:8, 0:8],
                                              y[0:8, 8508:8509], start=True, stop=True))
    ev_of = P.op("vector", lambda e: e.tensor_copy(y[0:8, 8518:8519], pa[0:8, 1, 0:1]), after=ev_m1)
    ps_release(k, "ps_a", 1, ev_of)
    ev_dg = P.op("vector", lambda e: e.tensor_scalar(y[0:8, 8510:8518], IDENT(t)[0:8, 0:8],
                                                     y[0:8, 8518:8519], None, ALU.mult))
    P.wait("tensor", ev_dg)
    ev_m2 = P.op("tensor", lambda e: e.matmul(pa[:, 2, 0:8], ALLONES(t)[0:8, 0:128],
                                              y[0:8, 8510:8518], start=True, stop=True))
    ev_ob = P.op("vector", lambda e: e.tensor_copy(y[:, OFFB:OFFB + 8], pa[:, 2, 0:8]), after=ev_m2)
    ps_release(k, "ps_a", 2, ev_ob)
    ev_dc = P.op("vector", lambda e: e.tensor_scalar(y[0:8, 8510:8518], IDENT(t)[0:8, 0:8],
                                                     y[0:8, 8508:8509], None, ALU.mult))
    P.wait("tensor", ev_dc)
    ev_m3 = P.op("tensor", lambda e: e.matmul(pa[:, 3, 0:8], ALLONES(t)[0:8, 0:128],
                                              y[0:8, 8510:8518], start=True, stop=True))
    ev_cb = P.op("vector", lambda e: e.tensor_copy(y[:, CNTB:CNTB + 8], pa[:, 3, 0:8]), after=ev_m3)
    ps_release(k, "ps_a", 3, ev_cb)
    ev_m4 = P.op("tensor", lambda e: e.matmul(pb[0:1, 0, 0:8], small[0:8, 113:114],
                                              y[0:8, 8500:8508], start=True, stop=True), after=ev_cf)
    ev_mr = P.op("vector", lambda e: e.tensor_copy(y[0:1, 8519:8527], pb[0:1, 0, 0:8]), after=ev_m4)
    ps_release(k, "ps_b", 0, ev_mr)
    P.wait("tensor", ev_mr)
    ev_m5 = P.op("tensor", lambda e: e.matmul(pb[:, 1, 0:8], ALLONES(t)[0:1, 0:128],
                                              y[0:1, 8519:8527], start=True, stop=True))
    ev_omy = P.op("vector", lambda e: e.tensor_copy(y[:, OFFMY:OFFMY + 8], pb[:, 1, 0:8]), after=ev_m5)
    ps_release(k, "ps_b", 1, ev_omy)
    ev_tot = P.op("vector", lambda e: e.tensor_tensor(y[:, TOTG:TOTG + 1], y[:, OFFB + 7:OFFB + 8],
                                                      y[:, CNTB + 7:CNTB + 8], ALU.add), after=[ev_ob, ev_cb])
    P.op("vector", lambda e: e.tensor_copy(y[:, GRK:GRK + 16], y[:, GLR:GLR + 16]),
         inc=False, after=ev_omy)
    for e_ in range(8):
        P.op("vector", lambda e, e_=e_: e.tensor_scalar(y[:, GTMP:GTMP + 16], y[:, GT1:GT1 + 16],
                                                        float(e_), None, ALU.is_equal), inc=False)
        P.op("vector", lambda e, e_=e_: e.tensor_scalar(y[:, GTMP:GTMP + 16], y[:, GTMP:GTMP + 16],
                                                        y[:, OFFMY + e_:OFFMY + e_ + 1], None, ALU.mult), inc=False)
        P.op("vector", lambda e: e.tensor_tensor(y[:, GRK:GRK + 16], y[:, GRK:GRK + 16],
                                                 y[:, GTMP:GTMP + 16], ALU.add), inc=False)
    P.op("vector", lambda e: e.tensor_scalar(y[:, GKP:GKP + 16], y[:, GRK:GRK + 16],
                                             2048.0, None, ALU.is_lt), inc=False)
    ev_wg = P.op("vector", lambda e: e.tensor_tensor(stats[:, 0:16], y[:, GW0:GW0 + 16],
                                                     y[:, GKP:GKP + 16], ALU.mult))
    P.op("vector", lambda e: e.tensor_tensor(y[:, GSL:GSL + 16], y[:, GRK:GRK + 16],
                                             y[:, GKP:GKP + 16], ALU.mult), inc=False)
    P.op("vector", lambda e: e.scalar_tensor_tensor(y[:, GTMP:GTMP + 16], y[:, GT1:GT1 + 16],
                                                    2048.0, y[:, GSL:GSL + 16], ALU.mult, ALU.add), inc=False)
    ev_own = P.op("vector", lambda e: e.tensor_copy(u32[:, 48:64], y[:, GTMP:GTMP + 16]))
    P.op("vector", lambda e: e.tensor_scalar(y[:, GIS:GIS + 16], small[:, 144:160],
                                             0.0, None, ALU.mult), inc=False)
    for i_ in range(1, 8):
        P.op("vector", lambda e, i_=i_: e.tensor_scalar(y[:, GTMP:GTMP + 16], small[:, 144:160],
                                                        y[:, OFFB + i_:OFFB + i_ + 1], None, ALU.is_ge), inc=False)
        P.op("vector", lambda e: e.tensor_tensor(y[:, GIS:GIS + 16], y[:, GIS:GIS + 16],
                                                 y[:, GTMP:GTMP + 16], ALU.add), inc=False)
    P.op("vector", lambda e: e.tensor_copy(y[:, GRS:GRS + 16], small[:, 144:160]), inc=False)
    for i_ in range(1, 8):
        P.op("vector", lambda e, i_=i_: e.tensor_scalar(y[:, GTMP:GTMP + 16], y[:, GIS:GIS + 16],
                                                        float(i_), None, ALU.is_equal), inc=False)
        P.op("vector", lambda e, i_=i_: e.tensor_scalar(y[:, GTMP:GTMP + 16], y[:, GTMP:GTMP + 16],
                                                        y[:, OFFB + i_:OFFB + i_ + 1], None, ALU.mult), inc=False)
        P.op("vector", lambda e: e.tensor_tensor(y[:, GRS:GRS + 16], y[:, GRS:GRS + 16],
                                                 y[:, GTMP:GTMP + 16], ALU.subtract), inc=False)
    P.op("vector", lambda e: e.scalar_tensor_tensor(y[:, GTMP:GTMP + 16], y[:, GIS:GIS + 16],
                                                    float(IDN), y[:, GRS:GRS + 16], ALU.mult, ALU.add), inc=False)
    P.op("vector", lambda e: e.tensor_scalar(y[:, GTMP:GTMP + 16], y[:, GTMP:GTMP + 16],
                                             small[:, 114:115], None, ALU.add), inc=False)
    ev_fx = P.op("vector", lambda e: e.tensor_copy(u32[:, 16:32], y[:, GTMP:GTMP + 16]),
                 after=sc_evs)
    ev_ks = P.op("vector", lambda e: e.tensor_scalar(y[:, GRS:GRS + 16], small[:, 144:160],
                                                     y[:, TOTG:TOTG + 1], None, ALU.is_lt), after=ev_tot)
    id_evs = []
    for q in range(16):
        ev = P.dma_custom("gpsimd", lambda e, q=q: e.indirect_dma_start(
            out=u32[:, q:q + 1], out_offset=None, in_=k.idt_out.ap(),
            in_offset=bass.IndirectOffsetOnAxis(ap=u32[:, 16 + q:17 + q], axis=0)),
            after=[ev_ag_i, ev_fx])
        id_evs.append(ev)
    P.op("vector", lambda e: e.tensor_copy(y[:, GIDS:GIDS + 16], u32[:, 0:16]),
         inc=False, after=id_evs + [ev_ks])
    P.op("vector", lambda e: e.tensor_tensor(y[:, GIDS:GIDS + 16], y[:, GIDS:GIDS + 16],
                                             y[:, GRS:GRS + 16], ALU.mult), inc=False)
    ev_idc = P.op("vector", lambda e: e.tensor_copy(u32[:, 0:16], y[:, GIDS:GIDS + 16]))
    import os as _os
    _d = _os.environ.get("DBGL2F")
    if _d:
        srcs = {"lrank": y[:, GLR:GLR + 16], "top1": y[:, GT1:GT1 + 16],
                "rankg": y[:, GRK:GRK + 16], "slot": y[:, GSL:GSL + 16],
                "wg": stats[:, 0:16], "ids": y[:, GIDS:GIDS + 16],
                "is": y[:, GIS:GIS + 16], "offb": y[:, OFFB:OFFB + 16],
                "w0": y[:, GW0:GW0 + 16]}
        k._l2f_tap = P.op("vector", lambda e, src=srcs[_d]: e.tensor_copy(
            stats[:, 1280:1296], src))
    dg_tr = []
    last_st = [st_dma[0], st_dma[1]]
    pbev = [ev_mr, ev_omy]
    for q in range(16):
        sb_ = stage[q % 2]
        ev_g = P.dma_custom("gpsimd", lambda e, q=q, sb_=sb_: e.indirect_dma_start(
            out=sb_, out_offset=None, in_=k.cc_t_out.ap(),
            in_offset=bass.IndirectOffsetOnAxis(ap=u32[:, q:q + 1], axis=0)),
            after=[ev_ag_t, ev_idc, last_st[q % 2]])
        P.wait("tensor", ev_g)
        for c in range(NCF):
            evT = P.op("tensor", lambda e, q=q, c=c, sb_=sb_: e.transpose(
                pb[:, c % 2, 0:128], sb_[:, c * 128:(c + 1) * 128], IDENT(t)), after=pbev[c % 2])
            eve = P.op("vector", lambda e, q=q, c=c: e.tensor_copy(
                big[:, c * T + q * 128: c * T + q * 128 + 128], pb[:, c % 2, 0:128]), after=evT)
            pbev[c % 2] = eve
            if c == NCF - 1:
                last_st[q % 2] = eve
            dg_tr.append(eve)
    ffn_dma = []
    wo_prev = None
    otmp_rd = None
    dma_tmp = [None, None]
    for q4 in range(4):
        h_evs = []
        for ic in range(32):
            bank = ic % 4
            P.wait("tensor", ps_wait_list(k, "ps_a", [bank]))
            ev1 = _stream_mm(k, "l2f_wi", ic * 128, 8,
                             lambda kc, q4=q4: big[:, kc * T + q4 * 512: kc * T + q4 * 512 + 512],
                             pa[:, bank, :],
                             extra_after=(dg_tr if (q4 == 0 and ic == 0) else None))
            ev = P.op("scalar", lambda e, ic=ic, bank=bank: e.activation(
                y[:, ic * 512:(ic + 1) * 512], pa[:, bank, :], AF.Gelu),
                after=[ev1] + ([wo_prev] if (ic == 0 and wo_prev is not None) else []))
            ps_release(k, "ps_a", bank, ev)
            h_evs.append(ev)
        for hc in range(8):
            bank = hc % 2
            P.wait("tensor", ps_wait_list(k, "ps_c", [bank]))
            ev2 = _stream_mm(k, "l2f_wo", hc * 128, 32,
                             lambda kc: y[:, kc * 512:(kc + 1) * 512],
                             pc[:, bank, :], extra_after=(h_evs if hc == 0 else None))
            wo_prev = ev2
            evo = P.op("vector", lambda e, bank=bank: e.tensor_copy(stats[:, 512:1024], pc[:, bank, :]),
                       after=[ev2] + ([otmp_rd] if otmp_rd is not None else []))
            ps_release(k, "ps_c", bank, evo)
            P.wait("tensor", evo)
            for st in range(4):
                tb = st % 2
                evT = P.op("tensor", lambda e, st=st, tb=tb: e.transpose(
                    pb[:, tb, 0:128], stats[:, 512 + st * 128: 512 + (st + 1) * 128], IDENT(t)),
                    after=pbev[tb])
                evp = P.op("vector", lambda e, tb=tb: e.tensor_copy(
                    stats[:, 1024 + tb * 128: 1152 + tb * 128], pb[:, tb, 0:128]),
                    after=[evT] + ([dma_tmp[tb]] if dma_tmp[tb] is not None else []))
                pbev[tb] = evp
                if st == 3:
                    otmp_rd = evp
                d = P.dma("sync",
                          out=k.cc_y_in.ap()[q4 * 512 + st * 128: q4 * 512 + (st + 1) * 128,
                                             hc * 128:(hc + 1) * 128],
                          in_=stats[:, 1024 + tb * 128: 1152 + tb * 128], after=evp)
                dma_tmp[tb] = d
                ffn_dma.append(d)
    ev_ag_y = P.cc("AllGather", ALU.bypass, ins=[k.cc_y_in.ap().opt()],
                   outs=[k.cc_y_out.ap().opt()], after=ffn_dma)
    x_evs = []
    own_rd = [None, None]
    for q in range(16):
        sb_ = stage[q % 2]
        ev_g = P.dma_custom("gpsimd", lambda e, q=q, sb_=sb_: e.indirect_dma_start(
            out=sb_, out_offset=None, in_=k.cc_y_out.ap(),
            in_offset=bass.IndirectOffsetOnAxis(ap=u32[:, 48 + q:49 + q], axis=0)),
            after=[ev_ag_y, ev_own, own_rd[q % 2]])
        ev_w = P.op("vector", lambda e, q=q, sb_=sb_: e.tensor_scalar(
            sb_, sb_, stats[:, q:q + 1], None, ALU.mult), after=[ev_g, ev_wg])
        P.wait("tensor", ev_w)
        for c in range(NCF):
            evT = P.op("tensor", lambda e, c=c, sb_=sb_: e.transpose(
                pb[:, c % 2, 0:128], sb_[:, c * 128:(c + 1) * 128], IDENT(t)), after=pbev[c % 2])
            eva = P.op("vector", lambda e, q=q, c=c: e.tensor_tensor(
                xcol(x, c, q * 128, 128), xcol(x, c, q * 128, 128), pb[:, c % 2, 0:128], ALU.add),
                after=evT)
            pbev[c % 2] = eva
            if c == NCF - 1:
                own_rd[q % 2] = eva
            x_evs.append(eva)
    for b_ in range(2):
        ps_release(k, "ps_b", b_, pbev[b_])
    import os as _os
    _d = _os.environ.get("DBGL2F")
    if _d:
        ev_tap = k._l2f_tap
        ev = P.op("vector", lambda e: e.tensor_copy(x[:, 0:16], stats[:, 1280:1296]),
                  after=[ev_tap] + x_evs)
        x_evs = x_evs + [ev]
    P.phase("")
    return x_evs


def emit_stub(k, evs):
    raise NotImplementedError


def _dbg_ln(k, evs):
    evs = emit_ln(k, "x", evs, dst_name="y", sq_scratch=xcol(k.t["big"], 0))
    out = []
    for c in range(NCF):
        out.append(k.P.op("vector", lambda e, c=c: e.tensor_copy(
            xcol(k.t["x"], c), xcol(k.t["y"], c)), after=(evs if c == 0 else None)))
    return out


EMITTERS = {"ln": _dbg_ln, "l1f": emit_l1f, "l0f": emit_l0f, "l2a": emit_l2a, "l3a": emit_l3a, "l3f": emit_l3f, "l0a": emit_l0a, "l1a": emit_l1a, "l2f": emit_l2f}
for b in BLOCKS_ALL:
    EMITTERS.setdefault(b, emit_stub)



def kernel_dev(inputs, blocks=None):
    blocks = blocks or BLOCKS_ALL
    w, xs = prep_host(inputs)
    k = build_program(blocks)
    need = {n for n in WSHAPES if any(n.startswith(b + "_") for b in blocks)}
    in_maps = []
    for b in range(8):
        m = {"x_t": xs[b], "core_id": np.full((1, 1), float(b), np.float32)}
        for n in sorted(need):
            if n in w:
                m[n] = w[n]
        if "l2f" in blocks:
            m["l2f_wi"] = np.ascontiguousarray(w["l2f_wi"][b])
            m["l2f_wo"] = np.ascontiguousarray(w["l2f_wo"][b])
            pcv = np.zeros((128, 24), np.float32)
            pcv[:, 0] = b * 2048.0
            pcv[:, 1] = (np.arange(128) < b).astype(np.float32)
            pcv[:, 2] = b * 2176.0
            pcv[:, 8 + b] = 1.0
            m["l2f_pc"] = pcv
        in_maps.append(m)
    res = run_bass_kernel_spmd(k.nc, in_maps, core_ids=list(range(8)))
    out = np.stack([np.ascontiguousarray(np.asarray(r["out_t"]).T) for r in res.results])
    return out


def kernel(**inputs):
    return kernel_dev(inputs, BLOCKS_ALL)


# version 04
# speedup vs baseline: 1.2477x; 1.2477x over previous
import contextlib
import os
import numpy as np
import concourse.bass as bass
import concourse.mybir as mybir
from concourse.bass_utils import run_bass_kernel_spmd

F32 = mybir.dt.float32
BF16 = mybir.dt.bfloat16
U8 = mybir.dt.uint8
AF = mybir.ActivationFunctionType
ALU = mybir.AluOpType
AX = mybir.AxisListType

T = 2048
D = 1024
NCF = 8
NTT = 4
TTW = 512
BLOCKS_ALL = ["l0a", "l0f", "l1a", "l1f", "l2a", "l2f", "l3a", "l3f"]


class Ev:
    __slots__ = ("sem", "val")
    def __init__(self, sem, val):
        self.sem, self.val = sem, val


class Prog:
    ENGINES = ["sync", "gpsimd", "tensor", "vector", "scalar"]
    NDMA = 4
    SEMS = (["sync", "gpsimd", "tensor", "vector", "scalar", "cc0"]
            + [f"d{q}{i}" for q in ("sync", "gp", "scalar") for i in range(4)])
    DMA_Q = {"sync": "dsync", "gpsimd": "dgp", "scalar": "dscalar"}

    SELF_SYNC = ("vector", "scalar", "gpsimd")

    def __init__(self, nc):
        self.nc = nc
        self.queues = {e: [] for e in self.ENGINES}
        self.cnt = {s: 0 for s in self.SEMS}
        self.waited = {e: {} for e in self.ENGINES}
        self.last_compute = {e: 0 for e in self.ENGINES}
        self.dma_rr = {}
        self.sems = {}

    def wait_self(self, q, val):
        w = self.waited[q]
        key = "_self_" + q
        if w.get(key, -1) >= val:
            return
        w[key] = val
        self.queues[q].append(lambda eng: eng.wait_ge(self.sems[q], val))

    def wait(self, q, ev):
        if ev is None:
            return
        if isinstance(ev, (list, tuple)):
            for e in ev:
                self.wait(q, e)
            return
        val = ev.val
        if ev.sem == q:
            return
        w = self.waited[q]
        if w.get(ev.sem, -1) >= val:
            return
        w[ev.sem] = val
        s, v = ev.sem, val
        self.queues[q].append(lambda eng: eng.wait_ge(self.sems[s], v))

    SERIAL = os.environ.get("KSERIAL", "") == "1"
    PHASES = set(p for p in os.environ.get("KSER_PHASES", "").split(",") if p)
    cur_phase = ""

    def phase(self, name):
        self.cur_phase = name
        if name in self.PHASES:
            self._ser_on = True
        else:
            self._ser_on = False

    def _serialize(self, q):
        for s in self.SEMS:
            if s != q and self.cnt[s] > 0:
                self.wait(q, Ev(s, self.cnt[s]))

    def op(self, q, fn, inc=True, after=None, drain=False, dsync=False):
        if q in self.SELF_SYNC:
            inc = True
            if self.last_compute[q] > 0:
                self.wait_self(q, self.last_compute[q])
        if getattr(self, "_ser_on", False):
            self._serialize(q)
            drain = True
            inc = True
        if self.SERIAL:
            self._serialize(q)
            drain = True
            inc = True
        if after is not None:
            self.wait(q, after)
        if drain:
            self.queues[q].append(lambda eng: eng.drain())
        if inc:
            self.cnt[q] += 1
            v = self.cnt[q]
            if q in self.SELF_SYNC:
                self.last_compute[q] = v
            def emit(eng, fn=fn, q=q):
                fn(eng).then_inc(self.sems[q], 1)
            self.queues[q].append(emit)
            return Ev(q, v)
        self.queues[q].append(lambda eng, fn=fn: fn(eng))
        return None

    def dma(self, q, out, in_, after=None, **kw):
        if getattr(self, "_ser_on", False):
            self._serialize(q)
        if self.SERIAL:
            self._serialize(q)
        if after is not None:
            self.wait(q, after)
        base = self.DMA_Q[q]
        idx = self.dma_rr.get(q, 0)
        self.dma_rr[q] = idx + 1
        ds = f"{base[1:] if False else base.replace('dma','')}"
        ds = base.replace("dsync", "dsync").replace("dgp", "dgp")
        ds = {"dsync": "dsync", "dgp": "dgp", "dscalar": "dscalar"}[base]
        ds = ds.replace("dsync", "dsync")
        sem = f"d{'sync' if q == 'sync' else ('gp' if q == 'gpsimd' else 'scalar')}{idx % self.NDMA}"
        if self.cnt[sem] > 0:
            self.wait(q, Ev(sem, self.cnt[sem]))
        self.cnt[sem] += 16
        v = self.cnt[sem]
        def emit(eng, out=out, in_=in_, kw=kw, sem=sem):
            eng.dma_start(out=out, in_=in_, **kw).then_inc(self.sems[sem], 16)
        self.queues[q].append(emit)
        return Ev(sem, v)

    def dma_custom(self, q, fn, after=None):
        if getattr(self, "_ser_on", False) or self.SERIAL:
            self._serialize(q)
        if after is not None:
            self.wait(q, after)
        idx = self.dma_rr.get(q, 0)
        self.dma_rr[q] = idx + 1
        sem = f"d{'sync' if q == 'sync' else ('gp' if q == 'gpsimd' else 'scalar')}{idx % self.NDMA}"
        if self.cnt[sem] > 0:
            self.wait(q, Ev(sem, self.cnt[sem]))
        self.cnt[sem] += 16
        v = self.cnt[sem]
        def emit(eng, fn=fn, sem=sem):
            fn(eng).then_inc(self.sems[sem], 16)
        self.queues[q].append(emit)
        return Ev(sem, v)

    def cc(self, kind, op, ins, outs, after=None):
        q = "gpsimd"
        if getattr(self, "_ser_on", False) or self.SERIAL:
            self._serialize(q)
        if after is not None:
            self.wait(q, after)
        self.cnt["cc0"] += 1
        v = self.cnt["cc0"]
        def emit(eng, kind=kind, op=op, ins=ins, outs=outs):
            eng.collective_compute(kind, op, replica_groups=[list(range(8))],
                                   ins=ins, outs=outs).then_inc(self.sems["cc0"], 1)
        self.queues[q].append(emit)
        return Ev("cc0", v)

    def fence(self, q):
        return Ev(q, self.cnt[q])

    def play(self, block):
        for e in self.ENGINES:
            qops = self.queues[e]
            def mk(qops):
                def body(eng):
                    for f in qops:
                        f(eng)
                return body
            getattr(block, e)(mk(qops))


WSHAPES = {
    "l0a_wT": (1024, 3072), "l0a_wout": (1024, 1024),
    "l0f_baseT": (1024, 1024), "l0f_baseb": (128, 8), "l0f_sharedAT": (1024, 16),
    "l0f_sharedBT": (16, 1024), "l0f_routedAT": (1024, 96), "l0f_routedBT": (96, 1024),
    "l0f_routerT": (1024, 6), "l0f_routerb": (128, 6), "l0f_blk16": (6, 96),
    "l1a_wqT": (1024, 1024), "l1a_wkT": (1024, 256), "l1a_wvT": (1024, 256),
    "l1a_wprojT": (1024, 1024), "l1a_cwq": (128, 8, 4), "l1a_cwk": (128, 2, 4),
    "l1a_cwv": (128, 2, 4), "l1a_wbetaT": (1024, 4), "l1a_bbeta": (1, 4),
    "l1a_wgkT": (1024, 4), "l1a_alog": (1, 4), "l1a_dt": (1, 4),
    "l1a_wgateT": (1024, 1024), "l1a_rmsw": (1, 64),
    "l1f_wrT": (1024, 1024), "l1f_wkT": (1024, 1024), "l1f_wvT": (1024, 1024),
    "l1f_mur": (128, 8), "l1f_muk": (128, 8),
    "l2a_wq": (1024, 1024), "l2a_bq": (128, 8), "l2a_wk": (1024, 1024),
    "l2a_bk": (128, 8), "l2a_wv": (1024, 1024), "l2a_bv": (1, 1024),
    "l2a_wo": (1024, 1024), "l2a_bo": (128, 8),
    "l2f_gateT": (1024, 8), "l2f_lng": (128, 8), "l2f_lnb": (128, 8),
    "l2f_wi": (1024, 4096), "l2f_wo": (4096, 1024), "l2f_pc": (128, 24),
    "l3a_wqT": (1024, 1024), "l3a_wkT": (1024, 256), "l3a_wvT": (1024, 256),
    "l3a_woT": (1024, 1024),
    "l3f_w1T": (1024, 4096), "l3f_w2T": (1024, 4096), "l3f_h1T": (4096, 4096),
    "l3f_h2T": (4096, 4096), "l3f_w3T": (4096, 1024),
    "l3f_s1": (128, 32), "l3f_s2": (128, 32), "l3f_sh1": (128, 32),
    "l3f_sh2": (128, 32), "l3f_s3": (128, 8),
}


def prep_host(inputs):
    g = lambda kk: np.asarray(inputs[kk])
    tr = lambda a: np.ascontiguousarray(np.asarray(a, dtype=np.float32).T)
    cc = lambda a: np.ascontiguousarray(np.asarray(a, dtype=np.float32))
    w = {}
    w["l0a_wT"] = tr(g("L0_attn_w"))
    w["l0a_wout"] = cc(g("L0_attn_w_out"))
    w["l0f_baseT"] = tr(g("L0_ffn_base_W"))
    w["l0f_baseb"] = cc(np.asarray(g("L0_ffn_base_b"), np.float32).reshape(8, 128).T)
    w["l0f_sharedAT"] = tr(g("L0_ffn_shared_A"))
    w["l0f_sharedBT"] = tr(g("L0_ffn_shared_B"))
    w["l0f_routedAT"] = np.ascontiguousarray(
        np.concatenate([np.asarray(g("L0_ffn_routed_A"), np.float32)[e].T for e in range(6)], axis=1))
    w["l0f_routedBT"] = np.ascontiguousarray(
        np.concatenate([np.asarray(g("L0_ffn_routed_B"), np.float32)[e].T for e in range(6)], axis=0))
    w["l0f_routerT"] = tr(g("L0_ffn_router_W"))
    w["l0f_routerb"] = cc(np.tile(np.asarray(g("L0_ffn_router_b"), np.float32).reshape(1, 6), (128, 1)))
    w["l0f_blk16"] = cc(np.kron(np.eye(6, dtype=np.float32), np.ones((1, 16), np.float32)))
    QPERM = [( (c % 4) + 4 * (rh + 2 * (c // 4)) ) for c in range(8) for rh in range(2)]
    def _permcols(wt):
        blocks = []
        for slot in QPERM:
            blocks.append(wt[:, slot * 64:(slot + 1) * 64])
        return np.ascontiguousarray(np.concatenate(blocks, axis=1))
    def _permrows(wt):
        blocks = []
        for slot in QPERM:
            blocks.append(wt[slot * 64:(slot + 1) * 64, :])
        return np.ascontiguousarray(np.concatenate(blocks, axis=0))
    w["l1a_wqT"] = _permcols(tr(g("L1_attn_Wq")))
    w["l1a_wkT"] = tr(g("L1_attn_Wk"))
    w["l1a_wvT"] = tr(g("L1_attn_Wv"))
    w["l1a_wprojT"] = _permrows(tr(g("L1_attn_Wproj")))
    _cwq = cc(g("L1_attn_cwq"))
    _cwq = np.concatenate([_cwq[s * 64:(s + 1) * 64] for s in QPERM], axis=0)
    w["l1a_cwq"] = np.ascontiguousarray(_cwq.reshape(8, 128, 4).transpose(1, 0, 2))
    w["l1a_cwk"] = np.ascontiguousarray(cc(g("L1_attn_cwk")).reshape(2, 128, 4).transpose(1, 0, 2))
    w["l1a_cwv"] = np.ascontiguousarray(cc(g("L1_attn_cwv")).reshape(2, 128, 4).transpose(1, 0, 2))
    w["l1a_wbetaT"] = tr(g("L1_attn_Wbeta"))
    w["l1a_bbeta"] = cc(g("L1_attn_b_beta").reshape(1, 4))
    w["l1a_wgkT"] = tr(g("L1_attn_Wgk"))
    w["l1a_alog"] = cc(g("L1_attn_A_log").reshape(1, 4))
    w["l1a_dt"] = cc(g("L1_attn_dt_bias").reshape(1, 4))
    w["l1a_wgateT"] = _permcols(tr(g("L1_attn_Wgate")))
    w["l1a_rmsw"] = cc(g("L1_attn_rms_w").reshape(1, 64))
    w["l1f_wrT"] = tr(g("L1_ffn_W_r"))
    w["l1f_wkT"] = tr(g("L1_ffn_W_k"))
    w["l1f_wvT"] = tr(g("L1_ffn_W_v"))
    w["l1f_mur"] = cc(np.asarray(g("L1_ffn_mu_r"), np.float32).reshape(8, 128).T)
    w["l1f_muk"] = cc(np.asarray(g("L1_ffn_mu_k"), np.float32).reshape(8, 128).T)
    w["l2a_wq"] = cc(g("L2_attn_Wq"))
    w["l2a_bq"] = cc(np.asarray(g("L2_attn_bq"), np.float32).reshape(8, 128).T)
    w["l2a_wk"] = cc(g("L2_attn_Wk"))
    w["l2a_bk"] = cc(np.asarray(g("L2_attn_bk"), np.float32).reshape(8, 128).T)
    w["l2a_wv"] = cc(g("L2_attn_Wv"))
    w["l2a_bv"] = cc(g("L2_attn_bv").reshape(1, D))
    w["l2a_wo"] = cc(g("L2_attn_Wo"))
    w["l2a_bo"] = cc(np.asarray(g("L2_attn_bo"), np.float32).reshape(8, 128).T)
    w["l2f_gateT"] = tr(g("L2_ffn_gate_weight"))
    w["l2f_lng"] = cc(np.asarray(g("L2_ffn_ln_gamma"), np.float32).reshape(8, 128).T)
    w["l2f_lnb"] = cc(np.asarray(g("L2_ffn_ln_beta"), np.float32).reshape(8, 128).T)
    w["l2f_wi"] = cc(g("L2_ffn_wi"))
    w["l2f_wo"] = cc(g("L2_ffn_wo"))
    _perm = [0, 4, 1, 5, 2, 6, 3, 7, 8, 12, 9, 13, 10, 14, 11, 15]
    _wqT = tr(g("L3_attn_wq"))
    w["l3a_wqT"] = np.ascontiguousarray(
        np.concatenate([_wqT[:, h * 64:(h + 1) * 64] for h in _perm], axis=1))
    w["l3a_wkT"] = tr(g("L3_attn_wk"))
    w["l3a_wvT"] = tr(g("L3_attn_wv"))
    _woT = tr(g("L3_attn_wo"))
    w["l3a_woT"] = np.ascontiguousarray(
        np.concatenate([_woT[h * 64:(h + 1) * 64, :] for h in _perm], axis=0))
    w["l3f_w1T"] = tr(g("L3_ffn_w1_w"))
    w["l3f_w2T"] = tr(g("L3_ffn_w2_w"))
    w["l3f_h1T"] = tr(g("L3_ffn_h1_w"))
    w["l3f_h2T"] = tr(g("L3_ffn_h2_w"))
    w["l3f_w3T"] = tr(g("L3_ffn_w3_w"))
    w["l3f_s1"] = cc(np.asarray(g("L3_ffn_w1_s"), np.float32).reshape(32, 128).T)
    w["l3f_s2"] = cc(np.asarray(g("L3_ffn_w2_s"), np.float32).reshape(32, 128).T)
    w["l3f_sh1"] = cc(np.asarray(g("L3_ffn_h1_s"), np.float32).reshape(32, 128).T)
    w["l3f_sh2"] = cc(np.asarray(g("L3_ffn_h2_s"), np.float32).reshape(32, 128).T)
    w["l3f_s3"] = cc(np.asarray(g("L3_ffn_w3_s"), np.float32).reshape(8, 128).T)
    x = np.asarray(inputs["L0_attn_x"], dtype=np.float32)
    xs = [np.ascontiguousarray(x[b].T) for b in range(8)]
    return w, xs


def xcol(tt_, c, t0=0, n=T):
    return tt_[:, c * T + t0: c * T + t0 + n]


class K:
    def __init__(self, blocks):
        self.blocks = blocks
        self.nc = bass.Bass("TRN2", target_bir_lowering=False)
        self.P = Prog(self.nc)
        self.hb = {}
        self.t = {}
        self.psfree = {}

    def hin(self, name, shape):
        self.hb[name] = self.nc.dram_tensor(name, list(shape), F32, kind="ExternalInput").ap()
        return self.hb[name]


IDENT = lambda t: t["consts"][:, 0:128]
MASK_GE = lambda t: t["consts"][:, 128:256]
MASK_GT = lambda t: t["consts"][:, 256:384]
ALLONES = lambda t: t["consts"][:, 384:512]
BLK64 = lambda t: t["consts"][:, 512:640]
PIDX = lambda t: t["consts"][:, 640:641]
IOTA8 = lambda t: t["consts"][:, 643:651]
ONESCOL = lambda t: t["consts"][:, 641:642]


def build_program(blocks):
    k = K(blocks)
    nc, P = k.nc, k.P
    k.hin("x_t", (D, T))
    k.hin("core_id", (1, 1))
    need = {w for w in WSHAPES if any(w.startswith(b + "_") for b in blocks)}
    for wname in sorted(need):
        k.hin(wname, WSHAPES[wname])
    k.out_t = nc.dram_tensor("out_t", [D, T], F32, kind="ExternalOutput").ap()
    k.row_scr = nc.dram_tensor("row_scr", [128, T], F32, kind="Internal").ap()
    k.spill = nc.dram_tensor("spill", [D, T], F32, kind="Internal").ap()
    k.spill2 = nc.dram_tensor("spill2", [D, T], F32, kind="Internal").ap()
    if "l2f" in blocks:
        U32 = mybir.dt.uint32
        IDN = 8 * 2176 + 8
        k.cc_t_in = nc.dram_tensor("cc_t_in", [T, D], F32, kind="Internal")
        k.cc_t_out = nc.dram_tensor("cc_t_out", [8 * T, D], F32, kind="Internal",
                                    addr_space="Shared")
        k.idt_in = nc.dram_tensor("idt_in", [IDN, 1], U32, kind="Internal")
        k.idt_out = nc.dram_tensor("idt_out", [8 * IDN, 1], U32, kind="Internal",
                                   addr_space="Shared")
        k.cc_y_in = nc.dram_tensor("cc_y_in", [T, D], F32, kind="Internal")
        k.cc_y_out = nc.dram_tensor("cc_y_out", [8 * T, D], F32, kind="Internal",
                                    addr_space="Shared")

    stack = contextlib.ExitStack()
    def sb(name, shape, dt=F32):
        tt_ = stack.enter_context(nc.sbuf_tensor(name, list(shape), dt))
        k.t[name] = tt_
        return tt_
    def psum(name, shape):
        tt_ = stack.enter_context(nc.psum_tensor(name, list(shape), F32))
        k.t[name] = tt_
        return tt_

    with stack:
        sb("x", (128, NCF * T))
        sb("y", (128, NCF * T))
        sb("big", (128, NCF * T))
        sb("stats", (128, T))
        sb("small", (128, 256))
        sb("wbuf", (128, 1024))
        sb("consts", (128, 652))
        sb("cid", (1, 1))
        if "l2f" in blocks:
            sb("u32a", (128, 64), dt=mybir.dt.uint32)
        psum("ps_a", (128, 4, 512))
        psum("ps_b", (128, 2, 512))
        psum("ps_c", (128, 2, 512))
        for s in Prog.SEMS:
            P.sems[s] = stack.enter_context(nc.semaphore("sem_" + s))

        with nc.Block() as block:
            emit_all(k)
            P.play(block)
    return k


def emit_all(k):
    P, t, hb = k.P, k.t, k.hb
    x = t["x"]
    ev_x = P.dma("sync", out=x[:].rearrange("p (c t) -> p c t", c=NCF),
                 in_=hb["x_t"].rearrange("(c p) t -> p c t", p=128))
    ev_cid = P.dma("sync", out=t["cid"][:], in_=hb["core_id"])
    ev_c = emit_consts(P, t)
    evs = [ev_x, ev_cid] + ev_c

    for b in k.blocks:
        evs = EMITTERS[b](k, evs)

    P.wait("gpsimd", evs)
    evo = P.dma("gpsimd", out=k.out_t.rearrange("(c p) t -> p c t", p=128),
                in_=x[:].rearrange("p (c t) -> p c t", c=NCF))
    P.wait("gpsimd", evo)
    for s in P.SEMS:
        if s.startswith("d") and P.cnt[s] > 0:
            def fw(eng, s=s, v=P.cnt[s]):
                eng.wait_ge(P.sems[s], v)
            P.queues["gpsimd"].append(fw)
    P.queues["gpsimd"].append(lambda eng: eng.wait_ge(P.sems["gpsimd"], P.cnt["gpsimd"]))


def emit_consts(P, t):
    c = t["consts"]
    e0 = P.op("gpsimd", lambda e: e.iota(c[:, 0:128], pattern=[[1, 128]], base=0,
                                         channel_multiplier=0, allow_small_or_imprecise_dtypes=True))
    e1 = P.op("gpsimd", lambda e: e.iota(c[:, 640:641], pattern=[[0, 1]], base=0,
                                         channel_multiplier=1, allow_small_or_imprecise_dtypes=True))
    e2 = P.op("gpsimd", lambda e: e.memset(c[:, 641:642], 1.0))
    e2b = P.op("gpsimd", lambda e: e.memset(c[:, 384:512], 1.0))
    P.wait("vector", [e0, e1, e2b])
    e3 = P.op("vector", lambda e: e.tensor_scalar(c[:, 128:256], c[:, 0:128], c[:, 640:641], None, ALU.is_ge))
    e4 = P.op("vector", lambda e: e.tensor_scalar(c[:, 256:384], c[:, 0:128], c[:, 640:641], None, ALU.is_gt))
    e4b = P.op("vector", lambda e: e.tensor_scalar(c[:, 512:640], c[:, 0:128], 64.0, None, ALU.is_ge))
    e4c = P.op("vector", lambda e: e.tensor_scalar(c[:, 642:643], c[:, 640:641], 64.0, None, ALU.is_ge))
    e4d = P.op("vector", lambda e: e.tensor_scalar(c[:, 512:640], c[:, 512:640], c[:, 642:643], None, ALU.is_equal))
    e5 = P.op("vector", lambda e: e.tensor_scalar(c[:, 0:128], c[:, 0:128], c[:, 640:641], None, ALU.is_equal))
    e6 = P.op("gpsimd", lambda e: e.iota(c[:, 643:651], pattern=[[1, 8]], base=0,
                                         channel_multiplier=0, allow_small_or_imprecise_dtypes=True))
    return [e2, e3, e4, e4d, e5, e6]



def bfence(k, evs):
    P = k.P
    return list(evs) + [P.fence(q) for q in ("vector", "scalar", "tensor", "gpsimd")]


def ps_wait_list(k, name, banks):
    out = []
    for b in banks:
        v = k.psfree.get((name, b))
        if v is None:
            continue
        out.extend(v if isinstance(v, list) else [v])
    return out


def ps_release(k, name, bank, ev):
    k.psfree[(name, bank)] = ev


def emit_ln(k, src_name, evs, dst_name="y", sq_scratch=None):
    P, t = k.P, k.t
    P.phase("ln")
    src, dst = t[src_name], t[dst_name]
    stats = t["stats"]
    pa, pb, pc = t["ps_a"], t["ps_b"], t["ps_c"]
    ao = ALLONES(t)
    sq = sq_scratch if sq_scratch is not None else xcol(t["big"], 0)

    def sq_bank(tt):
        return pb[:, tt, :] if tt < 2 else pc[:, tt - 2, :]

    P.wait("tensor", list(evs) + ps_wait_list(k, "ps_a", range(4))
           + ps_wait_list(k, "ps_b", range(2)) + ps_wait_list(k, "ps_c", range(2)))
    last_mm = None
    for c in range(NCF):
        e_sq = P.op("scalar", lambda e, c=c: e.activation(sq, xcol(src, c), AF.Square),
                    after=(evs if c == 0 else [last_mm]))
        for tt in range(NTT):
            P.op("tensor", lambda e, c=c, tt=tt: e.matmul(
                pa[:, tt, :], ao, xcol(src, c, tt * TTW, TTW),
                start=(c == 0), stop=(c == NCF - 1)), inc=False)
        P.wait("tensor", e_sq)
        for tt in range(NTT):
            ev = P.op("tensor", lambda e, tt=tt, c=c: e.matmul(
                sq_bank(tt), ao, sq[:, tt * TTW: tt * TTW + TTW],
                start=(c == 0), stop=(c == NCF - 1)), inc=(tt == NTT - 1))
        last_mm = ev
    P.wait("vector", last_mm)
    ev_nm = []
    for tt in range(NTT):
        ev = P.op("vector", lambda e, tt=tt: e.tensor_scalar(
            stats[:, tt * TTW:(tt + 1) * TTW], pa[:, tt, :], -1.0 / D, None, ALU.mult))
        ev_nm.append(ev)
        ps_release(k, "ps_a", tt, ev)
    add_evs = []
    for c in range(NCF):
        ev = P.op("vector", lambda e, c=c: e.tensor_tensor(xcol(dst, c), xcol(src, c), stats[:], ALU.add))
        add_evs.append(ev)
    ev_sq2 = P.op("vector", lambda e: e.tensor_tensor(stats[:], stats[:], stats[:], ALU.mult),
                  after=add_evs[-1])
    ev_var = []
    for tt in range(NTT):
        ev = P.op("vector", lambda e, tt=tt: e.scalar_tensor_tensor(
            stats[:, tt * TTW:(tt + 1) * TTW], sq_bank(tt), 1.0 / D,
            stats[:, tt * TTW:(tt + 1) * TTW], ALU.mult, ALU.subtract_rev)
            if False else e.scalar_tensor_tensor(
            stats[:, tt * TTW:(tt + 1) * TTW], sq_bank(tt), 1.0 / D,
            stats[:, tt * TTW:(tt + 1) * TTW], ALU.mult, ALU.subtract))
        ev_var.append(ev)
        ps_release(k, "ps_b" if tt < 2 else "ps_c", tt % 2, ev)
    ev_eps = P.op("vector", lambda e: e.tensor_scalar(stats[:], stats[:], 1e-5, None, ALU.add))
    P.wait("scalar", ev_eps)
    P.op("scalar", lambda e: e.activation(stats[:], stats[:], AF.Ln), inc=False)
    ev_r = P.op("scalar", lambda e: e.activation(stats[:], stats[:], AF.Exp, scale=-0.5))
    out_evs = []
    P.wait("vector", ev_r)
    for c in range(NCF):
        ev = P.op("vector", lambda e, c=c: e.tensor_tensor(xcol(dst, c), xcol(dst, c), stats[:], ALU.mult))
        out_evs.append(ev)
    P.phase("")
    return out_evs



def proj_fm(k, wname, src, src_base, HT, evs, writer, n_out=None, w_col_off=0,
            src_nchunks=NCF, psname="ps_a"):
    if src_base is None:
        src_base = 0
    """out[m-chunk, tokens HT] = sum_c W^T[c-chunk, m] @ src[c-chunk block of width HT].
    src: sbuf tensor with chunk c at cols [src_base + c*HT, +HT).
    writer(P, ps_ap, m, tt, ev_mm) -> ev  (must read psum and return completion ev).
    Returns list of writer events."""
    P, t, hb = k.P, k.t, k.hb
    wb = t["wbuf"]
    ps_ = t[psname]
    nbanks = 4 if psname == "ps_a" else 2
    ntt = max(1, HT // TTW)
    ttw = min(HT, TTW)
    wap = hb[wname]
    nks = src_nchunks
    nms = (n_out if n_out is not None else WSHAPES[wname][1]) // 128
    out_evs = []
    prev_use = [e_ for e_ in (getattr(k, "wbuf_free", None), getattr(k, "wbuf_free2", None))
                if e_ is not None] or None
    for m in range(nms):
        ev_w = P.dma("sync",
                     out=wb[:, 0:nks * 128].rearrange("p (kk mm) -> p kk mm", kk=nks),
                     in_=wap[:, w_col_off + m * 128: w_col_off + (m + 1) * 128]
                         .rearrange("(kk p) mm -> p kk mm", p=128),
                     after=prev_use)
        P.wait("tensor", [ev_w] + (list(evs) if m == 0 else []))
        mm_evs = []
        for tt in range(ntt):
            bank = (m * ntt + tt) % nbanks
            P.wait("tensor", ps_wait_list(k, psname, [bank]))
            for c in range(nks):
                ev_mm = P.op("tensor", lambda e, m=m, tt=tt, c=c, bank=bank, ttw=ttw: e.matmul(
                    ps_[:, bank, 0:ttw], wb[:, c * 128:(c + 1) * 128],
                    src[:, src_base + c * HT + tt * ttw: src_base + c * HT + tt * ttw + ttw],
                    start=(c == 0), stop=(c == nks - 1)),
                    inc=(c == nks - 1))
            mm_evs.append((tt, bank, ev_mm))
        prev_use = mm_evs[-1][2]
        for (tt, bank, ev_mm) in mm_evs:
            ev = writer(P, ps_[:, bank, 0:ttw], m, tt, ev_mm)
            ps_release(k, psname, bank, ev)
            out_evs.append(ev)
    k.wbuf_free = prev_use
    k.wbuf_free2 = prev_use
    return out_evs



def emit_l1f(k, evs):
    P, t, hb = k.P, k.t, k.hb
    x, y, big, small = t["x"], t["y"], t["big"], t["small"]
    evs = emit_ln(k, "x", evs, dst_name="y")
    _bf = bfence(k, evs)
    ev_mu = P.dma("sync", out=small[:, 16:24], in_=hb["l1f_mur"], after=_bf)
    ev_mk = P.dma("sync", out=small[:, 24:32], in_=hb["l1f_muk"])
    P.wait("vector", [ev_mu, ev_mk])
    ev_m2 = P.op("vector", lambda e: e.tensor_scalar(small[:, 32:48], small[:, 16:32], -1.0, 1.0, ALU.mult, ALU.add))
    HT = 512
    B = 8 * HT
    xr_b, xk_b, r_b, ka_b = 0, B, 2 * B, 3 * B
    x_evs = list(evs)
    for quart in range(4):
        t0 = quart * HT
        mu_r = lambda c: small[:, 16 + c:17 + c]
        mu_k = lambda c: small[:, 24 + c:25 + c]
        om_r = lambda c: small[:, 32 + c:33 + c]
        om_k = lambda c: small[:, 40 + c:41 + c]
        pre = []
        for c in range(NCF):
            for (base, mu, om) in ((xr_b, mu_r, om_r), (xk_b, mu_k, om_k)):
                col0 = lambda off, n, base=base, c=c: big[:, base + c * HT + off: base + c * HT + off + n]
                if quart == 0:
                    P.op("vector", lambda e, c=c, col0=col0, mu=mu: e.tensor_scalar(
                        col0(0, 1), xcol(y, c, 0, 1), mu(c), None, ALU.mult), inc=False,
                        after=(list(evs) + [ev_m2] if c == 0 and base == xr_b else None))
                else:
                    P.op("vector", lambda e, c=c, col0=col0, om=om, t0=t0: e.tensor_scalar(
                        col0(0, 1), xcol(y, c, t0 - 1, 1), om(c), None, ALU.mult), inc=False,
                        after=(x_evs if c == 0 and base == xr_b else None))
                    P.op("vector", lambda e, c=c, col0=col0, mu=mu, t0=t0: e.scalar_tensor_tensor(
                        col0(0, 1), xcol(y, c, t0, 1), mu(c), col0(0, 1), ALU.mult, ALU.add), inc=False)
                P.op("vector", lambda e, c=c, col0=col0, mu=mu, t0=t0: e.tensor_scalar(
                    col0(1, HT - 1), xcol(y, c, t0 + 1, HT - 1), mu(c), None, ALU.mult), inc=False)
                e2 = P.op("vector", lambda e, c=c, col0=col0, om=om, t0=t0: e.scalar_tensor_tensor(
                    col0(1, HT - 1), xcol(y, c, t0, HT - 1), om(c), col0(1, HT - 1), ALU.mult, ALU.add))
                pre.append(e2)
        def sigw(P_, ps_ap, m, tt, ev, r_b=r_b, HT=HT):
            return P_.op("scalar", lambda e: e.activation(
                big[:, r_b + m * HT: r_b + m * HT + HT], ps_ap, AF.Sigmoid), after=ev)
        r_evs = proj_fm(k, "l1f_wrT", big, xr_b, HT, pre, sigw)
        def kaw(P_, ps_ap, m, tt, ev, ka_b=ka_b, HT=HT):
            dst_ap = big[:, ka_b + m * HT: ka_b + m * HT + HT]
            e1 = P_.op("vector", lambda e: e.tensor_scalar(dst_ap, ps_ap, 0.0, None, ALU.max), after=ev)
            return P_.op("scalar", lambda e: e.activation(dst_ap, dst_ap, AF.Square), after=e1)
        ka_evs = proj_fm(k, "l1f_wkT", big, xk_b, HT, r_evs, kaw)
        import os
        dbg = os.environ.get("DBGL1F", "")
        if dbg:
            srcb = {"xr": xr_b, "xk": xk_b, "r": r_b, "ka": ka_b}[dbg]
            need = {"xr": pre, "xk": pre, "r": r_evs, "ka": ka_evs}[dbg]
            x_evs = []
            P.wait("vector", need)
            for c in range(NCF):
                x_evs.append(P.op("vector", lambda e, c=c, srcb=srcb, t0=t0: e.tensor_copy(
                    xcol(t["x"], c, t0, HT), big[:, srcb + c * HT: srcb + c * HT + HT])))
            evs = x_evs
            continue
        def finw(P_, ps_ap, m, tt, ev, r_b=r_b, HT=HT, t0=t0):
            rr = big[:, r_b + m * HT: r_b + m * HT + HT]
            e1 = P_.op("vector", lambda e: e.tensor_tensor(rr, rr, ps_ap, ALU.mult), after=ev)
            return P_.op("vector", lambda e, m=m: e.tensor_tensor(
                xcol(t["x"], m, t0, HT), xcol(t["x"], m, t0, HT), rr, ALU.add), after=e1)
        x_evs = proj_fm(k, "l1f_wvT", big, ka_b, HT, ka_evs, finw)
    return x_evs


def emit_l0f(k, evs):
    P, t, hb = k.P, k.t, k.hb
    x, y, big, small, stats = t["x"], t["y"], t["big"], t["small"], t["stats"]
    pa, pb, pc = t["ps_a"], t["ps_b"], t["ps_c"]
    evs = emit_ln(k, "x", evs, dst_name="y")
    evw = []
    evw.append(P.dma("sync", out=big[:, 4096:4864].rearrange("p (c r) -> p c r", c=8),
                     in_=hb["l0f_routedAT"].rearrange("(c p) r -> p c r", p=128),
                     after=getattr(k, "wbuf_free", None)))
    evw.append(P.dma("sync", out=big[:, 4864:4992].rearrange("p (c r) -> p c r", c=8),
                     in_=hb["l0f_sharedAT"].rearrange("(c p) r -> p c r", p=128)))
    evw.append(P.dma("sync", out=big[:, 4992:5040].rearrange("p (c r) -> p c r", c=8),
                     in_=hb["l0f_routerT"].rearrange("(c p) r -> p c r", p=128)))
    evw.append(P.dma("sync", out=big[0:16, 5120:6144], in_=hb["l0f_sharedBT"]))
    evw.append(P.dma("sync", out=big[0:96, 6144:7168], in_=hb["l0f_routedBT"]))
    evw.append(P.dma("sync", out=big[0:6, 7168:7264], in_=hb["l0f_blk16"]))
    evw.append(P.dma("sync", out=big[:, 7264:7270], in_=hb["l0f_routerb"]))
    evw.append(P.dma("sync", out=small[:, 48:56], in_=hb["l0f_baseb"], after=bfence(k, evs)))
    ev_ms1 = P.op("gpsimd", lambda e: e.memset(small[:, 64:72], 0.0))
    ev_ms2 = P.op("gpsimd", lambda e: e.memset(stats[:, 0:T], 0.0), after=evs)
    P.wait("vector", [ev_ms1, ev_ms2])
    w6_evs = []
    for tl in range(16):
        P.wait("tensor", evw + (evs if tl == 0 else []) + ps_wait_list(k, "ps_b", [0, 1]))
        for c in range(NCF):
            ev_mm = P.op("tensor", lambda e, tl=tl, c=c: e.matmul(
                pb[:, 0, 0:6], xcol(y, c, tl * 128, 128), big[:, 4992 + c * 6: 4998 + c * 6],
                start=(c == 0), stop=(c == NCF - 1)), inc=(c == NCF - 1))
        e1 = P.op("vector", lambda e: e.tensor_tensor(small[:, 64:70], pb[:, 0, 0:6], big[:, 7264:7270], ALU.add), after=ev_mm, drain=True)
        ps_release(k, "ps_b", 0, e1)
        e2 = P.op("vector", lambda e: e.reduce_max(small[:, 72:73], small[:, 64:70], axis=AX.X, negate=True), drain=True)
        P.wait("scalar", e2)
        e3 = P.op("scalar", lambda e: e.activation(small[:, 64:70], small[:, 64:70], AF.Exp,
                                                   bias=small[:, 72:73], accum_out=small[:, 73:74]), drain=True)
        P.wait("vector", e3)
        e4 = P.op("vector", lambda e: e.reciprocal(small[:, 72:73], small[:, 73:74]), drain=True)
        e5 = P.op("vector", lambda e: e.tensor_scalar(small[:, 64:70], small[:, 64:70], small[:, 72:73], None, ALU.mult), drain=True)
        P.op("vector", lambda e: e.reduce_max(small[:, 72:73], small[:, 64:70], axis=AX.X), inc=False, drain=True)
        P.op("vector", lambda e: e.tensor_scalar(small[:, 74:80], small[:, 64:70], small[:, 72:73], None, ALU.is_equal), inc=False, drain=True)
        P.op("vector", lambda e: e.scalar_tensor_tensor(small[:, 74:80], small[:, 74:80], -65536.0, IOTA8(t)[:, 0:6], ALU.mult, ALU.add), inc=False, drain=True)
        P.op("vector", lambda e: e.tensor_reduce(small[:, 73:74], small[:, 74:80], axis=AX.X, op=ALU.min), inc=False, drain=True)
        P.op("vector", lambda e: e.tensor_scalar(small[:, 73:74], small[:, 73:74], 65536.0, None, ALU.add), inc=False, drain=True)
        P.op("vector", lambda e: e.tensor_scalar(small[:, 74:80], IOTA8(t)[:, 0:6], small[:, 73:74], None, ALU.is_equal), inc=False, drain=True)
        P.op("vector", lambda e: e.tensor_copy(small[:, 82:83], small[:, 72:73]), inc=False, drain=True)
        P.op("vector", lambda e: e.scalar_tensor_tensor(small[:, 84:90], small[:, 74:80], -65536.0, small[:, 64:70], ALU.mult, ALU.add), inc=False, drain=True)
        P.op("vector", lambda e: e.reduce_max(small[:, 83:84], small[:, 84:90], axis=AX.X), inc=False, drain=True)
        P.op("vector", lambda e: e.tensor_tensor(small[:, 82:83], small[:, 82:83], small[:, 83:84], ALU.add), inc=False, drain=True)
        P.op("vector", lambda e: e.tensor_scalar(small[:, 82:83], small[:, 82:83], 1e-6, None, ALU.add), inc=False, drain=True)
        P.op("vector", lambda e: e.tensor_scalar(small[:, 90:96], small[:, 84:90], small[:, 83:84], None, ALU.is_equal), inc=False, drain=True)
        P.op("vector", lambda e: e.scalar_tensor_tensor(small[:, 90:96], small[:, 90:96], -65536.0, IOTA8(t)[:, 0:6], ALU.mult, ALU.add), inc=False, drain=True)
        P.op("vector", lambda e: e.tensor_reduce(small[:, 73:74], small[:, 90:96], axis=AX.X, op=ALU.min), inc=False, drain=True)
        P.op("vector", lambda e: e.tensor_scalar(small[:, 73:74], small[:, 73:74], 65536.0, None, ALU.add), inc=False, drain=True)
        P.op("vector", lambda e: e.tensor_scalar(small[:, 90:96], IOTA8(t)[:, 0:6], small[:, 73:74], None, ALU.is_equal), inc=False, drain=True)
        P.op("vector", lambda e: e.tensor_tensor(small[:, 74:80], small[:, 74:80], small[:, 90:96], ALU.add), inc=False, drain=True)
        P.op("vector", lambda e: e.tensor_tensor(small[:, 64:70], small[:, 64:70], small[:, 74:80], ALU.mult), inc=False, drain=True)
        P.op("vector", lambda e: e.reciprocal(small[:, 82:83], small[:, 82:83]), inc=False, drain=True)
        e6 = P.op("vector", lambda e: e.tensor_scalar(small[:, 64:72], small[:, 64:72], small[:, 82:83], None, ALU.mult), drain=True)
        import os as _os
        if _os.environ.get("DBGL0F") == "g":
            e6 = P.op("vector", lambda e: e.tensor_copy(small[:, 64:72], small[:, 64:72]))
        P.wait("tensor", [e6] + ps_wait_list(k, "ps_b", [1]))
        e7 = P.op("tensor", lambda e: e.transpose(pb[0:8, 1, 0:128], small[:, 64:72], IDENT(t)))
        e8 = P.op("vector", lambda e, tl=tl: e.tensor_copy(stats[0:6, tl * 128:(tl + 1) * 128], pb[0:6, 1, 0:128]), after=e7)
        ps_release(k, "ps_b", 1, e8)
        w6_evs.append(e8)
    import os
    if os.environ.get("DBGL0F") == "w6":
        x_evs = []
        P.wait("vector", w6_evs)
        x_evs.append(P.op("vector", lambda e: e.tensor_copy(xcol(x, 0)[0:6, :], stats[0:6, :])))
        return x_evs
    hall_evs = []
    for tt in range(NTT):
        P.wait("tensor", ps_wait_list(k, "ps_a", [tt % 4]))
        for c in range(NCF):
            ev_mm = P.op("tensor", lambda e, tt=tt, c=c: e.matmul(
                pa[0:96, tt, :], big[:, 4096 + c * 96: 4096 + (c + 1) * 96],
                xcol(y, c, tt * TTW, TTW), start=(c == 0), stop=(c == NCF - 1)),
                inc=(c == NCF - 1))
        ev = P.op("vector", lambda e, tt=tt: e.tensor_copy(
            big[0:96, tt * TTW:(tt + 1) * TTW], pa[0:96, tt, :]), after=ev_mm)
        ps_release(k, "ps_a", tt % 4, ev)
        hall_evs.append(ev)
    hs_evs = []
    for tt in range(NTT):
        P.wait("tensor", ps_wait_list(k, "ps_c", [tt % 2]))
        for c in range(NCF):
            ev_mm = P.op("tensor", lambda e, tt=tt, c=c: e.matmul(
                pc[0:16, tt % 2, :], big[:, 4864 + c * 16: 4864 + (c + 1) * 16],
                xcol(y, c, tt * TTW, TTW), start=(c == 0), stop=(c == NCF - 1)),
                inc=(c == NCF - 1))
        ev = P.op("vector", lambda e, tt=tt: e.tensor_copy(
            big[0:16, 2048 + tt * TTW: 2048 + (tt + 1) * TTW], pc[0:16, tt % 2, :]), after=ev_mm)
        ps_release(k, "ps_c", tt % 2, ev)
        hs_evs.append(ev)
    hw_evs = []
    for tt in range(NTT):
        P.wait("tensor", w6_evs + hall_evs + ps_wait_list(k, "ps_b", [0]))
        ev_mm = P.op("tensor", lambda e, tt=tt: e.matmul(
            pb[0:96, 0, :], big[0:6, 7168:7264], stats[0:6, tt * TTW:(tt + 1) * TTW],
            start=True, stop=True))
        ev = P.op("vector", lambda e, tt=tt: e.tensor_tensor(
            big[0:96, tt * TTW:(tt + 1) * TTW], big[0:96, tt * TTW:(tt + 1) * TTW],
            pb[0:96, 0, :], ALU.mult), after=ev_mm)
        ps_release(k, "ps_b", 0, ev)
        hw_evs.append(ev)
    wb = t["wbuf"]
    x_evs = []
    prev_use = getattr(k, "wbuf_free", None)
    for m in range(8):
        ev_w = P.dma("sync", out=wb[:, 0:1024].rearrange("p (kk mm) -> p kk mm", kk=8),
                     in_=hb["l0f_baseT"][:, m * 128:(m + 1) * 128].rearrange("(kk p) mm -> p kk mm", p=128),
                     after=prev_use)
        P.wait("tensor", [ev_w] + (hs_evs + hw_evs if m == 0 else []))
        for tt in range(NTT):
            bank = tt
            P.wait("tensor", ps_wait_list(k, "ps_a", [bank]))
            for c in range(NCF):
                P.op("tensor", lambda e, tt=tt, c=c, bank=bank: e.matmul(
                    pa[:, bank, :], wb[:, c * 128:(c + 1) * 128],
                    xcol(y, c, tt * TTW, TTW), start=(c == 0), stop=False), inc=False)
            P.op("tensor", lambda e, tt=tt, m=m, bank=bank: e.matmul(
                pa[:, bank, :], big[0:16, 5120 + m * 128: 5120 + (m + 1) * 128],
                big[0:16, 2048 + tt * TTW: 2048 + (tt + 1) * TTW], start=False, stop=False), inc=False)
            ev_mm = P.op("tensor", lambda e, tt=tt, m=m, bank=bank: e.matmul(
                pa[:, bank, :], big[0:96, 6144 + m * 128: 6144 + (m + 1) * 128],
                big[0:96, tt * TTW:(tt + 1) * TTW], start=False, stop=True))
            ev = P.op("vector", lambda e, tt=tt, m=m, bank=bank: e.scalar_tensor_tensor(
                xcol(x, m, tt * TTW, TTW), pa[:, bank, :], small[:, 48 + m:49 + m],
                xcol(x, m, tt * TTW, TTW), ALU.add, ALU.add), after=ev_mm)
            ps_release(k, "ps_a", bank, ev)
            x_evs.append(ev)
        prev_use = ev_mm
    k.wbuf_free = prev_use
    return x_evs


def emit_l2a(k, evs):
    P, t, hb = k.P, k.t, k.hb
    x, y, big, small = t["x"], t["y"], t["big"], t["small"]
    pa, pb, pc = t["ps_a"], t["ps_b"], t["ps_c"]
    wb = t["wbuf"]
    evs = emit_ln(k, "x", evs, dst_name="y")
    ev_b = [P.dma("sync", out=small[:, 16:24], in_=hb["l2a_bq"], after=bfence(k, evs)),
            P.dma("sync", out=small[:, 24:32], in_=hb["l2a_bk"]),
            P.dma("sync", out=small[:, 32:40], in_=hb["l2a_bo"])]
    ev_b.append(P.dma("sync", out=big[0:1, 15360:16384], in_=hb["l2a_bv"]))
    Q2, K2, V2, PB, O2, PT = 0, 2048, 4096, 6144, 8192, 10240
    x_evs = list(evs)
    prev_wb = getattr(k, "wbuf_free", None)
    for g in range(8):
        proj_evs = []
        for (wn, dstb, bc) in (("l2a_wq", Q2, 16), ("l2a_wk", K2, 24)):
            ev_w = P.dma("sync", out=wb[:, 0:1024].rearrange("p (kk mm) -> p kk mm", kk=8),
                         in_=hb[wn][:, g * 128:(g + 1) * 128].rearrange("(kk p) mm -> p kk mm", p=128),
                         after=prev_wb)
            P.wait("tensor", [ev_w] + (x_evs + ev_b if g == 0 else []))
            for tt in range(NTT):
                bank = tt
                P.wait("tensor", ps_wait_list(k, "ps_a", [bank]))
                for c in range(NCF):
                    ev_mm = P.op("tensor", lambda e, tt=tt, c=c, bank=bank: e.matmul(
                        pa[:, bank, :], wb[:, c * 128:(c + 1) * 128],
                        xcol(y, c, tt * TTW, TTW), start=(c == 0), stop=(c == NCF - 1)),
                        inc=(c == NCF - 1))
                ev = P.op("vector", lambda e, tt=tt, dstb=dstb, bank=bank, bc=bc, g=g: e.tensor_scalar(
                    big[:, dstb + tt * TTW: dstb + (tt + 1) * TTW], pa[:, bank, :],
                    small[:, bc + g: bc + g + 1], None, ALU.add), after=ev_mm)
                ps_release(k, "ps_a", bank, ev)
                proj_evs.append(ev)
            prev_wb = ev_mm
        ev_w = P.dma("sync", out=wb[:, 0:1024].rearrange("p (kk mm) -> p kk mm", kk=8),
                     in_=hb["l2a_wv"][:, g * 128:(g + 1) * 128].rearrange("(kk p) mm -> p kk mm", p=128),
                     after=prev_wb)
        P.wait("tensor", [ev_w])
        for tl in range(16):
            bank = tl % 4
            P.wait("tensor", ps_wait_list(k, "ps_a", [bank]))
            for c in range(NCF):
                P.op("tensor", lambda e, tl=tl, c=c, bank=bank: e.matmul(
                    pa[:, bank, 0:128], xcol(y, c, tl * 128, 128), wb[:, c * 128:(c + 1) * 128],
                    start=(c == 0), stop=False), inc=False)
            ev_mm = P.op("tensor", lambda e, bank=bank, g=g: e.matmul(
                pa[:, bank, 0:128], ALLONES(t)[0:1, 0:128], big[0:1, 15360 + g * 128: 15360 + (g + 1) * 128],
                start=False, stop=True))
            ev = P.op("vector", lambda e, tl=tl, bank=bank: e.tensor_copy(
                big[:, V2 + tl * 128: V2 + (tl + 1) * 128], pa[:, bank, 0:128]), after=ev_mm)
            ps_release(k, "ps_a", bank, ev)
            proj_evs.append(ev)
        prev_wb = ev_mm
        for hh in range(2):
            qb = hh * 64
            for qi in range(16):
                ns = qi + 1
                nsb = (ns + 3) // 4
                P.wait("tensor", proj_evs if (hh == 0 and qi == 0) else None)
                mm_evs = []
                for sb in range(nsb):
                    w_ = min(512, ns * 128 - sb * 512)
                    bank = sb
                    P.wait("tensor", ps_wait_list(k, "ps_a", [bank]))
                    ev_mm = P.op("tensor", lambda e, sb=sb, w_=w_, qi=qi, qb=qb, bank=bank: e.matmul(
                        pa[:, bank, 0:w_], big[qb:qb + 64, Q2 + qi * 128: Q2 + (qi + 1) * 128],
                        big[qb:qb + 64, K2 + sb * 512: K2 + sb * 512 + w_], start=True, stop=True))
                    mm_evs.append((sb, w_, bank, ev_mm))
                pevs = []
                for (sb, w_, bank, ev_mm) in mm_evs:
                    if (sb + 1) * 512 >= ns * 128:
                        dw = w_ - 128
                        if dw > 0:
                            P.op("vector", lambda e, sb=sb, dw=dw, bank=bank: e.tensor_copy(
                                big[:, PB + sb * 512: PB + sb * 512 + dw], pa[:, bank, 0:dw]),
                                after=ev_mm, inc=False)
                        ev = P.op("vector", lambda e, sb=sb, dw=dw, w_=w_, bank=bank: e.scalar_tensor_tensor(
                            big[:, PB + sb * 512 + dw: PB + sb * 512 + w_], MASK_GT(t), -30000.0,
                            pa[:, bank, dw:w_], ALU.mult, ALU.add), after=ev_mm)
                    else:
                        ev = P.op("vector", lambda e, sb=sb, w_=w_, bank=bank: e.tensor_copy(
                            big[:, PB + sb * 512: PB + sb * 512 + w_], pa[:, bank, 0:w_]), after=ev_mm)
                    ps_release(k, "ps_a", bank, ev)
                    pevs.append(ev)
                W = ns * 128
                P.wait("vector", pevs)
                P.op("vector", lambda e, W=W: e.reduce_max(small[:, 64:65], big[:, PB:PB + W], axis=AX.X),
                     inc=False, drain=True)
                ev_m = P.op("vector", lambda e: e.tensor_scalar(small[:, 65:66], small[:, 64:65], -0.125, None, ALU.mult), drain=True)
                P.wait("scalar", ev_m)
                ev_e = P.op("scalar", lambda e, W=W: e.activation(big[:, PB:PB + W], big[:, PB:PB + W], AF.Exp,
                                                                  bias=small[:, 65:66], scale=0.125,
                                                                  accum_out=small[:, 66:67]))
                P.wait("vector", ev_e)
                ev_r = P.op("vector", lambda e: e.reciprocal(small[:, 67:68], small[:, 66:67]), drain=True)
                ev_p = P.op("vector", lambda e, W=W: e.tensor_scalar(big[:, PB:PB + W], big[:, PB:PB + W],
                                                                     small[:, 67:68], None, ALU.mult), drain=True)
                P.wait("tensor", ev_p)
                ev_o = None
                for sb in range(ns):
                    P.wait("tensor", ps_wait_list(k, "ps_b", [0]))
                    ev_t = P.op("tensor", lambda e, sb=sb: e.transpose(
                        pb[:, 0, 0:128], big[:, PB + sb * 128: PB + (sb + 1) * 128], IDENT(t)))
                    ev_c = P.op("vector", lambda e: e.tensor_copy(big[:, PT:PT + 128], pb[:, 0, 0:128]),
                                after=ev_t)
                    ps_release(k, "ps_b", 0, ev_c)
                    P.wait("tensor", ev_c if sb > 0 else [ev_c] + ps_wait_list(k, "ps_b", [1]))
                    ev_o = P.op("tensor", lambda e, sb=sb, qb=qb, ns=ns: e.matmul(
                        pb[qb:qb + 64, 1, 0:128],
                        big[:, V2 + sb * 128 + qb: V2 + sb * 128 + qb + 64],
                        big[:, PT:PT + 128], start=(sb == 0), stop=(sb == ns - 1)))
                ev_oe = P.op("vector", lambda e, qb=qb, qi=qi: e.tensor_copy(
                    big[qb:qb + 64, O2 + qi * 128: O2 + (qi + 1) * 128], pb[qb:qb + 64, 1, 0:128]),
                    after=ev_o)
                ps_release(k, "ps_b", 1, ev_oe)
                x_evs.append(ev_oe)
        ev_w = P.dma("sync", out=wb[:, 0:1024].rearrange("p (kk mm) -> p kk mm", kk=8),
                     in_=hb["l2a_wo"][g * 128:(g + 1) * 128, :].rearrange("p (kk mm) -> p kk mm", kk=8),
                     after=prev_wb)
        P.wait("tensor", [ev_w] + x_evs[-32:])
        gevs = []
        for m in range(8):
            for tt in range(NTT):
                bank = tt
                P.wait("tensor", ps_wait_list(k, "ps_c", [bank % 2]))
                ev_mm = P.op("tensor", lambda e, m=m, tt=tt, bank=bank: e.matmul(
                    pc[:, bank % 2, :], wb[:, m * 128:(m + 1) * 128],
                    big[:, O2 + tt * TTW: O2 + (tt + 1) * TTW], start=True, stop=True))
                if g == 7:
                    ev = P.op("vector", lambda e, m=m, tt=tt, bank=bank: e.scalar_tensor_tensor(
                        xcol(x, m, tt * TTW, TTW), pc[:, bank % 2, :], small[:, 32 + m:33 + m],
                        xcol(x, m, tt * TTW, TTW), ALU.add, ALU.add), after=ev_mm)
                else:
                    ev = P.op("vector", lambda e, m=m, tt=tt, bank=bank: e.tensor_tensor(
                        xcol(x, m, tt * TTW, TTW), xcol(x, m, tt * TTW, TTW), pc[:, bank % 2, :],
                        ALU.add), after=ev_mm)
                ps_release(k, "ps_c", bank % 2, ev)
                gevs.append(ev)
        prev_wb = ev_mm
        x_evs = gevs
    k.wbuf_free = prev_wb
    return x_evs


def emit_l3a(k, evs):
    P, t, hb = k.P, k.t, k.hb
    x, y, big, small = t["x"], t["y"], t["big"], t["small"]
    pa, pb, pc = t["ps_a"], t["ps_b"], t["ps_c"]
    wb = t["wbuf"]
    evs = emit_ln(k, "x", evs, dst_name="y")
    Q2, KB, VB, PB, O2, PT = 0, 2048, 6144, 10240, 12288, 14336
    prev_wb = getattr(k, "wbuf_free", None)
    kv_evs = []
    for m in range(2):
        ev_w = P.dma("sync", out=wb[:, 0:1024].rearrange("p (kk mm) -> p kk mm", kk=8),
                     in_=hb["l3a_wkT"][:, m * 128:(m + 1) * 128].rearrange("(kk p) mm -> p kk mm", p=128),
                     after=prev_wb)
        P.wait("tensor", [ev_w] + (list(evs) if m == 0 else []))
        for tt in range(NTT):
            bank = tt
            P.wait("tensor", ps_wait_list(k, "ps_a", [bank]))
            for c in range(NCF):
                ev_mm = P.op("tensor", lambda e, tt=tt, c=c, bank=bank: e.matmul(
                    pa[:, bank, :], wb[:, c * 128:(c + 1) * 128],
                    xcol(y, c, tt * TTW, TTW), start=(c == 0), stop=(c == NCF - 1)),
                    inc=(c == NCF - 1))
            ev = P.op("vector", lambda e, m=m, tt=tt, bank=bank: e.tensor_copy(
                big[:, KB + m * 2048 + tt * TTW: KB + m * 2048 + (tt + 1) * TTW], pa[:, bank, :]),
                after=ev_mm)
            ps_release(k, "ps_a", bank, ev)
            kv_evs.append(ev)
        prev_wb = ev_mm
    for vh in range(2):
        ev_w = P.dma("sync", out=wb[:, 0:1024].rearrange("p (kk mm) -> p kk mm", kk=8),
                     in_=hb["l3a_wvT"][:, vh * 128:(vh + 1) * 128].rearrange("(kk p) mm -> p kk mm", p=128),
                     after=prev_wb)
        P.wait("tensor", [ev_w])
        for tl in range(16):
            bank = tl % 4
            P.wait("tensor", ps_wait_list(k, "ps_a", [bank]))
            for c in range(NCF):
                ev_mm = P.op("tensor", lambda e, tl=tl, c=c, bank=bank: e.matmul(
                    pa[:, bank, 0:128], xcol(y, c, tl * 128, 128), wb[:, c * 128:(c + 1) * 128],
                    start=(c == 0), stop=(c == NCF - 1)), inc=(c == NCF - 1))
            ev = P.op("vector", lambda e, tl=tl, vh=vh, bank=bank: e.tensor_copy(
                big[:, VB + tl * 256 + vh * 128: VB + tl * 256 + vh * 128 + 128], pa[:, bank, 0:128]),
                after=ev_mm)
            ps_release(k, "ps_a", bank, ev)
            kv_evs.append(ev)
        prev_wb = ev_mm
    x_evs = []
    for g in range(8):
        proj_evs = []
        ev_w = P.dma("sync", out=wb[:, 0:1024].rearrange("p (kk mm) -> p kk mm", kk=8),
                     in_=hb["l3a_wqT"][:, g * 128:(g + 1) * 128].rearrange("(kk p) mm -> p kk mm", p=128),
                     after=prev_wb)
        P.wait("tensor", [ev_w] + (kv_evs if g == 0 else []))
        for tt in range(NTT):
            bank = tt
            P.wait("tensor", ps_wait_list(k, "ps_a", [bank]))
            for c in range(NCF):
                ev_mm = P.op("tensor", lambda e, tt=tt, c=c, bank=bank: e.matmul(
                    pa[:, bank, :], wb[:, c * 128:(c + 1) * 128],
                    xcol(y, c, tt * TTW, TTW), start=(c == 0), stop=(c == NCF - 1)),
                    inc=(c == NCF - 1))
            ev = P.op("vector", lambda e, tt=tt, bank=bank: e.tensor_copy(
                big[:, Q2 + tt * TTW: Q2 + (tt + 1) * TTW], pa[:, bank, :]), after=ev_mm)
            ps_release(k, "ps_a", bank, ev)
            proj_evs.append(ev)
        prev_wb = ev_mm
        PERM = [0, 4, 1, 5, 2, 6, 3, 7, 8, 12, 9, 13, 10, 14, 11, 15]
        for hh in range(2):
            qb = hh * 64
            h = PERM[g * 2 + hh]
            n = h // 4
            kbase = KB + (n // 2) * 2048
            krow = (n % 2) * 64
            assert krow == qb, (g, hh, h, n)
            for qi in range(16):
                ns = qi + 1
                nsb = (ns + 3) // 4
                P.wait("tensor", proj_evs if (hh == 0 and qi == 0) else None)
                mm_evs = []
                for sb in range(nsb):
                    w_ = min(512, ns * 128 - sb * 512)
                    bank = sb
                    P.wait("tensor", ps_wait_list(k, "ps_a", [bank]))
                    ev_mm = P.op("tensor", lambda e, sb=sb, w_=w_, qi=qi, qb=qb, bank=bank, kbase=kbase, krow=krow: e.matmul(
                        pa[:, bank, 0:w_], big[qb:qb + 64, Q2 + qi * 128: Q2 + (qi + 1) * 128],
                        big[krow:krow + 64, kbase + sb * 512: kbase + sb * 512 + w_], start=True, stop=True))
                    mm_evs.append((sb, w_, bank, ev_mm))
                pevs = []
                for (sb, w_, bank, ev_mm) in mm_evs:
                    if (sb + 1) * 512 >= ns * 128:
                        dw = w_ - 128
                        if dw > 0:
                            P.op("vector", lambda e, sb=sb, dw=dw, bank=bank: e.tensor_copy(
                                big[:, PB + sb * 512: PB + sb * 512 + dw], pa[:, bank, 0:dw]),
                                after=ev_mm, inc=False)
                        ev = P.op("vector", lambda e, sb=sb, dw=dw, w_=w_, bank=bank: e.scalar_tensor_tensor(
                            big[:, PB + sb * 512 + dw: PB + sb * 512 + w_], MASK_GT(t), -30000.0,
                            pa[:, bank, dw:w_], ALU.mult, ALU.add), after=ev_mm)
                    else:
                        ev = P.op("vector", lambda e, sb=sb, w_=w_, bank=bank: e.tensor_copy(
                            big[:, PB + sb * 512: PB + sb * 512 + w_], pa[:, bank, 0:w_]), after=ev_mm)
                    ps_release(k, "ps_a", bank, ev)
                    pevs.append(ev)
                W = ns * 128
                P.wait("vector", pevs)
                P.op("vector", lambda e, W=W: e.reduce_max(small[:, 64:65], big[:, PB:PB + W], axis=AX.X),
                     inc=False, drain=True)
                ev_m = P.op("vector", lambda e: e.tensor_scalar(small[:, 65:66], small[:, 64:65], -0.125, None, ALU.mult), drain=True)
                P.wait("scalar", ev_m)
                ev_e = P.op("scalar", lambda e, W=W: e.activation(big[:, PB:PB + W], big[:, PB:PB + W], AF.Exp,
                                                                  bias=small[:, 65:66], scale=0.125,
                                                                  accum_out=small[:, 66:67]))
                P.wait("vector", ev_e)
                ev_r = P.op("vector", lambda e: e.reciprocal(small[:, 67:68], small[:, 66:67]), drain=True)
                ev_p = P.op("vector", lambda e, W=W: e.tensor_scalar(big[:, PB:PB + W], big[:, PB:PB + W],
                                                                     small[:, 67:68], None, ALU.mult), drain=True)
                P.wait("tensor", ev_p)
                ev_o = None
                for sb in range(ns):
                    P.wait("tensor", ps_wait_list(k, "ps_b", [0]))
                    ev_t = P.op("tensor", lambda e, sb=sb: e.transpose(
                        pb[:, 0, 0:128], big[:, PB + sb * 128: PB + (sb + 1) * 128], IDENT(t)))
                    ev_c = P.op("vector", lambda e: e.tensor_copy(big[:, PT:PT + 128], pb[:, 0, 0:128]),
                                after=ev_t)
                    ps_release(k, "ps_b", 0, ev_c)
                    P.wait("tensor", ev_c if sb > 0 else [ev_c] + ps_wait_list(k, "ps_b", [1]))
                    ev_o = P.op("tensor", lambda e, sb=sb, qb=qb, ns=ns, n=n: e.matmul(
                        pb[qb:qb + 64, 1, 0:128],
                        big[:, VB + sb * 256 + n * 64: VB + sb * 256 + n * 64 + 64],
                        big[:, PT:PT + 128], start=(sb == 0), stop=(sb == ns - 1)))
                ev_oe = P.op("vector", lambda e, qb=qb, qi=qi: e.tensor_copy(
                    big[qb:qb + 64, O2 + qi * 128: O2 + (qi + 1) * 128], pb[qb:qb + 64, 1, 0:128]),
                    after=ev_o)
                ps_release(k, "ps_b", 1, ev_oe)
                x_evs.append(ev_oe)
        ev_w = P.dma("sync", out=wb[:, 0:1024].rearrange("p (kk mm) -> p kk mm", kk=8),
                     in_=hb["l3a_woT"][g * 128:(g + 1) * 128, :].rearrange("p (kk mm) -> p kk mm", kk=8),
                     after=prev_wb)
        P.wait("tensor", [ev_w] + x_evs[-32:])
        gevs = []
        for m in range(8):
            for tt in range(NTT):
                bank = tt % 2
                P.wait("tensor", ps_wait_list(k, "ps_c", [bank]))
                ev_mm = P.op("tensor", lambda e, m=m, tt=tt, bank=bank: e.matmul(
                    pc[:, bank, :], wb[:, m * 128:(m + 1) * 128],
                    big[:, O2 + tt * TTW: O2 + (tt + 1) * TTW], start=True, stop=True))
                ev = P.op("vector", lambda e, m=m, tt=tt, bank=bank: e.tensor_tensor(
                    xcol(x, m, tt * TTW, TTW), xcol(x, m, tt * TTW, TTW), pc[:, bank, :],
                    ALU.add), after=ev_mm)
                ps_release(k, "ps_c", bank, ev)
                gevs.append(ev)
        prev_wb = ev_mm
        x_evs = gevs
    k.wbuf_free = prev_wb
    return x_evs


def emit_ln_range(k, src_name, t0, evs, ydst, ybase, sq_off=512, st_off=0):
    P, t = k.P, k.t
    src = t[src_name]
    stats = t["stats"]
    pa, pb = t["ps_a"], t["ps_b"]
    ao = ALLONES(t)
    W = 512
    sq = stats[:, sq_off:sq_off + W]
    stt = stats[:, st_off:st_off + W]
    P.wait("tensor", list(evs) + ps_wait_list(k, "ps_a", [0]) + ps_wait_list(k, "ps_b", [0]))
    last_mm = None
    for c in range(NCF):
        e_sq = P.op("scalar", lambda e, c=c: e.activation(sq, xcol(src, c, t0, W), AF.Square),
                    after=(evs if c == 0 else [last_mm]))
        P.op("tensor", lambda e, c=c: e.matmul(pa[:, 0, :], ao, xcol(src, c, t0, W),
                                               start=(c == 0), stop=(c == NCF - 1)), inc=False)
        P.wait("tensor", e_sq)
        last_mm = P.op("tensor", lambda e, c=c: e.matmul(pb[:, 0, :], ao, sq,
                                                         start=(c == 0), stop=(c == NCF - 1)))
    P.wait("vector", last_mm)
    ev_nm = P.op("vector", lambda e: e.tensor_scalar(stt, pa[:, 0, :], -1.0 / D, None, ALU.mult))
    ps_release(k, "ps_a", 0, ev_nm)
    add_evs = []
    for c in range(NCF):
        ev = P.op("vector", lambda e, c=c: e.tensor_tensor(
            ydst[:, ybase + c * 512: ybase + (c + 1) * 512], xcol(src, c, t0, W), stt, ALU.add))
        add_evs.append(ev)
    P.op("vector", lambda e: e.tensor_tensor(stt, stt, stt, ALU.mult), inc=False)
    ev_var = P.op("vector", lambda e: e.scalar_tensor_tensor(stt, pb[:, 0, :], 1.0 / D, stt, ALU.mult, ALU.subtract))
    ps_release(k, "ps_b", 0, ev_var)
    ev_eps = P.op("vector", lambda e: e.tensor_scalar(stt, stt, 1e-5, None, ALU.add))
    P.wait("scalar", ev_eps)
    P.op("scalar", lambda e: e.activation(stt, stt, AF.Ln), inc=False)
    ev_r = P.op("scalar", lambda e: e.activation(stt, stt, AF.Exp, scale=-0.5))
    out_evs = []
    P.wait("vector", ev_r)
    for c in range(NCF):
        ev = P.op("vector", lambda e, c=c: e.tensor_tensor(
            ydst[:, ybase + c * 512: ybase + (c + 1) * 512],
            ydst[:, ybase + c * 512: ybase + (c + 1) * 512], stt, ALU.mult))
        out_evs.append(ev)
    return out_evs


def _stream_mm(k, wname, col0, nk_chunks, rhs_fn, psum_ap, extra_after=None):
    P, t, hb = k.P, k.t, k.hb
    wb = t["wbuf"]
    nh = nk_chunks // 4
    ev_mm = None
    hfree = [getattr(k, "wbuf_free", None), getattr(k, "wbuf_free2", None)]
    for s in range(nh):
        h_ = s % 2
        ev_w = P.dma("sync",
                     out=wb[:, h_ * 512:(h_ + 1) * 512].rearrange("p (kk mm) -> p kk mm", kk=4),
                     in_=hb[wname][s * 512:(s + 1) * 512, col0:col0 + 128]
                         .rearrange("(kk p) mm -> p kk mm", p=128),
                     after=hfree[h_])
        P.wait("tensor", [ev_w] + (extra_after if (s == 0 and extra_after) else []))
        for kk in range(4):
            kc = s * 4 + kk
            ev_mm = P.op("tensor", lambda e, h_=h_, kk=kk, kc=kc: e.matmul(
                psum_ap, wb[:, h_ * 512 + kk * 128: h_ * 512 + (kk + 1) * 128], rhs_fn(kc),
                start=(kc == 0), stop=(kc == nk_chunks - 1)),
                inc=(kk == 3))
        hfree[h_] = ev_mm
    k.wbuf_free = hfree[0]
    k.wbuf_free2 = hfree[1]
    return ev_mm


def emit_l3f(k, evs):
    P, t, hb = k.P, k.t, k.hb
    x, y, big, small, stats = t["x"], t["y"], t["big"], t["small"], t["stats"]
    pa, pb, pc = t["ps_a"], t["ps_b"], t["ps_c"]
    ev_s = [P.dma("sync", out=small[:, 100:132], in_=hb["l3f_s1"], after=bfence(k, evs)),
            P.dma("sync", out=small[:, 132:164], in_=hb["l3f_s2"]),
            P.dma("sync", out=small[:, 164:196], in_=hb["l3f_sh1"]),
            P.dma("sync", out=small[:, 196:228], in_=hb["l3f_sh2"]),
            P.dma("sync", out=small[:, 228:236], in_=hb["l3f_s3"])]
    P.wait("tensor", ev_s)
    x_evs = list(evs)
    for q in range(4):
        t0 = q * 512
        ln_evs = emit_ln_range(k, "x", t0, x_evs, y, 0)
        h_evs = []
        for hc in range(32):
            bank_t = hc % 2
            P.wait("tensor", ps_wait_list(k, "ps_c", [bank_t]))
            ev1 = _stream_mm(k, "l3f_w1T", hc * 128, 8,
                             lambda kc: y[:, kc * 512:(kc + 1) * 512],
                             pc[:, bank_t, :], extra_after=ln_evs if hc == 0 else None)
            ev_t1 = P.op("scalar", lambda e, hc=hc, bank_t=bank_t: e.activation(
                stats[:, 1024:1536], pc[:, bank_t, :], AF.Silu, scale=small[:, 100 + hc:101 + hc]),
                after=ev1)
            ps_release(k, "ps_c", bank_t, ev_t1)
            bank2 = 2 + hc % 2
            P.wait("tensor", ps_wait_list(k, "ps_a", [bank2]))
            ev2 = _stream_mm(k, "l3f_w2T", hc * 128, 8,
                             lambda kc: y[:, kc * 512:(kc + 1) * 512],
                             pa[:, bank2, :])
            ev_h = P.op("vector", lambda e, hc=hc, bank2=bank2: e.scalar_tensor_tensor(
                big[:, hc * 512:(hc + 1) * 512], pa[:, bank2, :], small[:, 132 + hc:133 + hc],
                stats[:, 1024:1536], ALU.mult, ALU.mult), after=[ev2, ev_t1])
            ps_release(k, "ps_a", bank2, ev_h)
            h_evs.append(ev_h)
        h1_evs = []
        for oc in range(32):
            bank = oc % 2
            P.wait("tensor", ps_wait_list(k, "ps_a", [bank]))
            ev1 = _stream_mm(k, "l3f_h1T", oc * 128, 32,
                             lambda kc: big[:, kc * 512:(kc + 1) * 512],
                             pa[:, bank, :], extra_after=h_evs if oc == 0 else None)
            ev = P.op("scalar", lambda e, oc=oc, bank=bank: e.activation(
                y[:, oc * 512:(oc + 1) * 512], pa[:, bank, :], AF.Gelu,
                scale=small[:, 164 + oc:165 + oc]), after=ev1)
            ps_release(k, "ps_a", bank, ev)
            h1_evs.append(ev)
        h2_evs = []
        for oc in range(32):
            bank = oc % 2
            P.wait("tensor", ps_wait_list(k, "ps_a", [bank]))
            ev1 = _stream_mm(k, "l3f_h2T", oc * 128, 32,
                             lambda kc: y[:, kc * 512:(kc + 1) * 512],
                             pa[:, bank, :], extra_after=h1_evs if oc == 0 else None)
            ev = P.op("scalar", lambda e, oc=oc, bank=bank: e.activation(
                big[:, oc * 512:(oc + 1) * 512], pa[:, bank, :], AF.Gelu,
                scale=small[:, 196 + oc:197 + oc]), after=ev1)
            ps_release(k, "ps_a", bank, ev)
            h2_evs.append(ev)
        x_evs = []
        for m in range(8):
            bank = m % 2
            P.wait("tensor", ps_wait_list(k, "ps_a", [bank]))
            ev1 = _stream_mm(k, "l3f_w3T", m * 128, 32,
                             lambda kc: big[:, kc * 512:(kc + 1) * 512],
                             pa[:, bank, :], extra_after=h2_evs if m == 0 else None)
            ev = P.op("vector", lambda e, m=m, t0=t0, bank=bank: e.scalar_tensor_tensor(
                xcol(x, m, t0, 512), pa[:, bank, :], small[:, 228 + m:229 + m],
                xcol(x, m, t0, 512), ALU.mult, ALU.add), after=ev1)
            ps_release(k, "ps_a", bank, ev)
            x_evs.append(ev)
    return x_evs


def emit_l0a(k, evs):
    P, t, hb = k.P, k.t, k.hb
    x, y, big, small, stats = t["x"], t["y"], t["big"], t["big"], t["stats"]
    big = t["big"]
    pa, pb, pc = t["ps_a"], t["ps_b"], t["ps_c"]
    wb = t["wbuf"]
    evs = emit_ln(k, "x", evs, dst_name="y")
    QG, KE, VT, KN = 0, 4096, 8192, 12288
    prev_wb = getattr(k, "wbuf_free", None)
    x_evs = list(evs)
    import os as _os
    NG = int(_os.environ.get("NGROUPS", "4"))
    for g in range(NG):
        P.phase("proj")
        proj_evs = []
        for part, (colbase, dstbase, act) in enumerate(
                ((0 + g * 256, QG, "expq"), (1024 + g * 256, KE, "expk"))):
            for m in range(2):
                ev_w = P.dma("sync", out=wb[:, 0:1024].rearrange("p (kk mm) -> p kk mm", kk=8),
                             in_=hb["l0a_wT"][:, colbase + m * 128: colbase + (m + 1) * 128]
                                 .rearrange("(kk p) mm -> p kk mm", p=128),
                             after=prev_wb)
                P.wait("tensor", [ev_w] + (x_evs if g == 0 and part == 0 and m == 0 else []))
                for tt in range(NTT):
                    bank = tt
                    P.wait("tensor", ps_wait_list(k, "ps_a", [bank]))
                    for c in range(NCF):
                        ev_mm = P.op("tensor", lambda e, tt=tt, c=c, bank=bank: e.matmul(
                            pa[:, bank, :], wb[:, c * 128:(c + 1) * 128],
                            xcol(y, c, tt * TTW, TTW), start=(c == 0), stop=(c == NCF - 1)),
                            inc=(c == NCF - 1))
                    dst = big[:, dstbase + m * 2048 + tt * TTW: dstbase + m * 2048 + (tt + 1) * TTW]
                    ev1 = P.op("scalar", lambda e, dst=dst, bank=bank: e.activation(
                        dst, pa[:, bank, :], AF.Exp, scale=0.125), after=ev_mm)
                    if act == "expk":
                        ev1 = P.op("vector", lambda e, dst=dst: e.tensor_scalar(
                            dst, dst, 1e-6, None, ALU.add), after=ev1)
                    ps_release(k, "ps_a", bank, ev1)
                    proj_evs.append(ev1)
                prev_wb = ev_mm
        for vh in range(2):
            ev_w = P.dma("sync", out=wb[:, 0:1024].rearrange("p (kk mm) -> p kk mm", kk=8),
                         in_=hb["l0a_wT"][:, 2048 + g * 256 + vh * 128: 2048 + g * 256 + (vh + 1) * 128]
                             .rearrange("(kk p) mm -> p kk mm", p=128),
                         after=prev_wb)
            P.wait("tensor", [ev_w])
            for tl in range(16):
                bank = tl % 4
                P.wait("tensor", ps_wait_list(k, "ps_a", [bank]))
                for c in range(NCF):
                    ev_mm = P.op("tensor", lambda e, tl=tl, c=c, bank=bank: e.matmul(
                        pa[:, bank, 0:128], xcol(y, c, tl * 128, 128), wb[:, c * 128:(c + 1) * 128],
                        start=(c == 0), stop=(c == NCF - 1)), inc=(c == NCF - 1))
                ev = P.op("vector", lambda e, tl=tl, vh=vh, bank=bank: e.tensor_copy(
                    big[:, VT + tl * 256 + vh * 128: VT + tl * 256 + vh * 128 + 128], pa[:, bank, 0:128]),
                    after=ev_mm)
                ps_release(k, "ps_a", bank, ev)
                proj_evs.append(ev)
            prev_wb = ev_mm
        import os as _os
        _dbg = _os.environ.get("DBGL0A", "")
        if _dbg in ("q", "ke") and g == 0:
            srcb = QG if _dbg == "q" else KE
            P.wait("vector", proj_evs)
            return [P.op("vector", lambda e, c=c, srcb=srcb: e.tensor_copy(
                xcol(x, c), big[:, srcb // 2048 * 2048 + c * 2048 - 0 + (0 if c < 2 else 0):][:, 0:0] if False else big[:, srcb + c * 2048: srcb + (c + 1) * 2048])) for c in range(2)]
        if _dbg == "v" and g == 0:
            P.wait("vector", proj_evs)
            return [P.op("vector", lambda e, tl=tl: e.tensor_copy(
                xcol(x, 0, tl * 128, 128), big[:, VT + tl * 256: VT + tl * 256 + 128])) for tl in range(16)]
        P.phase("kn")
        kn_evs = []
        P.wait("vector", proj_evs)
        for ch in range(2):
            P.op("vector", lambda e, ch=ch: e.tensor_tensor_scan(
                big[:, KN + ch * 2048: KN + (ch + 1) * 2048],
                big[:, KE + ch * 2048: KE + (ch + 1) * 2048],
                big[:, KE + ch * 2048: KE + (ch + 1) * 2048], 0.0, ALU.add, ALU.bypass),
                inc=False, drain=True)
            ev = P.op("vector", lambda e, ch=ch: e.reciprocal(
                big[:, KN + ch * 2048: KN + (ch + 1) * 2048],
                big[:, KN + ch * 2048: KN + (ch + 1) * 2048]), drain=True)
            kn_evs.append(ev)
        P.phase("qs")
        qs_evs = []
        for ch in range(2):
            for tt in range(NTT):
                bank = 2 * (tt % 2)
                P.wait("tensor", ps_wait_list(k, "ps_b", [0, 1]) if False else ps_wait_list(k, "ps_b", [tt % 2]))
                ev_mm = P.op("tensor", lambda e, ch=ch, tt=tt: e.matmul(
                    pb[:, tt % 2, :], BLK64(t),
                    big[:, QG + ch * 2048 + tt * TTW: QG + ch * 2048 + (tt + 1) * TTW],
                    start=True, stop=True), after=(kn_evs if ch == 0 and tt == 0 else None))
                ev_r = P.op("vector", lambda e, tt=tt: e.reciprocal(
                    stats[:, 0:512], pb[:, tt % 2, :]), after=ev_mm, drain=True)
                ps_release(k, "ps_b", tt % 2, ev_r)
                qsl = big[:, QG + ch * 2048 + tt * TTW: QG + ch * 2048 + (tt + 1) * TTW]
                P.op("vector", lambda e, qsl=qsl: e.tensor_tensor(qsl, qsl, stats[:, 0:512], ALU.mult),
                     inc=False, drain=True)
                ev_q = P.op("vector", lambda e, qsl=qsl, ch=ch, tt=tt: e.tensor_tensor(
                    qsl, qsl, big[:, KN + ch * 2048 + tt * TTW: KN + ch * 2048 + (tt + 1) * TTW],
                    ALU.mult), dsync=True)
                qs_evs.append(ev_q)
        if _dbg in ("kn", "qs") and g == 0:
            srcb = KN if _dbg == "kn" else QG
            P.wait("vector", qs_evs)
            return [P.op("vector", lambda e, c=c, srcb=srcb: e.tensor_copy(
                xcol(x, c), big[:, srcb + c * 2048: srcb + (c + 1) * 2048])) for c in range(2)]
        P.phase("scan")
        oe_evs = []
        P.wait("gpsimd", qs_evs)
        last_s_read = {0: None, 64: None}
        for hh in range(4):
            qb = (hh % 2) * 64
            ch = hh // 2
            qoff = QG + ch * 2048
            keoff = KE + ch * 2048
            ooff = KN + ch * 2048
            ev_s0 = P.op("gpsimd", lambda e, qb=qb: e.memset(stats[qb:qb + 64, 0:64], 0.0),
                         after=last_s_read[qb])
            s_add = ev_s0
            for i in range(16):
                bank = i % 2
                P.wait("tensor", ps_wait_list(k, "ps_a", [bank]) + ([ev_s0] if i == 0 else []))
                ev_at = P.op("tensor", lambda e, i=i, qb=qb, qoff=qoff, keoff=keoff, bank=bank: e.matmul(
                    pa[:, bank, 0:128], big[qb:qb + 64, keoff + i * 128: keoff + (i + 1) * 128],
                    big[qb:qb + 64, qoff + i * 128: qoff + (i + 1) * 128], start=True, stop=True))
                ev_am = P.op("vector", lambda e, bank=bank: e.tensor_tensor(
                    stats[:, 64:192], pa[:, bank, 0:128], MASK_GE(t), ALU.mult), after=ev_at, dsync=True)
                ps_release(k, "ps_a", bank, ev_am)
                bank2 = 2 + i % 2
                P.wait("tensor", ps_wait_list(k, "ps_a", [bank2]))
                ev_kt = P.op("tensor", lambda e, i=i, qb=qb, keoff=keoff, bank2=bank2: e.transpose(
                    pa[0:128, bank2, 0:64], big[qb:qb + 64, keoff + i * 128: keoff + (i + 1) * 128],
                    IDENT(t)[qb:qb + 64, qb:qb + 64]))
                ev_kte = P.op("vector", lambda e, bank2=bank2: e.tensor_copy(
                    stats[:, 192:256], pa[0:128, bank2, 0:64]), after=ev_kt, dsync=True)
                ps_release(k, "ps_a", bank2, ev_kte)
                P.wait("tensor", ps_wait_list(k, "ps_b", [0]) + [ev_am, s_add])
                ev_y1 = P.op("tensor", lambda e, i=i, qb=qb, ch=ch: e.matmul(
                    pb[qb:qb + 64, 0, 0:128],
                    big[:, VT + i * 256 + ch * 128 + qb: VT + i * 256 + ch * 128 + qb + 64],
                    stats[:, 64:192], start=True, stop=True))
                P.wait("tensor", ps_wait_list(k, "ps_c", [0]))
                ev_y2 = P.op("tensor", lambda e, i=i, qb=qb, qoff=qoff: e.matmul(
                    pc[qb:qb + 64, 0, 0:128], stats[qb:qb + 64, 0:64],
                    big[qb:qb + 64, qoff + i * 128: qoff + (i + 1) * 128], start=True, stop=True))
                P.op("vector", lambda e, i=i, qb=qb, ooff=ooff: e.tensor_copy(
                    big[qb:qb + 64, ooff + i * 128: ooff + (i + 1) * 128], pb[qb:qb + 64, 0, 0:128]),
                    after=[ev_y1, ev_y2], inc=False)
                ev_oe = P.op("vector", lambda e, i=i, qb=qb, ooff=ooff: e.tensor_tensor(
                    big[qb:qb + 64, ooff + i * 128: ooff + (i + 1) * 128],
                    big[qb:qb + 64, ooff + i * 128: ooff + (i + 1) * 128],
                    pc[qb:qb + 64, 0, 0:128], ALU.add))
                ps_release(k, "ps_b", 0, ev_oe)
                ps_release(k, "ps_c", 0, ev_oe)
                oe_evs.append(ev_oe)
                last_s_read[qb] = ev_y2
                if i < 15:
                    P.wait("tensor", ps_wait_list(k, "ps_b", [1]) + [ev_kte])
                    ev_su = P.op("tensor", lambda e, i=i, qb=qb, ch=ch: e.matmul(
                        pb[qb:qb + 64, 1, 0:64], stats[:, 192:256],
                        big[:, VT + i * 256 + ch * 128 + qb: VT + i * 256 + ch * 128 + qb + 64],
                        start=True, stop=True))
                    s_add = P.op("vector", lambda e, qb=qb: e.tensor_tensor(
                        stats[qb:qb + 64, 0:64], stats[qb:qb + 64, 0:64], pb[qb:qb + 64, 1, 0:64], ALU.add),
                        after=ev_su, dsync=True)
                    ps_release(k, "ps_b", 1, s_add)
        if _dbg == "o" and g == 0:
            P.wait("vector", oe_evs)
            return [P.op("vector", lambda e, c=c: e.tensor_copy(
                xcol(x, c), big[:, KN + c * 2048: KN + (c + 1) * 2048])) for c in range(2)]
        P.phase("outproj")
        for m in range(8):
            ev_w = P.dma("sync", out=wb[:, 0:256].rearrange("p (kk mm) -> p kk mm", kk=2),
                         in_=hb["l0a_wout"][g * 256:(g + 1) * 256, m * 128:(m + 1) * 128]
                             .rearrange("(kk p) mm -> p kk mm", p=128),
                         after=prev_wb)
            P.wait("tensor", [ev_w] + (oe_evs if m == 0 else []))
            gevs = []
            for tt in range(NTT):
                bank = tt % 2
                P.wait("tensor", ps_wait_list(k, "ps_c", [bank]))
                for cc2 in range(2):
                    ev_mm = P.op("tensor", lambda e, m=m, tt=tt, cc2=cc2, bank=bank: e.matmul(
                        pc[:, bank, :], wb[:, cc2 * 128:(cc2 + 1) * 128],
                        big[:, KN + cc2 * 2048 + tt * TTW: KN + cc2 * 2048 + (tt + 1) * TTW],
                        start=(cc2 == 0), stop=(cc2 == 1)), inc=(cc2 == 1))
                ev = P.op("vector", lambda e, m=m, tt=tt, bank=bank: e.tensor_tensor(
                    xcol(x, m, tt * TTW, TTW), xcol(x, m, tt * TTW, TTW), pc[:, bank, :], ALU.add),
                    after=ev_mm)
                ps_release(k, "ps_c", bank, ev)
                gevs.append(ev)
            prev_wb = ev_mm
        x_evs = gevs
    P.phase("")
    k.wbuf_free = prev_wb
    return x_evs


def emit_l1a(k, evs):
    P, t, hb = k.P, k.t, k.hb
    x, y, big, small, stats = t["x"], t["y"], t["big"], t["small"], t["stats"]
    pa, pb, pc = t["ps_a"], t["ps_b"], t["ps_c"]
    evs = emit_ln(k, "x", evs, dst_name="y")

    ld = []
    ld.append(P.dma("sync", out=small[:, 0:32].rearrange("p (c f) -> p c f", c=8), in_=hb["l1a_cwq"],
                    after=bfence(k, evs)))
    ld.append(P.dma("sync", out=small[:, 32:40].rearrange("p (c f) -> p c f", c=2), in_=hb["l1a_cwk"]))
    ld.append(P.dma("sync", out=small[:, 40:48].rearrange("p (c f) -> p c f", c=2), in_=hb["l1a_cwv"]))
    ld.append(P.dma("sync", out=small[:, 48:52], in_=hb["l1a_dt"][0:1, :].squeeze(0).unsqueeze(0).broadcast_to([128, 4])))
    ld.append(P.dma("sync", out=small[:, 52:56], in_=hb["l1a_alog"][0:1, :].squeeze(0).unsqueeze(0).broadcast_to([128, 4])))
    ld.append(P.dma("sync", out=small[:, 56:60], in_=hb["l1a_bbeta"][0:1, :].squeeze(0).unsqueeze(0).broadcast_to([128, 4])))
    ld.append(P.dma("sync", out=small[:, 128:192], in_=hb["l1a_rmsw"][0:1, :].squeeze(0).unsqueeze(0).broadcast_to([128, 64])))
    ld.append(P.dma("sync", out=small[:, 192:224].rearrange("p (c f) -> p c f", c=8),
                    in_=hb["l1a_wbetaT"].rearrange("(c p) f -> p c f", p=128)))
    ld.append(P.dma("sync", out=small[:, 224:256].rearrange("p (c f) -> p c f", c=8),
                    in_=hb["l1a_wgkT"].rearrange("(c p) f -> p c f", p=128)))
    P.wait("vector", ld)
    e1 = P.op("vector", lambda e: e.tensor_scalar(small[:, 60:61], PIDX(t), 63.0, None, ALU.is_equal))
    e2 = P.op("vector", lambda e: e.tensor_scalar(small[:, 61:62], PIDX(t), 127.0, None, ALU.is_equal))
    e3 = P.op("vector", lambda e: e.tensor_tensor(small[:, 60:61], small[:, 60:61], small[:, 61:62], ALU.add))
    e63ev = P.op("vector", lambda e: e.tensor_copy(small[:, 64:128], small[:, 60:61].broadcast_to([128, 64])))
    ld.append(e63ev)

    def conv4(base_col, nch, cw_col, after):
        out_evs2 = []
        for c in range(nch):
            sl = lambda o, n, c=c: big[:, base_col + c * T + o: base_col + c * T + o + n]
            cw = lambda j, c=c: small[:, cw_col + c * 4 + j: cw_col + c * 4 + j + 1]
            P.op("vector", lambda e, sl=sl, cw=cw: e.tensor_scalar(
                stats[:, 0:T], sl(0, T), cw(3), None, ALU.mult), after=after if c == 0 else None)
            for j in range(3):
                sh = 3 - j
                P.op("vector", lambda e, sl=sl, cw=cw, j=j, sh=sh: e.scalar_tensor_tensor(
                    stats[:, sh:T], sl(0, T - sh), cw(j), stats[:, sh:T], ALU.mult, ALU.add))
            ev = P.op("vector", lambda e, sl=sl: e.tensor_copy(sl(0, T), stats[:, 0:T]))
            out_evs2.append(ev)
        return out_evs2

    def l2norm(base_col, nch, eps, after):
        oevs = []
        for c in range(nch):
            e_sq = P.op("scalar", lambda e, c=c: e.activation(
                stats[:, 0:T], big[:, base_col + c * T: base_col + (c + 1) * T], AF.Square),
                after=list(after) + oevs[-4:])
            for tt in range(NTT):
                bank = tt % 2
                P.wait("tensor", ps_wait_list(k, "ps_b", [bank]) + ([e_sq] if True else []))
                ev_mm = P.op("tensor", lambda e, tt=tt, bank=bank: e.matmul(
                    pb[:, bank, :], BLK64(t), stats[:, tt * TTW:(tt + 1) * TTW],
                    start=True, stop=True))
                ev_a = P.op("vector", lambda e, tt=tt, bank=bank, eps=eps: e.tensor_scalar(
                    stats[:, 1024 + 0:1024 + 512] if False else stats[:, tt * TTW:(tt + 1) * TTW],
                    pb[:, bank, :], eps, None, ALU.add), after=ev_mm)
                ps_release(k, "ps_b", bank, ev_a)
                P.wait("scalar", ev_a)
                P.op("scalar", lambda e, tt=tt: e.activation(
                    stats[:, tt * TTW:(tt + 1) * TTW], stats[:, tt * TTW:(tt + 1) * TTW], AF.Ln), inc=False)
                ev_l = P.op("scalar", lambda e, tt=tt: e.activation(
                    stats[:, tt * TTW:(tt + 1) * TTW], stats[:, tt * TTW:(tt + 1) * TTW], AF.Exp, scale=-0.5))
                ev = P.op("vector", lambda e, c=c, tt=tt: e.tensor_tensor(
                    big[:, base_col + c * T + tt * TTW: base_col + c * T + (tt + 1) * TTW],
                    big[:, base_col + c * T + tt * TTW: base_col + c * T + (tt + 1) * TTW],
                    stats[:, tt * TTW:(tt + 1) * TTW], ALU.mult), after=ev_l)
                oevs.append(ev)
        return oevs

    stg = [stats[:, 0:512], stats[:, 512:1024]]
    stg_dma = [None, None]
    def gatew(P_, ps_ap, m, tt, ev):
        slot = (m * NTT + tt) % 2
        e1_ = P_.op("scalar", lambda e, slot=slot: e.activation(stg[slot], ps_ap, AF.Silu),
                    after=[ev] + ([stg_dma[slot]] if stg_dma[slot] is not None else []))
        d = P_.dma("gpsimd", out=k.spill[m * 128:(m + 1) * 128, tt * TTW:(tt + 1) * TTW],
                   in_=stg[slot], after=e1_)
        stg_dma[slot] = d
        return d
    gevs = proj_fm(k, "l1a_wgateT", y, None, T, list(evs) + ld, gatew)
    def qw(P_, ps_ap, m, tt, ev):
        return P_.op("vector", lambda e: e.tensor_copy(
            big[:, m * T + tt * TTW: m * T + tt * TTW + TTW], ps_ap), after=ev)
    qevs = proj_fm(k, "l1a_wqT", y, None, T, gevs, qw)
    qevs = conv4(0, 8, 0, qevs)
    qevs = l2norm(0, 8, 1e-12, qevs)
    q_sp = []
    for c in range(NCF):
        q_sp.append(P.dma("gpsimd", out=k.spill2[c * 128:(c + 1) * 128, :], in_=xcol(big, c),
                          after=qevs))
    import os as _os
    _dbg = _os.environ.get("DBGL1A", "")
    if _dbg == "q":
        P.wait("vector", q_sp)
        return [P.op("vector", lambda e, c=c: e.tensor_copy(xcol(x, c), xcol(big, c))) for c in range(8)]
    KF, VF = 1024, 5120
    QCH = 0
    def kw(P_, ps_ap, m, tt, ev):
        return P_.op("vector", lambda e: e.tensor_copy(
            big[:, KF + m * T + tt * TTW: KF + m * T + tt * TTW + TTW], ps_ap), after=ev)
    kevs = proj_fm(k, "l1a_wkT", y, None, T, q_sp, kw, n_out=256)
    kevs = conv4(KF, 2, 32, kevs)
    kevs = l2norm(KF, 2, 1e-12, kevs)
    def vw(P_, ps_ap, m, tt, ev):
        return P_.op("vector", lambda e: e.tensor_copy(
            big[:, VF + m * T + tt * TTW: VF + m * T + tt * TTW + TTW], ps_ap), after=ev)
    vevs = proj_fm(k, "l1a_wvT", y, None, T, kevs, vw, n_out=256)
    vevs = conv4(VF, 2, 40, vevs)
    if _dbg in ("k", "v"):
        bcol0 = KF if _dbg == "k" else VF
        P.wait("vector", vevs)
        return [P.op("vector", lambda e, c=c, bcol0=bcol0: e.tensor_copy(
            xcol(x, c), big[:, bcol0 + c * T: bcol0 + (c + 1) * T])) for c in range(2)]
    BG = 10240
    BB = 10624
    SREG = 11008
    SCR = 11264
    bz_evs = []
    for i in range(32):
        for base in (0, 64):
            for (wcol, dst) in ((192, BB), (224, BG)):
                bank = (i + base // 64) % 2
                P.wait("tensor", ps_wait_list(k, "ps_c", [bank]) + (vevs if i == 0 else []))
                for c in range(NCF):
                    ev_mm = P.op("tensor", lambda e, i=i, c=c, base=base, wcol=wcol, bank=bank: e.matmul(
                        pc[base:base + 64, bank, 0:4], xcol(y, c, i * 64, 64),
                        small[:, wcol + c * 4: wcol + (c + 1) * 4],
                        start=(c == 0), stop=(c == NCF - 1)), inc=(c == NCF - 1))
                ev = P.op("vector", lambda e, i=i, base=base, dst=dst, bank=bank: e.tensor_copy(
                    big[base:base + 64, dst + i * 4: dst + (i + 1) * 4], pc[base:base + 64, bank, 0:4]),
                    after=ev_mm)
                ps_release(k, "ps_c", bank, ev)
                bz_evs.append(ev)
    P.wait("vector", bz_evs)
    ev_bb = P.op("vector", lambda e: e.tensor_tensor(
        big[:, BB:BB + 128].rearrange("p (i f) -> p i f", i=32), 
        big[:, BB:BB + 128].rearrange("p (i f) -> p i f", i=32),
        small[:, 56:60].unsqueeze(1).broadcast_to([128, 32, 4]), ALU.add))
    P.wait("scalar", ev_bb)
    ev_bb = P.op("scalar", lambda e: e.activation(big[:, BB:BB + 128], big[:, BB:BB + 128], AF.Sigmoid))
    ev_nb = P.op("vector", lambda e: e.tensor_scalar(big[:, BB + 256:BB + 384], big[:, BB:BB + 128], -1.0, None, ALU.mult), after=ev_bb)
    ev_z = P.op("vector", lambda e: e.tensor_tensor(
        big[:, BG:BG + 128].rearrange("p (i f) -> p i f", i=32),
        big[:, BG:BG + 128].rearrange("p (i f) -> p i f", i=32),
        small[:, 48:52].unsqueeze(1).broadcast_to([128, 32, 4]), ALU.add))
    P.wait("scalar", ev_z)
    P.op("scalar", lambda e: e.activation(big[:, BG:BG + 128], big[:, BG:BG + 128], AF.Exp), inc=False)
    ev_e = P.op("vector", lambda e: e.tensor_scalar(big[:, BG:BG + 128], big[:, BG:BG + 128], 1.0, None, ALU.add),
                after=P.fence("scalar"))
    P.wait("scalar", ev_e)
    ev_sp = P.op("scalar", lambda e: e.activation(big[:, BG:BG + 128], big[:, BG:BG + 128], AF.Ln))
    P.op("scalar", lambda e: e.activation(small[:, 52:56], small[:, 52:56], AF.Exp), inc=False)
    ev_an = P.op("scalar", lambda e: e.activation(small[:, 52:56], small[:, 52:56], AF.Sign) if False else
                 e.activation(small[:, 52:56], small[:, 52:56], AF.Abs))
    ev_an = P.op("vector", lambda e: e.tensor_scalar(small[:, 52:56], small[:, 52:56], -1.0, None, ALU.mult),
                 after=ev_an)
    ev_gk = P.op("vector", lambda e: e.tensor_tensor(
        big[:, BG:BG + 128].rearrange("p (i f) -> p i f", i=32),
        big[:, BG:BG + 128].rearrange("p (i f) -> p i f", i=32),
        small[:, 52:56].unsqueeze(1).broadcast_to([128, 32, 4]), ALU.mult), after=[ev_sp, ev_an])
    bs_evs = []
    for i in range(32):
        for base in (0, 64):
            bank = (i + base // 64) % 2
            P.wait("tensor", ps_wait_list(k, "ps_c", [bank]) + ([ev_gk] if i == 0 else []))
            ev_mm = P.op("tensor", lambda e, i=i, base=base, bank=bank: e.matmul(
                pc[base:base + 64, bank, 0:4], MASK_GE(t)[base:base + 64, base:base + 64],
                big[base:base + 64, BG + i * 4: BG + (i + 1) * 4], start=True, stop=True))
            ev = P.op("vector", lambda e, i=i, base=base, bank=bank: e.tensor_copy(
                big[base:base + 64, BG + 128 + i * 4: BG + 128 + (i + 1) * 4],
                pc[base:base + 64, bank, 0:4]), after=ev_mm)
            ps_release(k, "ps_c", bank, ev)
            bs_evs.append(ev)
    P.wait("scalar", bs_evs)
    ev_g = P.op("scalar", lambda e: e.activation(stats[:, 0:128], big[:, BG + 128:BG + 256], AF.Exp))
    ev_rg = P.op("scalar", lambda e: e.activation(stats[:, 128:256], big[:, BG + 128:BG + 256], AF.Exp, scale=-1.0))
    ev_gc = P.op("vector", lambda e: e.tensor_copy(big[:, BG + 256:BG + 384], stats[:, 0:128]), after=ev_g)
    ev_rgc = P.op("vector", lambda e: e.tensor_copy(big[:, 11136:11264], stats[:, 128:256]), after=ev_rg)
    ev_bg = P.op("vector", lambda e: e.tensor_tensor(big[:, BB + 128:BB + 256], big[:, BB:BB + 128],
                                                     big[:, BG + 256:BG + 384], ALU.mult), after=[ev_bb, ev_gc])
    GAM, RGAM, BGAM, NBET, BSUM = BG + 256, 11136, BB + 128, BB + 256, BG + 128
    pre_evs = [ev_nb, ev_bg, ev_rgc] + list(vevs) + list(kevs)

    if _dbg == "bg":
        P.wait("vector", pre_evs)
        evd = [P.op("vector", lambda e: e.tensor_copy(xcol(x, 0, 0, 128), big[:, BB:BB + 128])),
               P.op("vector", lambda e: e.tensor_copy(xcol(x, 0, 128, 128), big[:, GAM:GAM + 128])),
               P.op("vector", lambda e: e.tensor_copy(xcol(x, 0, 256, 128), big[:, BSUM:BSUM + 128]))]
        return evd
    oe_evs = []
    sc = lambda j: SCR + j * 64
    SCR2 = 12288
    ev_s0 = []
    P.wait("gpsimd", pre_evs)
    for n in range(4):
        base = (n % 2) * 64
        scol = SREG + (n // 2) * 64
        ev_s0.append(P.op("gpsimd", lambda e, base=base, scol=scol: e.memset(
            big[base:base + 64, scol:scol + 64], 0.0)))
    qslot_ev = [None, None]
    last_q_use = [None, None]
    for i in range(32):
        slot = i % 2
        ev_q = P.dma("gpsimd", out=big[:, QCH + slot * 512: QCH + slot * 512 + 512]
                     .rearrange("p (c w) -> p c w", c=8),
                     in_=k.spill2.rearrange("(c p) t -> p c t", p=128)[:, :, i * 64:(i + 1) * 64],
                     after=last_q_use[slot] if last_q_use[slot] is not None else (q_sp if i < 2 else None))
        qslot_ev[slot] = ev_q
        opair_evs = []
        for n in range(4):
            base = (n % 2) * 64
            scol = SREG + (n // 2) * 64
            S0 = big[base:base + 64, scol:scol + 64]
            kcol = KF + (n // 2) * T + i * 64
            vcol = VF + (n // 2) * T + i * 64
            Kf = big[base:base + 64, kcol:kcol + 64]
            Kpair = big[:, kcol:kcol + 64]
            Vpair = big[:, vcol:vcol + 64]
            I64 = IDENT(t)[0:64, 0:64]
            GE64 = MASK_GE(t)[0:64, 0:64]
            GT64 = MASK_GT(t)[0:64, 0:64]
            _c = {buf: big[0:64, buf + i * 4 + n: buf + i * 4 + n + 1]
                  for buf in (GAM, RGAM, BGAM, NBET, BSUM)}
            _cq = {buf: big[base:base + 64, buf + i * 4 + n: buf + i * 4 + n + 1]
                   for buf in (GAM, RGAM, BGAM, NBET, BSUM)}
            _T = [big[0:64, sc(j):sc(j) + 64] for j in range(16)]
            VT_P = big[0:64, sc(0):sc(0) + 128]
            KT_P = big[0:64, sc(2):sc(2) + 128]
            KHF_P = big[:, sc(4):sc(4) + 64]
            KTF_P = big[:, sc(5):sc(5) + 64]
            _rms = big[0:64, sc(14) + 63:sc(14) + 64]
            _gbar = big[base:base + 64, sc(15) + 62 - n: sc(15) + 63 - n] if False else big[base:base + 64, sc(14) + 62:sc(14) + 63]
            _sfac = big[base:base + 64, sc(14) + 61:sc(14) + 62]
            if n % 2 == 0:
                P.wait("tensor", ps_wait_list(k, "ps_b", [0, 1]) + (ev_s0 if i == 0 and n == 0 else []))
                evt1 = P.op("tensor", lambda e, Vpair=Vpair: e.transpose(pb[0:64, 0, 0:128], Vpair, IDENT(t)))
                evt2 = P.op("tensor", lambda e, Kpair=Kpair: e.transpose(pb[0:64, 1, 0:128], Kpair, IDENT(t)))
                ev_vt = P.op("vector", lambda e, VT_P=VT_P: e.tensor_copy(VT_P, pb[0:64, 0, 0:128]), after=evt1)
                ev_kt = P.op("vector", lambda e, KT_P=KT_P: e.tensor_copy(KT_P, pb[0:64, 1, 0:128]), after=evt2)
                ps_release(k, "ps_b", 0, ev_vt)
                ps_release(k, "ps_b", 1, ev_kt)
                k._l1a_pair_ev = (ev_vt, ev_kt)
            ev_vt, ev_kt = k._l1a_pair_ev
            nh = n % 2
            VT_n = big[0:64, sc(0) + nh * 64: sc(0) + nh * 64 + 64]
            KT_n = big[0:64, sc(2) + nh * 64: sc(2) + nh * 64 + 64]
            P.op("vector", lambda e, _T=_T, KT_n=KT_n, _c=_c: e.tensor_scalar(
                _T[6], KT_n, _c[RGAM], None, ALU.mult), after=ev_kt, inc=False)
            ev_sc2 = P.op("vector", lambda e, _T=_T, KT_n=KT_n, _c=_c: e.tensor_scalar(
                _T[7], KT_n, _c[BGAM], None, ALU.mult))
            P.wait("tensor", ps_wait_list(k, "ps_b", [0, 1]) + [ev_sc2])
            evt3 = P.op("tensor", lambda e, _T=_T: e.transpose(pb[0:64, 0, 0:64], _T[6], I64))
            evt4 = P.op("tensor", lambda e, _T=_T: e.transpose(pb[0:64, 1, 0:64], _T[7], I64))
            if base == 0:
                ev_c4 = P.op("vector", lambda e, KHF_P=KHF_P: e.tensor_copy(KHF_P[0:64, :], pb[0:64, 0, 0:64]), after=evt3)
                ev_c5 = P.op("vector", lambda e, KTF_P=KTF_P: e.tensor_copy(KTF_P[0:64, :], pb[0:64, 1, 0:64]), after=evt4)
                ps_release(k, "ps_b", 0, ev_c4)
                ps_release(k, "ps_b", 1, ev_c5)
            else:
                P.op("vector", lambda e, _T=_T: e.tensor_copy(_T[8], pb[0:64, 0, 0:64]), after=evt3, inc=False)
                ev_tmp = P.op("vector", lambda e, _T=_T: e.tensor_copy(_T[9], pb[0:64, 1, 0:64]), after=evt4)
                ps_release(k, "ps_b", 0, ev_tmp)
                ps_release(k, "ps_b", 1, ev_tmp)
                P.wait("tensor", ps_wait_list(k, "ps_b", [0, 1]) + [ev_tmp])
                evr1 = P.op("tensor", lambda e, _T=_T: e.matmul(pb[64:128, 0, 0:64], I64, _T[8], start=True, stop=True))
                evr2 = P.op("tensor", lambda e, _T=_T: e.matmul(pb[64:128, 1, 0:64], I64, _T[9], start=True, stop=True))
                ev_c4 = P.op("vector", lambda e, KHF_P=KHF_P: e.tensor_copy(KHF_P[64:128, :], pb[64:128, 0, 0:64]), after=evr1)
                ev_c5 = P.op("vector", lambda e, KTF_P=KTF_P: e.tensor_copy(KTF_P[64:128, :], pb[64:128, 1, 0:64]), after=evr2)
                ps_release(k, "ps_b", 0, ev_c4)
                ps_release(k, "ps_b", 1, ev_c5)
            KHF_n = KHF_P[base:base + 64, :]
            KTF_n = KTF_P[base:base + 64, :]
            P.wait("tensor", ps_wait_list(k, "ps_b", [0]))
            ev_ks = P.op("tensor", lambda e, Kf=Kf, S0=S0: e.matmul(
                pb[0:64, 0, 0:64], Kf, S0, start=True, stop=True))
            P.op("vector", lambda e, _T=_T, _c=_c, VT_n=VT_n: e.scalar_tensor_tensor(
                _T[10], pb[0:64, 0, 0:64], _c[GAM], VT_n, ALU.mult, ALU.subtract), after=[ev_ks, ev_vt], inc=False)
            ev_r = P.op("vector", lambda e, _T=_T, _c=_c: e.tensor_scalar(_T[10], _T[10], _c[NBET], None, ALU.mult))
            ps_release(k, "ps_b", 0, ev_r)
            P.wait("tensor", ps_wait_list(k, "ps_a", [0, 1]) + [ev_c4, ev_c5])
            ev_a1 = P.op("tensor", lambda e, KTF_n=KTF_n, KHF_n=KHF_n: e.matmul(pa[0:64, 0, 0:64], KTF_n, KHF_n, start=True, stop=True))
            ev_a2 = P.op("tensor", lambda e, KTF_n=KTF_n, KHF_n=KHF_n: e.matmul(pa[0:64, 1, 0:64], KHF_n, KTF_n, start=True, stop=True))
            P.op("vector", lambda e, _T=_T: e.tensor_tensor(_T[11], pa[0:64, 0, 0:64], GE64, ALU.mult), after=ev_a1, inc=False)
            ev_m1 = P.op("vector", lambda e, _T=_T: e.tensor_tensor(_T[11], pa[0:64, 0, 0:64], _T[11], ALU.subtract))
            ev_m2 = P.op("vector", lambda e, _T=_T: e.tensor_tensor(_T[12], pa[0:64, 1, 0:64], GT64, ALU.mult), after=ev_a2)
            ps_release(k, "ps_a", 0, ev_m1)
            ps_release(k, "ps_a", 1, ev_m2)
            P.op("vector", lambda e, _T=_T: e.tensor_tensor(_T[13], I64, _T[11], ALU.subtract), after=[ev_m1, ev_m2], inc=False)
            P.op("vector", lambda e, _T=_T: e.tensor_tensor(_T[14], I64, _T[12], ALU.subtract), inc=False)
            P.op("vector", lambda e, _T=_T: e.tensor_copy(_T[8], _T[11]), inc=False)
            ev_x0 = P.op("vector", lambda e, _T=_T: e.tensor_copy(_T[9], _T[12]))
            lev_ev = ev_x0
            for lev in range(5):
                P.wait("tensor", ps_wait_list(k, "ps_a", [0, 1, 2, 3]) + [lev_ev])
                ev_s1 = P.op("tensor", lambda e, _T=_T: e.matmul(pa[0:64, 0, 0:64], _T[9], _T[8], start=True, stop=True))
                ev_s2 = P.op("tensor", lambda e, _T=_T: e.matmul(pa[0:64, 1, 0:64], _T[8], _T[9], start=True, stop=True))
                ev_xe = P.op("vector", lambda e, _T=_T: e.tensor_copy(_T[8], pa[0:64, 0, 0:64]), after=[ev_s1, ev_s2])
                ev_xe2 = P.op("vector", lambda e, _T=_T: e.tensor_copy(_T[9], pa[0:64, 1, 0:64]), after=ev_s2)
                ps_release(k, "ps_a", 0, ev_xe)
                ps_release(k, "ps_a", 1, ev_xe2)
                P.wait("tensor", [ev_xe, ev_xe2])
                ev_g1 = P.op("tensor", lambda e, _T=_T: e.matmul(pa[0:64, 2, 0:64], _T[14], _T[8], start=True, stop=True))
                ev_g2 = P.op("tensor", lambda e, _T=_T: e.matmul(pa[0:64, 3, 0:64], _T[8], _T[14], start=True, stop=True))
                ev_tu = P.op("vector", lambda e, _T=_T: e.tensor_tensor(_T[13], _T[13], pa[0:64, 2, 0:64], ALU.add), after=ev_g1)
                ev_tu2 = P.op("vector", lambda e, _T=_T: e.tensor_tensor(_T[14], _T[14], pa[0:64, 3, 0:64], ALU.add), after=ev_g2)
                ps_release(k, "ps_a", 2, ev_tu)
                ps_release(k, "ps_a", 3, ev_tu2)
                lev_ev = ev_tu2
            P.wait("tensor", ps_wait_list(k, "ps_b", [0]) + [ev_r, lev_ev])
            ev_umm = P.op("tensor", lambda e, _T=_T: e.matmul(pb[0:64, 0, 0:64], _T[14], _T[10], start=True, stop=True))
            ev_u = P.op("vector", lambda e, _T=_T: e.tensor_copy(_T[8], pb[0:64, 0, 0:64]), after=ev_umm)
            ps_release(k, "ps_b", 0, ev_u)
            if _dbg == "c0" and i == 0 and n == 0:
                P.wait("vector", [ev_u])
                tap = []
                for (name, ap, dst) in (("A", _T[11], 0), ("At", _T[12], 64), ("T", _T[13], 128),
                                         ("Tt", _T[14], 192), ("r", _T[10], 256), ("u", _T[8], 320),
                                         ("Khf", KHF_P[0:64, :], 384), ("Ktf", KTF_P[0:64, :], 448),
                                         ("Kt", KT_n, 512), ("Vt", VT_n, 576)):
                    tap.append(P.op("vector", lambda e, ap=ap, dst=dst: e.tensor_copy(
                        x[0:64, dst:dst + 64], ap)))
                return tap
            o2_list = []
            for jq in range(4):
                h = 4 * n + jq
                c_h = (h % 4) + 4 * (n // 2)
                Qf = big[base:base + 64, QCH + slot * 512 + c_h * 64: QCH + slot * 512 + c_h * 64 + 64]
                KHF_q = KHF_P[base:base + 64, :]
                P.wait("tensor", ps_wait_list(k, "ps_a", [0, 1]) + [qslot_ev[slot], ev_u])
                ev_b1 = P.op("tensor", lambda e, Qf=Qf, KHF_q=KHF_q: e.matmul(
                    pa[0:64, 0, 0:64], Qf, KHF_q, start=True, stop=True))
                P.op("vector", lambda e, _T=_T, _c=_c: e.tensor_scalar(_T[11], pa[0:64, 0, 0:64], _c[GAM], None, ALU.mult), after=ev_b1, inc=False)
                P.op("vector", lambda e, _T=_T: e.tensor_tensor(_T[12], _T[11], GT64, ALU.mult), inc=False)
                ev_bm = P.op("vector", lambda e, _T=_T: e.tensor_tensor(_T[11], _T[11], _T[12], ALU.subtract))
                ps_release(k, "ps_a", 0, ev_bm)
                P.wait("tensor", ps_wait_list(k, "ps_a", [1]) + [ev_bm])
                ev_bt = P.op("tensor", lambda e, _T=_T: e.transpose(pa[0:64, 1, 0:64], _T[11], I64))
                ev_bte = P.op("vector", lambda e, _T=_T: e.tensor_copy(_T[12], pa[0:64, 1, 0:64]), after=ev_bt)
                ps_release(k, "ps_a", 1, ev_bte)
                P.wait("tensor", ps_wait_list(k, "ps_b", [0, 1]) + [ev_bte])
                ev_o1 = P.op("tensor", lambda e, _T=_T: e.matmul(pb[0:64, 0, 0:64], _T[12], _T[8], start=True, stop=True))
                ev_o2 = P.op("tensor", lambda e, Qf=Qf, S0=S0: e.matmul(pb[0:64, 1, 0:64], Qf, S0, start=True, stop=True))
                o2_list.append(ev_o2)
                P.op("vector", lambda e, _T=_T, _c=_c: e.tensor_scalar(_T[15], pb[0:64, 1, 0:64], _c[GAM], 0.125, ALU.mult, ALU.mult), after=ev_o2, inc=False)
                ev_oo = P.op("vector", lambda e, _T=_T: e.scalar_tensor_tensor(_T[15], pb[0:64, 0, 0:64], 0.125, _T[15], ALU.mult, ALU.add), after=ev_o1)
                ps_release(k, "ps_b", 0, ev_oo)
                ps_release(k, "ps_b", 1, ev_oo)
                if _dbg == "h0" and i == 0 and n == 0 and jq == 0:
                    P.op("vector", lambda e, _T=_T: e.tensor_copy(x[0:64, 256:320], _T[15]), after=ev_oo, inc=False)
                ev_sq = P.op("scalar", lambda e, _T=_T: e.activation(_T[14] if False else _T[11], _T[15], AF.Square), after=ev_oo)
                P.wait("vector", ev_sq)
                P.op("vector", lambda e, _T=_T, _rms=_rms: e.tensor_reduce(_rms, _T[11], axis=AX.X, op=ALU.add), inc=False)
                ev_rd = P.op("vector", lambda e, _rms=_rms: e.tensor_scalar(_rms, _rms, 1.0 / 64, 1e-5, ALU.mult, ALU.add))
                P.wait("scalar", ev_rd)
                P.op("scalar", lambda e, _rms=_rms: e.activation(_rms, _rms, AF.Ln), inc=False)
                ev_rs = P.op("scalar", lambda e, _rms=_rms: e.activation(_rms, _rms, AF.Exp, scale=-0.5))
                P.wait("vector", ev_rs)
                P.op("vector", lambda e, _T=_T, _rms=_rms: e.tensor_scalar(_T[15], _T[15], _rms, None, ALU.mult), inc=False)
                ev_on = P.op("vector", lambda e, _T=_T, c_h=c_h, nh2=(n % 2): e.tensor_tensor(
                    big[0:64, SCR2 + c_h * 128 + nh2 * 64: SCR2 + c_h * 128 + nh2 * 64 + 64],
                    _T[15], small[0:64, 128:192], ALU.mult))
                oe_evs.append(ev_on)
                opair_evs.append(ev_on)
                if _dbg == "h0" and i == 0 and n == 0 and jq == 0:
                    P.wait("vector", [ev_on])
                    tap = []
                    for (ap, dst) in ((_T[15], 0), (_T[12], 64), (big[0:64, SCR2 + c_h * 128: SCR2 + c_h * 128 + 64], 128), (Qf, 192)):
                        tap.append(P.op("vector", lambda e, ap=ap, dst=dst: e.tensor_copy(
                            x[0:64, dst:dst + 64], ap)))
                    return tap
                if jq == 3:
                    last_q_use[slot] = ev_o2
            P.wait("tensor", ps_wait_list(k, "ps_b", [1]))
            ev_e63 = P.op("tensor", lambda e, base=base, _cq=_cq: e.matmul(
                pb[base:base + 64, 1, 0:1], small[base:base + 64, 64:128], _cq[BSUM], start=True, stop=True))
            P.wait("scalar", ev_e63)
            ev_gb = P.op("scalar", lambda e, base=base, _gbar=_gbar: e.activation(
                _gbar, pb[base:base + 64, 1, 0:1], AF.Exp))
            ev_sf0 = P.op("vector", lambda e, base=base, _sfac=_sfac, _cq=_cq: e.tensor_tensor(
                _sfac, pb[base:base + 64, 1, 0:1], _cq[BSUM], ALU.subtract), after=ev_e63)
            ps_release(k, "ps_b", 1, [ev_gb, ev_sf0])
            P.wait("scalar", ev_sf0)
            ev_sf = P.op("scalar", lambda e, _sfac=_sfac: e.activation(_sfac, _sfac, AF.Exp))
            ev_sg = P.op("vector", lambda e, S0=S0, _gbar=_gbar: e.tensor_scalar(
                S0, S0, _gbar, None, ALU.mult), after=[ev_gb] + o2_list)
            ev_e63b = P.op("tensor", lambda e, _c=_c: e.matmul(
                pc[0:64, 0, 0:1], small[0:64, 64:128], _c[BSUM], start=True, stop=True),
                after=ps_wait_list(k, "ps_c", [0]))
            ev_sf0b = P.op("vector", lambda e, _T=_T, _c=_c: e.tensor_tensor(
                big[0:64, sc(14) + 60:sc(14) + 61], pc[0:64, 0, 0:1], _c[BSUM], ALU.subtract), after=ev_e63b)
            ps_release(k, "ps_c", 0, ev_sf0b)
            P.wait("scalar", ev_sf0b)
            ev_sfb = P.op("scalar", lambda e: e.activation(
                big[0:64, sc(14) + 60:sc(14) + 61], big[0:64, sc(14) + 60:sc(14) + 61], AF.Exp))
            ev_kb = P.op("vector", lambda e, _T=_T, KT_n=KT_n: e.tensor_scalar(
                _T[9], KT_n, big[0:64, sc(14) + 60:sc(14) + 61], None, ALU.mult), after=ev_sfb)
            P.wait("tensor", ps_wait_list(k, "ps_b", [1]) + [ev_kb, ev_u])
            ev_ds = P.op("tensor", lambda e, _T=_T, base=base: e.matmul(
                pb[base:base + 64, 1, 0:64], _T[9], _T[8], start=True, stop=True))
            ev_sa = P.op("vector", lambda e, S0=S0, base=base: e.tensor_tensor(
                S0, S0, pb[base:base + 64, 1, 0:64], ALU.add), after=[ev_ds, ev_sg])
            ps_release(k, "ps_b", 1, ev_sa)
            if _dbg == "s0" and i == 1 and n == 0:
                P.wait("vector", [ev_sa])
                return [P.op("vector", lambda e, S0=S0: e.tensor_copy(x[0:64, 0:64], S0))]
        for c in range(8):
            P.wait("tensor", ps_wait_list(k, "ps_c", [1]) + opair_evs)
            ev_pt = P.op("tensor", lambda e, c=c: e.transpose(
                pc[:, 1, 0:64], big[0:64, SCR2 + c * 128: SCR2 + (c + 1) * 128], IDENT(t)[0:64, 0:64]))
            ev_pe = P.op("vector", lambda e, c=c, i=i: e.tensor_copy(
                y[:, c * T + i * 64: c * T + (i + 1) * 64], pc[:, 1, 0:64]), after=ev_pt)
            ps_release(k, "ps_c", 1, ev_pe)
            oe_evs.append(ev_pe)
    if _dbg == "o":
        P.wait("vector", oe_evs)
        return [P.op("vector", lambda e, c=c: e.tensor_copy(xcol(x, c), xcol(y, c))) for c in range(8)]
    P.wait("vector", oe_evs)
    og_evs = []
    for c in range(NCF):
        ev_gd = P.dma("sync", out=stats[:, 0:T], in_=k.spill[c * 128:(c + 1) * 128, :],
                      after=(og_evs[-1:] if og_evs else list(oe_evs)))
        ev = P.op("vector", lambda e, c=c: e.tensor_tensor(xcol(y, c), xcol(y, c), stats[:, 0:T], ALU.mult),
                  after=ev_gd)
        og_evs.append(ev)
    def xw(P_, ps_ap, m, tt, ev):
        return P_.op("vector", lambda e, m=m, tt=tt: e.tensor_tensor(
            xcol(x, m, tt * TTW, TTW), xcol(x, m, tt * TTW, TTW), ps_ap, ALU.add), after=ev)
    x_evs = proj_fm(k, "l1a_wprojT", y, None, T, og_evs, xw)
    return x_evs


def emit_l2f(k, evs):
    P, t, hb = k.P, k.t, k.hb
    x, y, big, small, stats = t["x"], t["y"], t["big"], t["small"], t["stats"]
    u32 = t["u32a"]
    pa, pb, pc = t["ps_a"], t["ps_b"], t["ps_c"]
    P.phase("l2f")
    ROWL, IDN = 2176, 8 * 2176 + 8
    ld = [P.dma("sync", out=small[:, 32:96].rearrange("p (c e) -> p c e", c=8),
                in_=hb["l2f_gateT"].rearrange("(c p) e -> p c e", p=128),
                after=bfence(k, evs)),
          P.dma("sync", out=small[:, 96:104], in_=hb["l2f_lng"]),
          P.dma("sync", out=small[:, 104:112], in_=hb["l2f_lnb"]),
          P.dma("sync", out=small[:, 112:136], in_=hb["l2f_pc"])]
    ev_ib = P.op("vector", lambda e: e.tensor_scalar(small[:, 136:144], IOTA8(t), 65536.0, None, ALU.add),
                 after=evs)
    ev_ig = P.op("gpsimd", lambda e: e.iota(small[:, 144:160], pattern=[[128, 16]], base=0,
                                            channel_multiplier=1, allow_small_or_imprecise_dtypes=True),
                 after=bfence(k, evs))
    ln1 = emit_ln(k, "x", evs, dst_name="y", sq_scratch=xcol(big, 0))
    ln2 = emit_ln(k, "y", ln1, dst_name="big", sq_scratch=stats[:, 0:T])
    gb_evs = []
    for c in range(NCF):
        ev = P.op("vector", lambda e, c=c: e.scalar_tensor_tensor(
            xcol(big, c), xcol(big, c), small[:, 96 + c:97 + c],
            small[:, 104 + c:105 + c].broadcast_to([128, T]), ALU.mult, ALU.add),
            after=([ln2[c]] + ([ld[1], ld[2]] if c == 0 else [])))
        gb_evs.append(ev)
    xadd_evs = []
    for c in range(NCF):
        xadd_evs.append(P.op("vector", lambda e, c=c: e.tensor_tensor(
            xcol(x, c), xcol(x, c), xcol(y, c), ALU.add)))
    LG, NEGM, SUME, MAXL, T8 = 8192, 8200, 8201, 8202, 8204
    GLR, GT1, GW0, GRK, GKP, GSL, GIS, GRS, GTMP, GIDS = (8300, 8316, 8332, 8348,
                                                          8364, 8380, 8396, 8412, 8428, 8444)
    OFFB, CNTB, OFFMY, TOTG = 8460, 8468, 8476, 8484
    stage = [y[:, 0:1024], y[:, 1024:2048]]
    st_dma = [None, None]
    ag_t_dep = []
    oh_evs = []
    e_lg_prev, ohe_prev = None, None
    pbev = [None, None]
    P.wait("tensor", ld + [ev_ig, ev_ib] + gb_evs
           + ps_wait_list(k, "ps_a", range(4)) + ps_wait_list(k, "ps_b", [0, 1])
           + ps_wait_list(k, "ps_c", [0, 1]))
    for q in range(16):
        P.wait("tensor", e_lg_prev)
        for c in range(NCF):
            ev_mm = P.op("tensor", lambda e, q=q, c=c: e.matmul(
                pa[:, 0, 0:8], big[:, c * T + q * 128: c * T + q * 128 + 128],
                small[:, 32 + c * 8: 40 + c * 8],
                start=(c == 0), stop=(c == NCF - 1)), inc=(c == NCF - 1))
        e_lg = P.op("vector", lambda e: e.tensor_copy(y[:, LG:LG + 8], pa[:, 0, 0:8]), after=ev_mm)
        e_lg_prev = e_lg
        P.op("vector", lambda e: e.reduce_max(y[:, NEGM:NEGM + 1], y[:, LG:LG + 8],
                                              axis=AX.X, negate=True), inc=False)
        ev_nm = P.op("vector", lambda e: e.reduce_max(y[:, MAXL:MAXL + 1], y[:, LG:LG + 8], axis=AX.X))
        P.wait("scalar", ev_nm)
        e_se = P.op("scalar", lambda e: e.activation(y[:, T8:T8 + 8], y[:, LG:LG + 8], AF.Exp,
                                                     bias=y[:, NEGM:NEGM + 1],
                                                     accum_out=y[:, SUME:SUME + 1]))
        P.wait("vector", e_se)
        P.op("vector", lambda e, q=q: e.reciprocal(y[:, GW0 + q:GW0 + q + 1],
                                                   y[:, SUME:SUME + 1]), inc=False)
        P.op("vector", lambda e: e.tensor_scalar(y[:, T8:T8 + 8], y[:, LG:LG + 8],
                                                 y[:, MAXL:MAXL + 1], None, ALU.is_equal), inc=False)
        P.op("vector", lambda e: e.scalar_tensor_tensor(y[:, T8:T8 + 8], y[:, T8:T8 + 8], -65536.0,
                                                        small[:, 136:144], ALU.mult, ALU.add), inc=False)
        P.op("vector", lambda e, q=q: e.tensor_reduce(y[:, GT1 + q:GT1 + q + 1], y[:, T8:T8 + 8],
                                                      axis=AX.X, op=ALU.min), inc=False)
        ev_oh = P.op("vector", lambda e, q=q: e.tensor_scalar(
            y[:, T8:T8 + 8], IOTA8(t), y[:, GT1 + q:GT1 + q + 1], None, ALU.is_equal))
        P.wait("tensor", [ev_oh, ohe_prev])
        ev_T = P.op("tensor", lambda e: e.transpose(pc[0:8, 0, 0:128], y[:, T8:T8 + 8], IDENT(t)))
        ev_ohe = P.op("vector", lambda e, q=q: e.tensor_copy(
            y[0:8, 2048 + q * 128: 2048 + (q + 1) * 128], pc[0:8, 0, 0:128]), after=ev_T)
        ohe_prev = ev_ohe
        oh_evs.append(ev_ohe)
        sb_ = stage[q % 2]
        tev = []
        P.wait("tensor", st_dma[q % 2])
        for c in range(NCF):
            evT = P.op("tensor", lambda e, q=q, c=c: e.transpose(
                pb[:, c % 2, 0:128], big[:, c * T + q * 128: c * T + q * 128 + 128], IDENT(t)),
                after=pbev[c % 2])
            eve = P.op("vector", lambda e, c=c, sb_=sb_: e.tensor_copy(
                sb_[:, c * 128:(c + 1) * 128], pb[:, c % 2, 0:128]), after=evT)
            pbev[c % 2] = eve
            tev.append(eve)
        d = P.dma("sync", out=k.cc_t_in.ap()[q * 128:(q + 1) * 128, :], in_=sb_, after=tev)
        st_dma[q % 2] = d
        ag_t_dep.append(d)
    ev_ag_t = P.cc("AllGather", ALU.bypass, ins=[k.cc_t_in.ap().opt()],
                   outs=[k.cc_t_out.ap().opt()], after=ag_t_dep)
    ps_release(k, "ps_a", 0, e_lg_prev)
    ps_release(k, "ps_c", 0, ohe_prev)
    P.op("vector", lambda e: e.tensor_tensor_scan(
        y[0:8, 4096:6144], y[0:8, 2048:4096], y[0:8, 2048:4096], 0.0,
        ALU.add, ALU.bypass), inc=False, after=oh_evs)
    P.op("vector", lambda e: e.tensor_copy(y[0:8, 8509:8510], y[0:8, 6143:6144]), inc=False)
    P.op("vector", lambda e: e.tensor_tensor(y[0:8, 6144:8192], y[0:8, 4096:6144],
                                             y[0:8, 2048:4096], ALU.subtract), inc=False)
    ev_pr = P.op("vector", lambda e: e.tensor_tensor(y[0:8, 6144:8192], y[0:8, 6144:8192],
                                                     y[0:8, 2048:4096], ALU.mult))
    lr_evs = []
    prev_lrT = None
    P.wait("tensor", ev_pr)
    for q in range(16):
        P.wait("tensor", prev_lrT)
        evT = P.op("tensor", lambda e, q=q: e.transpose(
            pc[:, 1, 0:8], y[0:8, 6144 + q * 128: 6144 + (q + 1) * 128], IDENT(t)[0:8, 0:8]))
        ev = P.op("vector", lambda e, q=q: e.tensor_reduce(
            y[:, GLR + q:GLR + q + 1], pc[:, 1, 0:8], axis=AX.X, op=ALU.add), after=evT)
        prev_lrT = ev
        lr_evs.append(ev)
    ps_release(k, "ps_c", 1, lr_evs[-1])
    P.op("vector", lambda e: e.scalar_tensor_tensor(
        y[:, GTMP:GTMP + 16], y[:, GT1:GT1 + 16], float(ROWL), y[:, GLR:GLR + 16],
        ALU.mult, ALU.add), inc=False, after=lr_evs)
    ev_si = P.op("vector", lambda e: e.tensor_copy(u32[:, 0:16], y[:, GTMP:GTMP + 16]))
    P.op("vector", lambda e: e.tensor_scalar(y[:, GIDS:GIDS + 16], small[:, 144:160],
                                             small[:, 112:113], None, ALU.add), inc=False,
         after=[ev_ig, ld[3]])
    ev_sv = P.op("vector", lambda e: e.tensor_copy(u32[:, 16:32], y[:, GIDS:GIDS + 16]))
    sc_evs = []
    for q in range(16):
        ev = P.dma_custom("gpsimd", lambda e, q=q: e.indirect_dma_start(
            out=k.idt_in.ap(), out_offset=bass.IndirectOffsetOnAxis(ap=u32[:, q:q + 1], axis=0),
            in_=u32[:, 16 + q:17 + q], in_offset=None), after=[ev_si, ev_sv])
        sc_evs.append(ev)
    ev_cu = P.op("vector", lambda e: e.tensor_copy(u32[0:8, 32:33], y[0:8, 8509:8510]))
    ev_cd = P.dma("sync", out=k.idt_in.ap()[8 * ROWL:8 * ROWL + 8, :], in_=u32[0:8, 32:33],
                  after=ev_cu)
    ev_ag_i = P.cc("AllGather", ALU.bypass, ins=[k.idt_in.ap().opt()],
                   outs=[k.idt_out.ap().opt()], after=sc_evs + [ev_cd])
    ev_cl = P.dma("sync", out=u32[0:8, 36:44],
                  in_=k.idt_out.ap().rearrange("(i r) one -> i (r one)", i=8)[:, 8 * ROWL:8 * ROWL + 8],
                  after=ev_ag_i)
    ev_cf = P.op("vector", lambda e: e.tensor_copy(y[0:8, 8500:8508], u32[0:8, 36:44]), after=ev_cl)
    P.op("vector", lambda e: e.tensor_tensor(y[0:8, 8510:8518], y[0:8, 8500:8508],
                                             small[0:8, 120:128], ALU.mult), inc=False)
    ev_ce = P.op("vector", lambda e: e.tensor_reduce(y[0:8, 8508:8509], y[0:8, 8510:8518],
                                                     axis=AX.X, op=ALU.add))
    P.wait("tensor", [ev_ce, pbev[0], pbev[1]] + ps_wait_list(k, "ps_a", [1, 2, 3]))
    ev_m1 = P.op("tensor", lambda e: e.matmul(pa[0:8, 1, 0:1], MASK_GT(t)[0:8, 0:8],
                                              y[0:8, 8508:8509], start=True, stop=True))
    ev_of = P.op("vector", lambda e: e.tensor_copy(y[0:8, 8518:8519], pa[0:8, 1, 0:1]), after=ev_m1)
    ps_release(k, "ps_a", 1, ev_of)
    ev_dg = P.op("vector", lambda e: e.tensor_scalar(y[0:8, 8510:8518], IDENT(t)[0:8, 0:8],
                                                     y[0:8, 8518:8519], None, ALU.mult))
    P.wait("tensor", ev_dg)
    ev_m2 = P.op("tensor", lambda e: e.matmul(pa[:, 2, 0:8], ALLONES(t)[0:8, 0:128],
                                              y[0:8, 8510:8518], start=True, stop=True))
    ev_ob = P.op("vector", lambda e: e.tensor_copy(y[:, OFFB:OFFB + 8], pa[:, 2, 0:8]), after=ev_m2)
    ps_release(k, "ps_a", 2, ev_ob)
    ev_dc = P.op("vector", lambda e: e.tensor_scalar(y[0:8, 8510:8518], IDENT(t)[0:8, 0:8],
                                                     y[0:8, 8508:8509], None, ALU.mult))
    P.wait("tensor", ev_dc)
    ev_m3 = P.op("tensor", lambda e: e.matmul(pa[:, 3, 0:8], ALLONES(t)[0:8, 0:128],
                                              y[0:8, 8510:8518], start=True, stop=True))
    ev_cb = P.op("vector", lambda e: e.tensor_copy(y[:, CNTB:CNTB + 8], pa[:, 3, 0:8]), after=ev_m3)
    ps_release(k, "ps_a", 3, ev_cb)
    ev_m4 = P.op("tensor", lambda e: e.matmul(pb[0:1, 0, 0:8], small[0:8, 113:114],
                                              y[0:8, 8500:8508], start=True, stop=True), after=ev_cf)
    ev_mr = P.op("vector", lambda e: e.tensor_copy(y[0:1, 8519:8527], pb[0:1, 0, 0:8]), after=ev_m4)
    ps_release(k, "ps_b", 0, ev_mr)
    P.wait("tensor", ev_mr)
    ev_m5 = P.op("tensor", lambda e: e.matmul(pb[:, 1, 0:8], ALLONES(t)[0:1, 0:128],
                                              y[0:1, 8519:8527], start=True, stop=True))
    ev_omy = P.op("vector", lambda e: e.tensor_copy(y[:, OFFMY:OFFMY + 8], pb[:, 1, 0:8]), after=ev_m5)
    ps_release(k, "ps_b", 1, ev_omy)
    ev_tot = P.op("vector", lambda e: e.tensor_tensor(y[:, TOTG:TOTG + 1], y[:, OFFB + 7:OFFB + 8],
                                                      y[:, CNTB + 7:CNTB + 8], ALU.add), after=[ev_ob, ev_cb])
    P.op("vector", lambda e: e.tensor_copy(y[:, GRK:GRK + 16], y[:, GLR:GLR + 16]),
         inc=False, after=ev_omy)
    for e_ in range(8):
        P.op("vector", lambda e, e_=e_: e.tensor_scalar(y[:, GTMP:GTMP + 16], y[:, GT1:GT1 + 16],
                                                        float(e_), None, ALU.is_equal), inc=False)
        P.op("vector", lambda e, e_=e_: e.tensor_scalar(y[:, GTMP:GTMP + 16], y[:, GTMP:GTMP + 16],
                                                        y[:, OFFMY + e_:OFFMY + e_ + 1], None, ALU.mult), inc=False)
        P.op("vector", lambda e: e.tensor_tensor(y[:, GRK:GRK + 16], y[:, GRK:GRK + 16],
                                                 y[:, GTMP:GTMP + 16], ALU.add), inc=False)
    P.op("vector", lambda e: e.tensor_scalar(y[:, GKP:GKP + 16], y[:, GRK:GRK + 16],
                                             2048.0, None, ALU.is_lt), inc=False)
    ev_wg = P.op("vector", lambda e: e.tensor_tensor(stats[:, 0:16], y[:, GW0:GW0 + 16],
                                                     y[:, GKP:GKP + 16], ALU.mult))
    P.op("vector", lambda e: e.tensor_tensor(y[:, GSL:GSL + 16], y[:, GRK:GRK + 16],
                                             y[:, GKP:GKP + 16], ALU.mult), inc=False)
    P.op("vector", lambda e: e.scalar_tensor_tensor(y[:, GTMP:GTMP + 16], y[:, GT1:GT1 + 16],
                                                    2048.0, y[:, GSL:GSL + 16], ALU.mult, ALU.add), inc=False)
    ev_own = P.op("vector", lambda e: e.tensor_copy(u32[:, 48:64], y[:, GTMP:GTMP + 16]))
    P.op("vector", lambda e: e.tensor_scalar(y[:, GIS:GIS + 16], small[:, 144:160],
                                             0.0, None, ALU.mult), inc=False)
    for i_ in range(1, 8):
        P.op("vector", lambda e, i_=i_: e.tensor_scalar(y[:, GTMP:GTMP + 16], small[:, 144:160],
                                                        y[:, OFFB + i_:OFFB + i_ + 1], None, ALU.is_ge), inc=False)
        P.op("vector", lambda e: e.tensor_tensor(y[:, GIS:GIS + 16], y[:, GIS:GIS + 16],
                                                 y[:, GTMP:GTMP + 16], ALU.add), inc=False)
    P.op("vector", lambda e: e.tensor_copy(y[:, GRS:GRS + 16], small[:, 144:160]), inc=False)
    for i_ in range(1, 8):
        P.op("vector", lambda e, i_=i_: e.tensor_scalar(y[:, GTMP:GTMP + 16], y[:, GIS:GIS + 16],
                                                        float(i_), None, ALU.is_equal), inc=False)
        P.op("vector", lambda e, i_=i_: e.tensor_scalar(y[:, GTMP:GTMP + 16], y[:, GTMP:GTMP + 16],
                                                        y[:, OFFB + i_:OFFB + i_ + 1], None, ALU.mult), inc=False)
        P.op("vector", lambda e: e.tensor_tensor(y[:, GRS:GRS + 16], y[:, GRS:GRS + 16],
                                                 y[:, GTMP:GTMP + 16], ALU.subtract), inc=False)
    P.op("vector", lambda e: e.scalar_tensor_tensor(y[:, GTMP:GTMP + 16], y[:, GIS:GIS + 16],
                                                    float(IDN), y[:, GRS:GRS + 16], ALU.mult, ALU.add), inc=False)
    P.op("vector", lambda e: e.tensor_scalar(y[:, GTMP:GTMP + 16], y[:, GTMP:GTMP + 16],
                                             small[:, 114:115], None, ALU.add), inc=False)
    ev_fx = P.op("vector", lambda e: e.tensor_copy(u32[:, 16:32], y[:, GTMP:GTMP + 16]),
                 after=sc_evs)
    ev_ks = P.op("vector", lambda e: e.tensor_scalar(y[:, GRS:GRS + 16], small[:, 144:160],
                                                     y[:, TOTG:TOTG + 1], None, ALU.is_lt), after=ev_tot)
    id_evs = []
    for q in range(16):
        ev = P.dma_custom("gpsimd", lambda e, q=q: e.indirect_dma_start(
            out=u32[:, q:q + 1], out_offset=None, in_=k.idt_out.ap(),
            in_offset=bass.IndirectOffsetOnAxis(ap=u32[:, 16 + q:17 + q], axis=0)),
            after=[ev_ag_i, ev_fx])
        id_evs.append(ev)
    P.op("vector", lambda e: e.tensor_copy(y[:, GIDS:GIDS + 16], u32[:, 0:16]),
         inc=False, after=id_evs + [ev_ks])
    P.op("vector", lambda e: e.tensor_tensor(y[:, GIDS:GIDS + 16], y[:, GIDS:GIDS + 16],
                                             y[:, GRS:GRS + 16], ALU.mult), inc=False)
    ev_idc = P.op("vector", lambda e: e.tensor_copy(u32[:, 0:16], y[:, GIDS:GIDS + 16]))
    import os as _os
    _d = _os.environ.get("DBGL2F")
    if _d:
        srcs = {"lrank": y[:, GLR:GLR + 16], "top1": y[:, GT1:GT1 + 16],
                "rankg": y[:, GRK:GRK + 16], "slot": y[:, GSL:GSL + 16],
                "wg": stats[:, 0:16], "ids": y[:, GIDS:GIDS + 16],
                "is": y[:, GIS:GIS + 16], "offb": y[:, OFFB:OFFB + 16],
                "w0": y[:, GW0:GW0 + 16]}
        k._l2f_tap = P.op("vector", lambda e, src=srcs[_d]: e.tensor_copy(
            stats[:, 1280:1296], src))
    dg_tr = []
    last_st = [st_dma[0], st_dma[1]]
    pbev = [ev_mr, ev_omy]
    for q in range(16):
        sb_ = stage[q % 2]
        ev_g = P.dma_custom("gpsimd", lambda e, q=q, sb_=sb_: e.indirect_dma_start(
            out=sb_, out_offset=None, in_=k.cc_t_out.ap(),
            in_offset=bass.IndirectOffsetOnAxis(ap=u32[:, q:q + 1], axis=0)),
            after=[ev_ag_t, ev_idc, last_st[q % 2]])
        P.wait("tensor", ev_g)
        for c in range(NCF):
            evT = P.op("tensor", lambda e, q=q, c=c, sb_=sb_: e.transpose(
                pb[:, c % 2, 0:128], sb_[:, c * 128:(c + 1) * 128], IDENT(t)), after=pbev[c % 2])
            eve = P.op("vector", lambda e, q=q, c=c: e.tensor_copy(
                big[:, c * T + q * 128: c * T + q * 128 + 128], pb[:, c % 2, 0:128]), after=evT)
            pbev[c % 2] = eve
            if c == NCF - 1:
                last_st[q % 2] = eve
            dg_tr.append(eve)
    ffn_dma = []
    wo_prev = None
    otmp_rd = None
    dma_tmp = [None, None]
    for q4 in range(4):
        h_evs = []
        for ic in range(32):
            bank = ic % 4
            P.wait("tensor", ps_wait_list(k, "ps_a", [bank]))
            ev1 = _stream_mm(k, "l2f_wi", ic * 128, 8,
                             lambda kc, q4=q4: big[:, kc * T + q4 * 512: kc * T + q4 * 512 + 512],
                             pa[:, bank, :],
                             extra_after=(dg_tr if (q4 == 0 and ic == 0) else None))
            ev = P.op("scalar", lambda e, ic=ic, bank=bank: e.activation(
                y[:, ic * 512:(ic + 1) * 512], pa[:, bank, :], AF.Gelu),
                after=[ev1] + ([wo_prev] if (ic == 0 and wo_prev is not None) else []))
            ps_release(k, "ps_a", bank, ev)
            h_evs.append(ev)
        for hc in range(8):
            bank = hc % 2
            P.wait("tensor", ps_wait_list(k, "ps_c", [bank]))
            ev2 = _stream_mm(k, "l2f_wo", hc * 128, 32,
                             lambda kc: y[:, kc * 512:(kc + 1) * 512],
                             pc[:, bank, :], extra_after=(h_evs if hc == 0 else None))
            wo_prev = ev2
            evo = P.op("vector", lambda e, bank=bank: e.tensor_copy(stats[:, 512:1024], pc[:, bank, :]),
                       after=[ev2] + ([otmp_rd] if otmp_rd is not None else []))
            ps_release(k, "ps_c", bank, evo)
            P.wait("tensor", evo)
            for st in range(4):
                tb = st % 2
                evT = P.op("tensor", lambda e, st=st, tb=tb: e.transpose(
                    pb[:, tb, 0:128], stats[:, 512 + st * 128: 512 + (st + 1) * 128], IDENT(t)),
                    after=pbev[tb])
                evp = P.op("vector", lambda e, tb=tb: e.tensor_copy(
                    stats[:, 1024 + tb * 128: 1152 + tb * 128], pb[:, tb, 0:128]),
                    after=[evT] + ([dma_tmp[tb]] if dma_tmp[tb] is not None else []))
                pbev[tb] = evp
                if st == 3:
                    otmp_rd = evp
                d = P.dma("sync",
                          out=k.cc_y_in.ap()[q4 * 512 + st * 128: q4 * 512 + (st + 1) * 128,
                                             hc * 128:(hc + 1) * 128],
                          in_=stats[:, 1024 + tb * 128: 1152 + tb * 128], after=evp)
                dma_tmp[tb] = d
                ffn_dma.append(d)
    ev_ag_y = P.cc("AllGather", ALU.bypass, ins=[k.cc_y_in.ap().opt()],
                   outs=[k.cc_y_out.ap().opt()], after=ffn_dma)
    x_evs = []
    own_rd = [None, None]
    for q in range(16):
        sb_ = stage[q % 2]
        ev_g = P.dma_custom("gpsimd", lambda e, q=q, sb_=sb_: e.indirect_dma_start(
            out=sb_, out_offset=None, in_=k.cc_y_out.ap(),
            in_offset=bass.IndirectOffsetOnAxis(ap=u32[:, 48 + q:49 + q], axis=0)),
            after=[ev_ag_y, ev_own, own_rd[q % 2]])
        ev_w = P.op("vector", lambda e, q=q, sb_=sb_: e.tensor_scalar(
            sb_, sb_, stats[:, q:q + 1], None, ALU.mult), after=[ev_g, ev_wg])
        P.wait("tensor", ev_w)
        for c in range(NCF):
            evT = P.op("tensor", lambda e, c=c, sb_=sb_: e.transpose(
                pb[:, c % 2, 0:128], sb_[:, c * 128:(c + 1) * 128], IDENT(t)), after=pbev[c % 2])
            eva = P.op("vector", lambda e, q=q, c=c: e.tensor_tensor(
                xcol(x, c, q * 128, 128), xcol(x, c, q * 128, 128), pb[:, c % 2, 0:128], ALU.add),
                after=evT)
            pbev[c % 2] = eva
            if c == NCF - 1:
                own_rd[q % 2] = eva
            x_evs.append(eva)
    for b_ in range(2):
        ps_release(k, "ps_b", b_, pbev[b_])
    import os as _os
    _d = _os.environ.get("DBGL2F")
    if _d:
        ev_tap = k._l2f_tap
        ev = P.op("vector", lambda e: e.tensor_copy(x[:, 0:16], stats[:, 1280:1296]),
                  after=[ev_tap] + x_evs)
        x_evs = x_evs + [ev]
    P.phase("")
    return x_evs


def emit_stub(k, evs):
    raise NotImplementedError


def _dbg_ln(k, evs):
    evs = emit_ln(k, "x", evs, dst_name="y", sq_scratch=xcol(k.t["big"], 0))
    out = []
    for c in range(NCF):
        out.append(k.P.op("vector", lambda e, c=c: e.tensor_copy(
            xcol(k.t["x"], c), xcol(k.t["y"], c)), after=(evs if c == 0 else None)))
    return out


EMITTERS = {"ln": _dbg_ln, "l1f": emit_l1f, "l0f": emit_l0f, "l2a": emit_l2a, "l3a": emit_l3a, "l3f": emit_l3f, "l0a": emit_l0a, "l1a": emit_l1a, "l2f": emit_l2f}
for b in BLOCKS_ALL:
    EMITTERS.setdefault(b, emit_stub)



def kernel_dev(inputs, blocks=None):
    blocks = blocks or BLOCKS_ALL
    w, xs = prep_host(inputs)
    k = build_program(blocks)
    need = {n for n in WSHAPES if any(n.startswith(b + "_") for b in blocks)}
    in_maps = []
    for b in range(8):
        m = {"x_t": xs[b], "core_id": np.full((1, 1), float(b), np.float32)}
        for n in sorted(need):
            if n in w:
                m[n] = w[n]
        if "l2f" in blocks:
            m["l2f_wi"] = np.ascontiguousarray(w["l2f_wi"][b])
            m["l2f_wo"] = np.ascontiguousarray(w["l2f_wo"][b])
            pcv = np.zeros((128, 24), np.float32)
            pcv[:, 0] = b * 2048.0
            pcv[:, 1] = (np.arange(128) < b).astype(np.float32)
            pcv[:, 2] = b * 2176.0
            pcv[:, 8 + b] = 1.0
            m["l2f_pc"] = pcv
        in_maps.append(m)
    res = run_bass_kernel_spmd(k.nc, in_maps, core_ids=list(range(8)))
    out = np.stack([np.ascontiguousarray(np.asarray(r["out_t"]).T) for r in res.results])
    return out


def kernel(**inputs):
    return kernel_dev(inputs, BLOCKS_ALL)
